# version 17
# baseline (speedup 1.0000x reference)
import contextlib
import numpy as np
import concourse.bass as bass
import concourse.mybir as mybir
from concourse.bass_utils import run_bass_kernel_spmd

F32 = mybir.dt.float32
BF16 = mybir.dt.bfloat16
AF = mybir.ActivationFunctionType
I32 = mybir.dt.int32
ALU = mybir.AluOpType
AX = mybir.AxisListType

P = 128
T = 2048
D = 1024
NT = 16
ND = 8
H = 8
E = 8
NQ = 4
FT_Q = 8
TC = 8
NIDX = E * NQ

WG_MQ, WG_MK, WG_MV, WG_RWT = 16, 32, 48, 64
def wg_q(idx):
    return 64 + 32 * idx + 32
N_PJEV = 64
DEBUG_STAGE = None


def build_nc():
    nc = bass.Bass("TRN2", target_bir_lowering=False, detect_race_conditions=False)

    x_d = nc.dram_tensor("x", [T, D], F32, kind="ExternalInput").ap()
    mq_d = nc.dram_tensor("mq", [D, 512], F32, kind="ExternalInput").ap()
    mk_d = nc.dram_tensor("mk", [D, 512], F32, kind="ExternalInput").ap()
    mv_d = nc.dram_tensor("mv", [D, D], F32, kind="ExternalInput").ap()
    rwt_d = nc.dram_tensor("rwt", [D, E], F32, kind="ExternalInput").ap()
    esc_d = nc.dram_tensor("esc", [P, E], F32, kind="ExternalInput").ap()
    w1t_d = nc.dram_tensor("w1t", [E, D, 4096], F32, kind="ExternalInput").ap()
    w2t_d = nc.dram_tensor("w2t", [E, 4096, D], F32, kind="ExternalInput").ap()
    out_d = nc.dram_tensor("out", [T, D], F32, kind="ExternalOutput").ap()
    yd_d = nc.dram_tensor("yd", [T, D], BF16, kind="Internal").ap()
    rsnm_d = nc.dram_tensor("rsnm", [T, 2], F32, kind="Internal").ap()
    if DEBUG_STAGE == "attn":
        dbg = {nm: nc.dram_tensor(nm, shp, dt, kind="ExternalOutput").ap()
               for nm, shp, dt in [("d_xnt", [P, ND * T], BF16), ("d_vp", [P, NT * D], BF16),
                                   ("d_qt", [P, 4 * T], BF16), ("d_kt", [P, 4 * T], BF16),
                                   ("d_comb", [P, NT * E], F32),
                                   ("d_s1", [P, NT], F32), ("d_nm", [P, NT], F32),
                                   ("d_ss", [P, NT], F32), ("d_veps", [P, NT], F32),
                                   ("d_sd", [P, NT], F32), ("d_rstd", [P, NT], F32),
                                   ("d_nmrs", [P, NT], F32)]}

    ctx = contextlib.ExitStack()
    with ctx:
        sb = lambda name, shape, dt: ctx.enter_context(nc.sbuf_tensor(name, shape, dt))
        X = sb("X", [P, NT * D], F32)
        XNT = sb("XNT", [P, ND * T], BF16)
        COMB = sb("COMB", [P, NT * E], F32)
        ESC = sb("ESC", [P, E], F32)
        RWTS = sb("RWTS", [P, ND * E], BF16)
        S1 = sb("S1", [P, NT], F32)
        NM = sb("NM", [P, NT], F32)
        SS = sb("SS", [P, NT], F32)
        VEPS = sb("VEPS", [P, NT], F32)
        SD = sb("SD", [P, NT], F32)
        RSTD = sb("RSTD", [P, NT], F32)
        NMRS = sb("NMRS", [P, NT], F32)
        MSQ = sb("MSQ", [P, NT], F32)
        V2 = sb("V2", [P, NT], F32)
        NM2 = sb("NM2", [P, 2], F32)
        NMmin = sb("NMmin", [P, 1], F32)
        NMmin8 = sb("NMmin8", [P, 1], F32)
        Zs = sb("Zs", [P, 1], F32)
        Rz = sb("Rz", [P, 1], F32)
        NMX = [sb(f"NMX{i}", [P, 1], F32) for i in range(2)]
        PBUF = [sb(f"PBUF{i}", [P, E], F32) for i in range(2)]
        SZ = [sb(f"SZ{i}", [P, 1], F32) for i in range(2)]
        RZr = [sb(f"RZr{i}", [P, 1], F32) for i in range(2)]
        PRB = [sb(f"PRB{i}", [P, E], F32) for i in range(2)]
        M12 = [sb(f"M12{i}", [P, 2], F32) for i in range(2)]
        MSK = [sb(f"MSK{i}", [P, E], F32) for i in range(2)]
        MKD = [sb(f"MKD{i}", [P, E], F32) for i in range(2)]
        EM2 = [sb(f"EM2{i}", [P, 2], F32) for i in range(2)]
        EPB = [sb(f"EPB{i}", [P, E], F32) for i in range(2)]
        DD = [sb(f"DD{i}", [P, 1], F32) for i in range(2)]
        RD = [sb(f"RD{i}", [P, 1], F32) for i in range(2)]
        CWA = [sb(f"CWA{i}", [P, E], F32) for i in range(2)]
        Z2p = [sb(f"Z2p{i}", [P, 2], F32) for i in range(2)]
        ZZ = sb("ZZ", [P, 512], BF16)
        IOTC = sb("IOTC", [P, 1], F32)
        IOTR = sb("IOTR", [P, P], F32)
        GIDC = sb("GIDC", [P, NT], F32)
        IDM = sb("IDM", [P, P], F32)
        RSNMS = sb("RSNMS", [P, 2 * NT], F32)
        CWB = [sb(f"CWB{i}", [P, E], F32) for i in range(2)]

        sem = {}
        for name in ["s_x", "s_hw", "s_wg", "s_xns", "s_xnt", "s_ln1", "s_ln2",
                     "s_ln3", "s_ln4", "s_pj", "s_pjev", "s_rt", "s_rv", "s_rs",
                     "s_gather", "s_schead", "s_mm", "s_vmax", "s_exp", "s_scl",
                     "s_av", "s_avev", "s_qdone", "s_m1", "s_silu", "s_m2",
                     "s_yev", "s_out", "s_zz", "s_x0", "s_x1", "s_xnt0", "s_xnt1",
                     "s_wq0", "s_wq1", "s_zs", "s_io", "s_rnv", "s_rn", "s_yz",
                     "s_p3v", "s_p3t", "s_ptv", "s_ptt", "s_tki", "s_gx0", "s_gx1",
                     "s_xgb", "s_xt0", "s_xt1", "s_ysc", "s_sc0", "s_sc1", "s_yl"]:
            sem[name] = ctx.enter_context(nc.semaphore(name))

        blk = ctx.enter_context(nc.Block())

        with (nc.sbuf_tensor("QT", [P, 4 * T], BF16) as QT,
              nc.sbuf_tensor("KT", [P, 4 * T], BF16) as KT,
              nc.sbuf_tensor("VP", [P, NT * D], BF16) as VP):

            p1 = contextlib.ExitStack()
            MQS = p1.enter_context(nc.sbuf_tensor("MQS", [P, ND * 512], BF16))
            MKS = p1.enter_context(nc.sbuf_tensor("MKS", [P, ND * 512], BF16))
            MVS = p1.enter_context(nc.sbuf_tensor("MVS", [P, ND * D], BF16))
            DUMP = p1.enter_context(nc.sbuf_tensor("DUMP", [P, D], BF16))
            XNS0 = p1.enter_context(nc.sbuf_tensor("XNS0", [P, D], BF16))
            XNS1 = p1.enter_context(nc.sbuf_tensor("XNS1", [P, D], BF16))
            LOG = p1.enter_context(nc.sbuf_tensor("LOG", [P, NT * E], F32))
            NMXb = p1.enter_context(nc.sbuf_tensor("NMXb", [P, NT], F32))
            PB = p1.enter_context(nc.sbuf_tensor("PB", [P, NT * E], F32))
            SZb = p1.enter_context(nc.sbuf_tensor("SZb", [P, NT], F32))
            RZb = p1.enter_context(nc.sbuf_tensor("RZb", [P, NT], F32))
            PRb = p1.enter_context(nc.sbuf_tensor("PRb", [P, NT * E], F32))
            M1b = p1.enter_context(nc.sbuf_tensor("M1b", [P, NT], F32))
            M2b = p1.enter_context(nc.sbuf_tensor("M2b", [P, NT], F32))
            MKb = p1.enter_context(nc.sbuf_tensor("MKb", [P, NT * E], F32))
            MDb = p1.enter_context(nc.sbuf_tensor("MDb", [P, NT * E], F32))
            EM1b = p1.enter_context(nc.sbuf_tensor("EM1b", [P, NT], F32))
            EM2b = p1.enter_context(nc.sbuf_tensor("EM2b", [P, NT], F32))
            EPb = p1.enter_context(nc.sbuf_tensor("EPb", [P, NT * E], F32))
            DDb = p1.enter_context(nc.sbuf_tensor("DDb", [P, NT], F32))
            RDb = p1.enter_context(nc.sbuf_tensor("RDb", [P, NT], F32))
            CAb = p1.enter_context(nc.sbuf_tensor("CAb", [P, NT * E], F32))
            CBb = p1.enter_context(nc.sbuf_tensor("CBb", [P, NT * E], F32))
            PJ0 = p1.enter_context(nc.psum_tensor("PJ0", [P, 512], F32))
            PJ1 = p1.enter_context(nc.psum_tensor("PJ1", [P, 512], F32))
            with p1:
                XNS = [XNS0, XNS1]
                PJ = [PJ0, PJ1]

                @blk.gpsimd
                def _(g):
                    g.iota(IOTC[:, 0:1], [[0, 1]], base=0, channel_multiplier=1,
                           allow_small_or_imprecise_dtypes=True)
                    g.iota(IOTR[:, :], [[1, P]], base=0, channel_multiplier=0,
                           allow_small_or_imprecise_dtypes=True)
                    g.iota(GIDC[:, :], [[P, NT]], base=0, channel_multiplier=1,
                           allow_small_or_imprecise_dtypes=True).then_inc(sem["s_io"], 1)
                    def xload(tt):
                        sx = sem["s_x0"] if tt % 2 == 0 else sem["s_x1"]
                        if tt >= 2:
                            g.wait_ge(sx, 16 * (tt // 2))
                        g.dma_start(X[:, tt * D:(tt + 1) * D],
                                    x_d[tt * P:(tt + 1) * P, :]).then_inc(sx, 16)
                    for tt in range(4):
                        xload(tt)
                    g.dma_start(MVS[:, :].rearrange("p (dt j) -> p dt j", dt=ND), mv_d.rearrange("(dt p) j -> p dt j", p=P)).then_inc(sem["s_wg"], 16)
                    for tt in range(4, 8):
                        xload(tt)
                    g.wait_ge(sem["s_wg"], 16)
                    g.dma_start(MQS[:, :].rearrange("p (dt j) -> p dt j", dt=ND), mq_d.rearrange("(dt p) j -> p dt j", p=P)).then_inc(sem["s_wg"], 16)
                    g.wait_ge(sem["s_wg"], 32)
                    g.dma_start(MKS[:, :].rearrange("p (dt j) -> p dt j", dt=ND), mk_d.rearrange("(dt p) j -> p dt j", p=P)).then_inc(sem["s_wg"], 16)
                    for tt in range(8, NT):
                        xload(tt)
                    g.wait_ge(sem["s_wg"], 48)
                    g.dma_start(RWTS[:, :].rearrange("p (dt j) -> p dt j", dt=ND), rwt_d.rearrange("(dt p) j -> p dt j", p=P)).then_inc(sem["s_wg"], 16)
                    g.wait_ge(sem["s_zz"], 1)
                    for tt in range(NT):
                        for hh in range(2):
                            g.dma_start(yd_d[tt * P:(tt + 1) * P, hh * 512:(hh + 1) * 512],
                                        ZZ[:, :]).then_inc(sem["s_yz"], 16)

                @blk.sync
                def _(s):
                    s.dma_start(ESC[:, :], esc_d).then_inc(sem["s_hw"], 16)
                    for tt in range(NT):
                        s.wait_ge(sem["s_xns"], tt + 1)
                        sxn = sem["s_xnt0"] if tt % 2 == 0 else sem["s_xnt1"]
                        if tt >= 2:
                            s.wait_ge(sxn, 128 * (tt // 2))
                        for dt in range(ND):
                            s.dma_start(XNT[:, dt * T + tt * P: dt * T + (tt + 1) * P],
                                        XNS[tt % 2][:, dt * P:(dt + 1) * P],
                                        transpose=True).then_inc(sxn, 16)
                    s.wait_ge(sem["s_rnv"], 1)
                    s.dma_start(rsnm_d.rearrange("(tt p) two -> p tt two", p=P),
                                RSNMS[:, :].rearrange("p (tt two) -> p tt two", two=2)
                                ).then_inc(sem["s_rn"], 16)

                @blk.tensor
                def _(t):
                    t.wait_ge(sem["s_wg"], 16)
                    gidx = 0
                    for tt in range(NT):
                        t.wait_ge(sem["s_xnt0"], 128 * (tt // 2 + 1))
                        if tt >= 1:
                            t.wait_ge(sem["s_xnt1"], 128 * ((tt + 1) // 2))
                        for jc in range(2):
                            ps = PJ[gidx % 2]
                            if gidx >= 2:
                                t.wait_ge(sem["s_pjev"], gidx - 1)
                            for dt in range(ND):
                                ins = t.matmul(ps[:, 0:512],
                                               XNT[:, dt * T + tt * P: dt * T + (tt + 1) * P],
                                               MVS[:, dt * D + jc * 512: dt * D + (jc + 1) * 512],
                                               start=(dt == 0), stop=(dt == ND - 1))
                            ins.then_inc(sem["s_pj"], 1)
                            gidx += 1
                    t.wait_ge(sem["s_xnt0"], 128 * (NT // 2))
                    t.wait_ge(sem["s_xnt1"], 128 * (NT // 2))
                    t.wait_ge(sem["s_wg"], 48)
                    for wsb in (MQS, MKS):
                        for rp in range(4):
                            for nb in range(4):
                                ps = PJ[gidx % 2]
                                t.wait_ge(sem["s_pjev"], gidx - 1)
                                for dt in range(ND):
                                    ins = t.matmul(ps[:, 0:512],
                                                   wsb[:, dt * 512 + rp * P: dt * 512 + (rp + 1) * P],
                                                   XNT[:, dt * T + nb * 512: dt * T + (nb + 1) * 512],
                                                   start=(dt == 0), stop=(dt == ND - 1))
                                ins.then_inc(sem["s_pj"], 1)
                                gidx += 1
                    t.wait_ge(sem["s_wg"], WG_RWT)
                    t.wait_ge(sem["s_pjev"], N_PJEV)
                    for tt in range(NT):
                        ps = PJ[tt % 2]
                        if tt >= 2:
                            t.wait_ge(sem["s_rt"], tt - 1)
                        for dt in range(ND):
                            ins = t.matmul(ps[:, 0:E],
                                           XNT[:, dt * T + tt * P: dt * T + (tt + 1) * P],
                                           RWTS[:, dt * E:(dt + 1) * E],
                                           start=(dt == 0), stop=(dt == ND - 1))
                        ins.then_inc(sem["s_pj"], 1)

                @blk.scalar
                def _(sc):
                    for tt in range(NT):
                        sc.wait_ge(sem["s_x0"] if tt % 2 == 0 else sem["s_x1"], 16 * (tt // 2 + 1))
                        sc.activation(DUMP[:, :], X[:, tt * D:(tt + 1) * D], AF.Square,
                                      accum_out=SS[:, tt:tt + 1]).then_inc(sem["s_ln2"], 1)
                    sc.wait_ge(sem["s_ln3"], 1)
                    sc.activation(SD[:, 0:NT], VEPS[:, 0:NT], AF.Sqrt).then_inc(sem["s_ln4"], 1)
                    gidx = 0
                    for tt in range(NT):
                        for jc in range(2):
                            sc.wait_ge(sem["s_pj"], gidx + 1)
                            sc.copy(VP[:, tt * D + jc * 512: tt * D + (jc + 1) * 512],
                                    PJ[gidx % 2][:, 0:512]).then_inc(sem["s_pjev"], 1)
                            gidx += 1
                    for dst in (QT, KT):
                        for rp in range(4):
                            for nb in range(4):
                                sc.wait_ge(sem["s_pj"], gidx + 1)
                                sc.copy(dst[:, rp * T + nb * 512: rp * T + (nb + 1) * 512],
                                        PJ[gidx % 2][:, 0:512]).then_inc(sem["s_pjev"], 1)
                                gidx += 1
                    sc.wait_ge(sem["s_rv"], 1)
                    for tt in range(NT):
                        ins = sc.activation(PB[:, tt * E:(tt + 1) * E], LOG[:, tt * E:(tt + 1) * E],
                                            AF.Exp, bias=NMXb[:, tt:tt + 1],
                                            accum_out=SZb[:, tt:tt + 1])
                    ins.then_inc(sem["s_rs"], 1)
                    sc.wait_ge(sem["s_rv"], 2)
                    sc.activation(EM1b[:, 0:NT], M1b[:, 0:NT], AF.Exp)
                    sc.activation(EM2b[:, 0:NT], M2b[:, 0:NT], AF.Exp)
                    sc.activation(EPb[:, :], PRb[:, :], AF.Exp).then_inc(sem["s_rs"], 1)

                @blk.vector
                def _(v):
                    v.memset(ZZ[:, :], 0.0).then_inc(sem["s_zz"], 1)
                    for tt in range(NT):
                        v.wait_ge(sem["s_x0"] if tt % 2 == 0 else sem["s_x1"], 16 * (tt // 2 + 1))
                        v.reduce_sum(S1[:, tt:tt + 1], X[:, tt * D:(tt + 1) * D], AX.X)
                    for tt in range(NT):
                        v.tensor_scalar(NM[:, tt:tt + 1], S1[:, tt:tt + 1], -1.0 / D, None,
                                        ALU.mult)
                    v.wait_ge(sem["s_ln2"], NT)
                    v.tensor_scalar(V2[:, 0:NT], SS[:, 0:NT], 1.0 / D, 1e-5,
                                    ALU.mult, ALU.add)
                    v.tensor_tensor(MSQ[:, 0:NT], NM[:, 0:NT], NM[:, 0:NT], ALU.mult)
                    v.drain()
                    v.tensor_tensor(VEPS[:, 0:NT], V2[:, 0:NT], MSQ[:, 0:NT],
                                    ALU.subtract).then_inc(sem["s_ln3"], 1)
                    v.wait_ge(sem["s_ln4"], 1)
                    v.reciprocal(RSTD[:, 0:NT], SD[:, 0:NT])
                    v.wait_ge(sem["s_io"], 1)
                    v.tensor_scalar(IDM[:, :], IOTR[:, :], IOTC[:, 0:1], None, ALU.is_equal)
                    v.tensor_tensor(NMRS[:, 0:NT], NM[:, 0:NT], RSTD[:, 0:NT], ALU.mult)
                    v.drain()
                    rsv = RSNMS[:, :].rearrange("p (tt two) -> p tt two", two=2)
                    v.tensor_copy(rsv[:, :, 0:1], RSTD[:, 0:NT].rearrange("p (tt one) -> p tt one", one=1))
                    v.tensor_copy(rsv[:, :, 1:2], NMRS[:, 0:NT].rearrange("p (tt one) -> p tt one", one=1)).then_inc(sem["s_rnv"], 1)
                    for tt in range(NT):
                        if tt >= 2:
                            v.wait_ge(sem["s_xnt0"] if tt % 2 == 0 else sem["s_xnt1"],
                                      128 * (tt // 2))
                        v.tensor_scalar(XNS[tt % 2][:, :], X[:, tt * D:(tt + 1) * D],
                                        RSTD[:, tt:tt + 1], NMRS[:, tt:tt + 1],
                                        ALU.mult, ALU.add).then_inc(sem["s_xns"], 1)
                    v.wait_ge(sem["s_hw"], 16)
                    for tt in range(NT):
                        v.wait_ge(sem["s_pj"], N_PJEV + tt + 1)
                        v.tensor_copy(LOG[:, tt * E:(tt + 1) * E],
                                      PJ[tt % 2][:, 0:E]).then_inc(sem["s_rt"], 1)
                    for tt in range(NT):
                        ins = v.tensor_reduce(NMXb[:, tt:tt + 1], LOG[:, tt * E:(tt + 1) * E],
                                              AX.X, ALU.max, negate=True)
                    ins.then_inc(sem["s_rv"], 1)
                    v.wait_ge(sem["s_rs"], 1)
                    for tt in range(NT):
                        v.reciprocal(RZb[:, tt:tt + 1], SZb[:, tt:tt + 1])
                    for tt in range(NT):
                        v.tensor_scalar(PRb[:, tt * E:(tt + 1) * E], PB[:, tt * E:(tt + 1) * E],
                                        RZb[:, tt:tt + 1], None, ALU.mult)
                    for tt in range(NT):
                        v.tensor_reduce(M1b[:, tt:tt + 1], PRb[:, tt * E:(tt + 1) * E], AX.X, ALU.max)
                    for tt in range(NT):
                        v.tensor_scalar(MKb[:, tt * E:(tt + 1) * E], PRb[:, tt * E:(tt + 1) * E],
                                        M1b[:, tt:tt + 1], None, ALU.is_ge)
                    for tt in range(NT):
                        v.scalar_tensor_tensor(MDb[:, tt * E:(tt + 1) * E], MKb[:, tt * E:(tt + 1) * E],
                                               -10.0, PRb[:, tt * E:(tt + 1) * E], ALU.mult, ALU.add)
                    for tt in range(NT):
                        ins = v.tensor_reduce(M2b[:, tt:tt + 1], MDb[:, tt * E:(tt + 1) * E],
                                              AX.X, ALU.max)
                    ins.then_inc(sem["s_rv"], 1)
                    v.wait_ge(sem["s_rs"], 2)
                    v.tensor_tensor(DDb[:, 0:NT], EM1b[:, 0:NT], EM2b[:, 0:NT], ALU.add)
                    for tt in range(NT):
                        v.tensor_scalar(MKb[:, tt * E:(tt + 1) * E], PRb[:, tt * E:(tt + 1) * E],
                                        M2b[:, tt:tt + 1], None, ALU.is_ge)
                    v.reciprocal(RDb[:, 0:NT], DDb[:, 0:NT])
                    v.tensor_tensor(CAb[:, :], EPb[:, :], MKb[:, :], ALU.mult)
                    v.drain()
                    for tt in range(NT):
                        v.tensor_scalar(CBb[:, tt * E:(tt + 1) * E], CAb[:, tt * E:(tt + 1) * E],
                                        RDb[:, tt:tt + 1], None, ALU.mult)
                    for tt in range(NT):
                        ins = v.tensor_tensor(COMB[:, tt * E:(tt + 1) * E], CBb[:, tt * E:(tt + 1) * E],
                                              ESC[:, :], ALU.mult)
                    ins.then_inc(sem["s_rv"], 1)

            PSA = ctx.enter_context(nc.psum_tensor("PSA", [P, 1024], F32))
            PSB = ctx.enter_context(nc.psum_tensor("PSB", [P, 1024], F32))
            AVP0 = ctx.enter_context(nc.psum_tensor("AVP0", [P, 1024], F32))
            AVP1 = ctx.enter_context(nc.psum_tensor("AVP1", [P, 1024], F32))
            with (nc.sbuf_tensor("QHp", [P, T], BF16) as QHp,
                  nc.sbuf_tensor("KHp", [P, T], BF16) as KHp,
                  nc.sbuf_tensor("EB0", [P, T], BF16) as EB0,
                  nc.sbuf_tensor("EB1", [P, T], BF16) as EB1,
                  nc.sbuf_tensor("E20", [P, T], BF16) as E20,
                  nc.sbuf_tensor("E21", [P, T], BF16) as E21):
                EB = [EB0, EB1]
                E2 = [E20, E21]
                AVP = [AVP0, AVP1]

                @blk.sync
                def _(s):
                    s.wait_ge(sem["s_pjev"], N_PJEV)
                    for h in range(H):
                        hb = (h % 2) * 64
                        if h >= 1:
                            s.wait_ge(sem["s_gather"], 256 * h)
                        if h >= 2:
                            s.wait_ge(sem["s_exp"], 32 * (h - 1))
                        for (buf, src) in ((QHp, QT), (KHp, KT)):
                            for pi in range(2):
                                for rp in range(4):
                                    s.dma_start(
                                        buf[hb:hb + 64, (2 * rp + pi) * 256:(2 * rp + pi + 1) * 256],
                                        src[pi * 64: pi * 64 + 64,
                                            rp * T + h * 256: rp * T + (h + 1) * 256],
                                    ).then_inc(sem["s_gather"], 16)

                @blk.tensor
                def _(t):
                    t.wait_ge(sem["s_rt"], NT)
                    t.wait_ge(sem["s_zz"], 1)
                    for h in range(H):
                        hb = (h % 2) * 64
                        t.wait_ge(sem["s_gather"], 256 * (h + 1))
                        if h >= 1:
                            t.wait_ge(sem["s_avev"], 2 * h)
                        for ha in range(2):
                            for bk in range(2):
                                t.matmul(AVP[ha][:, bk * 512:(bk + 1) * 512],
                                         ZZ[:, 0:128], ZZ[:, 0:512],
                                         start=True, stop=False)
                        def emit_av(j):
                            t.wait_ge(sem["s_scl"], 16 * h + j + 1)
                            rc, halfc = j // 2, j % 2
                            nmm = 0
                            for ha in range(2):
                                for r in range(8):
                                    nmm += 1
                                    ins = t.matmul(
                                        AVP[ha][:, r * P:(r + 1) * P],
                                        E2[j % 2][:, r * 256 + ha * P: r * 256 + ha * P + P],
                                        VP[:, (2 * r + halfc) * D + rc * P: (2 * r + halfc) * D + (rc + 1) * P],
                                        start=False, stop=(j == 15) and (r % 4 == 3))
                                    if nmm == 16:
                                        ins.then_inc(sem["s_av"], 1)
                        for i in range(16):
                            for half, ps in ((0, PSA), (1, PSB)):
                                w = 32 * h + 2 * (i - 1) + half + 1
                                if w > 0:
                                    t.wait_ge(sem["s_exp"], w)
                                for u in range(2):
                                    ins = t.matmul(ps[:, u * 512:(u + 1) * 512],
                                                   KHp[hb:hb + 64, i * P:(i + 1) * P],
                                                   QHp[hb:hb + 64,
                                                       half * 1024 + u * 512: half * 1024 + (u + 1) * 512],
                                                   start=True, stop=True)
                                ins.then_inc(sem["s_mm"], 1)
                            if i >= 1:
                                emit_av(i - 1)
                        emit_av(15)

                @blk.scalar
                def _(sc):
                    for h in range(H):
                        for i in range(16):
                            sc.wait_ge(sem["s_vmax"], 16 * h + i + 1)
                            w = 16 * h + i - 1
                            if w > 0:
                                sc.wait_ge(sem["s_zs"], w)
                            w2 = 16 * h + i - 1
                            if w2 > 0:
                                sc.wait_ge(sem["s_av"], w2)
                            sc.activation(EB[i % 2][:, 0:1024], PSA[:, 0:1024], AF.Exp,
                                          bias=NMmin8[:, 0:1], scale=0.125,
                                          accum_out=Z2p[i % 2][:, 0:1]).then_inc(sem["s_exp"], 1)
                            sc.activation(EB[i % 2][:, 1024:2048], PSB[:, 0:1024], AF.Exp,
                                          bias=NMmin8[:, 0:1], scale=0.125,
                                          accum_out=Z2p[i % 2][:, 1:2]).then_inc(sem["s_exp"], 1)

                @blk.vector
                def _(v):
                    for h in range(H):
                        for i in range(17):
                            j = i - 1
                            if j >= 0:
                                v.wait_ge(sem["s_exp"], 32 * h + 2 * j + 2)
                                v.reduce_sum(Zs[:, 0:1], Z2p[j % 2][:, 0:2],
                                             AX.X).then_inc(sem["s_zs"], 1)
                            if i < 16:
                                base = 32 * h + 2 * i
                                if base > 0:
                                    v.wait_ge(sem["s_exp"], base)
                                v.wait_ge(sem["s_mm"], base + 1)
                                v.tensor_reduce(NMmin[:, 0:1], PSA[:, 0:512], AX.X,
                                                ALU.max, negate=True)
                            if j >= 0:
                                if i >= 16:
                                    v.drain()
                                v.reciprocal(Rz[:, 0:1], Zs[:, 0:1])
                            if i < 16:
                                if j < 0:
                                    v.drain()
                                v.tensor_scalar(NMmin8[:, 0:1], NMmin[:, 0:1], 0.125, None,
                                                ALU.mult).then_inc(sem["s_vmax"], 1)
                            if j >= 0:
                                w = 16 * h + j - 1
                                if w > 0:
                                    v.wait_ge(sem["s_av"], w)
                                if i >= 16:
                                    v.drain()
                                v.tensor_scalar(E2[j % 2][:, 0:1024], EB[j % 2][:, 0:1024],
                                                Rz[:, 0:1], None, ALU.mult)
                                v.tensor_scalar(E2[j % 2][:, 1024:2048], EB[j % 2][:, 1024:2048],
                                                Rz[:, 0:1], None, ALU.mult).then_inc(sem["s_scl"], 1)
                        v.wait_ge(sem["s_av"], 16 * (h + 1))
                        for ha in range(2):
                            tt = 2 * h + ha
                            v.tensor_tensor(X[:, tt * D:(tt + 1) * D], X[:, tt * D:(tt + 1) * D],
                                            AVP[ha][:, 0:1024], ALU.add).then_inc(sem["s_avev"], 1)

                if DEBUG_STAGE == "attn":
                    @blk.sync
                    def _(s):
                        s.wait_ge(sem["s_avev"], 2 * H)
                        s.dma_start(out_d.rearrange("(tt p) d -> p tt d", p=P),
                                    X[:, :].rearrange("p (tt d) -> p tt d", d=D)).then_inc(sem["s_out"], 16)
                        s.dma_start(dbg["d_xnt"], XNT[:, :]).then_inc(sem["s_out"], 16)
                        s.dma_start(dbg["d_vp"], VP[:, :]).then_inc(sem["s_out"], 16)
                        s.dma_start(dbg["d_qt"], QT[:, :]).then_inc(sem["s_out"], 16)
                        s.dma_start(dbg["d_kt"], KT[:, :]).then_inc(sem["s_out"], 16)
                        s.dma_start(dbg["d_comb"], COMB[:, :]).then_inc(sem["s_out"], 16)
                        for nm_, tt_ in [("d_s1", S1), ("d_nm", NM), ("d_ss", SS), ("d_veps", VEPS),
                                         ("d_sd", SD), ("d_rstd", RSTD), ("d_nmrs", NMRS)]:
                            s.dma_start(dbg[nm_], tt_[:, :]).then_inc(sem["s_out"], 16)
                        s.wait_ge(sem["s_out"], 208)



        if DEBUG_STAGE == "attn":
            return nc
        CAP = 128
        p3s = contextlib.ExitStack()
        W1Q0 = p3s.enter_context(nc.sbuf_tensor("W1Q0", [P, ND * 1024], BF16))
        W2Q0 = p3s.enter_context(nc.sbuf_tensor("W2Q0", [P, FT_Q * 1024], BF16))
        W1Q1 = p3s.enter_context(nc.sbuf_tensor("W1Q1", [P, ND * 1024], BF16))
        W2Q1 = p3s.enter_context(nc.sbuf_tensor("W2Q1", [P, FT_Q * 1024], BF16))
        SHS0 = p3s.enter_context(nc.sbuf_tensor("SHS0", [P, FT_Q * 2 * CAP], BF16))
        SHS1 = p3s.enter_context(nc.sbuf_tensor("SHS1", [P, FT_Q * 2 * CAP], BF16))
        XGT = p3s.enter_context(nc.sbuf_tensor("XGT", [P, TC * ND * CAP], BF16))
        XROW0 = p3s.enter_context(nc.sbuf_tensor("XROW0", [P, D], BF16))
        XROW1 = p3s.enter_context(nc.sbuf_tensor("XROW1", [P, D], BF16))
        XGB0 = p3s.enter_context(nc.sbuf_tensor("XGB0", [P, D], BF16))
        XGB1 = p3s.enter_context(nc.sbuf_tensor("XGB1", [P, D], BF16))
        RSG0 = p3s.enter_context(nc.sbuf_tensor("RSG0", [P, 2], F32))
        RSG1 = p3s.enter_context(nc.sbuf_tensor("RSG1", [P, 2], F32))
        YSC0 = p3s.enter_context(nc.sbuf_tensor("YSC0", [P, D], BF16))
        YSC1 = p3s.enter_context(nc.sbuf_tensor("YSC1", [P, D], BF16))
        TKIA = p3s.enter_context(nc.sbuf_tensor("TKIA", [P, E * TC], I32))
        TKWA = p3s.enter_context(nc.sbuf_tensor("TKWA", [P, E * TC], F32))
        TKF = p3s.enter_context(nc.sbuf_tensor("TKF", [P, 2], F32))
        SELB = p3s.enter_context(nc.sbuf_tensor("SELB", [P, E], F32))
        SELT = p3s.enter_context(nc.sbuf_tensor("SELT", [P, 256], F32))
        SCN = p3s.enter_context(nc.sbuf_tensor("SCN", [P, 256], F32))
        SLX = p3s.enter_context(nc.sbuf_tensor("SLX", [P, 256], F32))
        SLT = p3s.enter_context(nc.sbuf_tensor("SLT", [P, TC * NT], F32))
        PTT0 = p3s.enter_context(nc.sbuf_tensor("PTT0", [P, 256], F32))
        PTT1 = p3s.enter_context(nc.sbuf_tensor("PTT1", [P, 256], F32))
        with p3s:
            W1Q = [W1Q0, W1Q1]
            W2Q = [W2Q0, W2Q1]
            SHS = [SHS0, SHS1]
            XROW = [XROW0, XROW1]
            XGB = [XGB0, XGB1]
            RSG = [RSG0, RSG1]
            YSC = [YSC0, YSC1]
            YDS = YSC
            PTTB = [PTT0, PTT1]
            YP = [PSA, PSB]
            HPq = AVP0
            PRE = AVP1

            def prepP(e, c):
                return e * TC + c + 1

            from concourse.bass import IndirectOffsetOnAxis

            @blk.gpsimd
            def _(g):
                g.wait_ge(sem["s_yz"], 16 * 2 * NT)

                def gathers(e, c):
                    pc = prepP(e, c)
                    sgx = sem["s_gx0"] if c % 2 == 0 else sem["s_gx1"]
                    npar = e * (TC // 2) + c // 2
                    if npar >= 1:
                        g.wait_ge(sgx, 32 * npar)
                    if pc >= 3:
                        g.wait_ge(sem["s_xgb"], pc - 2)
                    g.wait_ge(sem["s_tki"], pc)
                    g.indirect_dma_start(
                        XROW[c % 2][:, :], None,
                        x_d[0:T, :], IndirectOffsetOnAxis(TKIA[:, e * TC + c:e * TC + c + 1], 0),
                    ).then_inc(sgx, 16)
                    g.indirect_dma_start(
                        RSG[c % 2][:, :], None,
                        rsnm_d[0:T, :], IndirectOffsetOnAxis(TKIA[:, e * TC + c:e * TC + c + 1], 0),
                    ).then_inc(sgx, 16)

                def qload(idx):
                    e, q = divmod(idx, NQ)
                    par = idx % 2
                    swq = sem["s_wq0"] if par == 0 else sem["s_wq1"]
                    if idx == 0:
                        g.wait_ge(sem["s_gather"], 256 * H)
                    elif idx == 1:
                        g.wait_ge(sem["s_av"], 16 * H)
                    else:
                        g.wait_ge(sem["s_m2"], max(0, (idx - 1) * TC))
                    if idx >= 2:
                        g.wait_ge(swq, 32 * (idx // 2))
                    src1 = w1t_d[e].rearrange("(dt p) f -> p dt f", p=P)[:, :, q * 1024:(q + 1) * 1024]
                    dst1 = W1Q[par][:, :].rearrange("p (dt f) -> p dt f", dt=ND)
                    g.dma_start(dst1, src1).then_inc(swq, 16)
                    src2 = w2t_d[e, q * 1024:(q + 1) * 1024, :].rearrange("(ft p) d -> p ft d", p=P)
                    dst2 = W2Q[par][:, :].rearrange("p (ft d) -> p ft d", ft=FT_Q)
                    g.dma_start(dst2, src2).then_inc(swq, 16)

                def scatter_one(yc):
                    e2 = yc // (NQ * TC)
                    c2 = yc % TC
                    ssc = sem["s_sc0"] if yc % 2 == 0 else sem["s_sc1"]
                    if yc // 2 >= 1:
                        g.wait_ge(ssc, 16 * (yc // 2))
                    g.wait_ge(sem["s_ysc"], yc + 1)
                    g.indirect_dma_start(
                        yd_d[0:T, :], IndirectOffsetOnAxis(TKIA[:, e2 * TC + c2:e2 * TC + c2 + 1], 0),
                        YSC[yc % 2][:, :], None,
                        compute_op=ALU.add).then_inc(ssc, 16)

                for c in range(TC):
                    gathers(0, c)
                qload(0)
                qload(1)
                for idx in range(NIDX):
                    e, q = divmod(idx, NQ)
                    if idx + 1 < NIDX:
                        qload(idx + 1)
                    for c in range(TC):
                        yc = idx * TC + c
                        if yc >= 2:
                            scatter_one(yc - 2)
                        if q == 3 and e < E - 1:
                            gathers(e + 1, c)
                scatter_one(NIDX * TC - 2)
                scatter_one(NIDX * TC - 1)

            @blk.tensor
            def _(t):
                t.wait_ge(sem["s_avev"], 2 * H)
                t.wait_ge(sem["s_rv"], 3)
                for c in range(TC):
                    t.wait_ge(sem["s_p3v"], 3 * c + 1)
                    t.transpose(PRE[0:E, 0:P], SELB[:, 0:E], IDM[:, :]).then_inc(sem["s_p3t"], 1)
                    t.wait_ge(sem["s_p3v"], 3 * c + 2)
                    t.transpose(PRE[0:E, 0:P], SELB[:, 0:E], IDM[:, :]).then_inc(sem["s_p3t"], 1)
                    t.wait_ge(sem["s_p3v"], 3 * c + 3)
                    for tau in range(2):
                        ins = t.transpose(PRE[0:P, 512 + tau * E: 512 + (tau + 1) * E],
                                          SLX[0:E, tau * P:(tau + 1) * P], IDM[0:E, 0:E])
                    ins.then_inc(sem["s_p3t"], 1)
                for k in range(E * TC):
                    e3, c3 = divmod(k, TC)
                    t.wait_ge(sem["s_ptv"], k + 1)
                    if k >= 2:
                        t.wait_ge(sem["s_tki"], k - 1)
                    off = (k % 2) * 512
                    for tau in range(2):
                        t.matmul(PRE[0:P, off:off + 1], PTTB[k % 2][:, tau * P:(tau + 1) * P],
                                 GIDC[:, 2 * c3 + tau: 2 * c3 + tau + 1],
                                 start=(tau == 0), stop=(tau == 1))
                    for tau in range(2):
                        ins = t.matmul(PRE[0:P, off + 1:off + 2], PTTB[k % 2][:, tau * P:(tau + 1) * P],
                                       COMB[:, (2 * c3 + tau) * E + e3: (2 * c3 + tau) * E + e3 + 1],
                                       start=(tau == 0), stop=(tau == 1))
                    ins.then_inc(sem["s_ptt"], 1)
                for e in range(E):
                    for q in range(NQ):
                        idx = e * NQ + q
                        par = idx % 2
                        t.wait_ge(sem["s_wq0"] if par == 0 else sem["s_wq1"],
                                  32 * (idx // 2 + 1))
                        for cp in range(TC // 2):
                            gp = (idx * (TC // 2) + cp)
                            t.wait_ge(sem["s_xt0"], 128 * (e * (TC // 2) + cp + 1))
                            t.wait_ge(sem["s_xt1"], 128 * (e * (TC // 2) + cp + 1))
                            for ft in range(FT_Q):
                                gg = gp * FT_Q + ft
                                if gg >= 2:
                                    t.wait_ge(sem["s_silu"], gg - 1)
                                hp = HPq[:, (gg % 2) * 512: (gg % 2) * 512 + 2 * CAP]
                                for dt in range(ND):
                                    ins = t.matmul(hp,
                                                   W1Q[par][:, dt * 1024 + ft * P: dt * 1024 + (ft + 1) * P],
                                                   XGT[:, dt * (TC * CAP) + 2 * cp * CAP: dt * (TC * CAP) + (2 * cp + 2) * CAP],
                                                   start=(dt == 0), stop=(dt == ND - 1))
                                ins.then_inc(sem["s_m1"], 1)
                            for sub in range(2):
                                c = 2 * cp + sub
                                yc = idx * TC + c
                                if yc >= 2:
                                    t.wait_ge(sem["s_ysc"], yc - 1)
                                for ft in range(FT_Q):
                                    gg = gp * FT_Q + ft
                                    t.wait_ge(sem["s_silu"], gg + 1)
                                    for nh in range(2):
                                        ins = t.matmul(YP[yc % 2][:, nh * 512:(nh + 1) * 512],
                                                       SHS[cp % 2][:, ft * 2 * CAP + sub * CAP: ft * 2 * CAP + (sub + 1) * CAP],
                                                       W2Q[par][:, ft * 1024 + nh * 512: ft * 1024 + (nh + 1) * 512],
                                                       start=(ft == 0), stop=(ft == FT_Q - 1))
                                ins.then_inc(sem["s_m2"], 1)

            @blk.scalar
            def _(sc):
                for idx in range(NIDX):
                    for cp in range(TC // 2):
                        gp = idx * (TC // 2) + cp
                        for ft in range(FT_Q):
                            gg = gp * FT_Q + ft
                            sc.wait_ge(sem["s_m1"], gg + 1)
                            if gp >= 2 and ft == 0:
                                sc.wait_ge(sem["s_m2"], 2 * gp - 2)
                            sc.activation(SHS[cp % 2][:, ft * 2 * CAP:(ft + 1) * 2 * CAP],
                                          HPq[:, (gg % 2) * 512: (gg % 2) * 512 + 2 * CAP],
                                          AF.Silu).then_inc(sem["s_silu"], 1)

            @blk.vector
            def _(v):
                for c in range(TC):
                    v.tensor_scalar(SELB[:, :], COMB[:, 2 * c * E:(2 * c + 1) * E],
                                    0.0, None, ALU.not_equal).then_inc(sem["s_p3v"], 1)
                    v.wait_ge(sem["s_p3t"], 3 * c + 1)
                    v.tensor_copy(SELT[0:E, 0:P], PRE[0:E, 0:P])
                    v.tensor_scalar(SELB[:, :], COMB[:, (2 * c + 1) * E:(2 * c + 2) * E],
                                    0.0, None, ALU.not_equal).then_inc(sem["s_p3v"], 1)
                    v.wait_ge(sem["s_p3t"], 3 * c + 2)
                    v.tensor_copy(SELT[0:E, P:256], PRE[0:E, 0:P])
                    v.drain()
                    v.tensor_tensor_scan(SCN[0:E, 0:256], SELT[0:E, 0:256], SELT[0:E, 0:256],
                                         0.0, ALU.add, ALU.bypass)
                    v.drain()
                    v.tensor_tensor(SCN[0:E, 0:256], SCN[0:E, 0:256], SELT[0:E, 0:256],
                                    ALU.subtract)
                    v.scalar_tensor_tensor(SLX[0:E, 0:256], SELT[0:E, 0:256], -999.0,
                                           SCN[0:E, 0:256], ALU.mult, ALU.add)
                    v.drain()
                    v.tensor_scalar(SLX[0:E, 0:256], SLX[0:E, 0:256], 999.0, None,
                                    ALU.add).then_inc(sem["s_p3v"], 1)
                    v.wait_ge(sem["s_p3t"], 3 * c + 3)
                    v.tensor_copy(SLT[:, c * NT: c * NT + E], PRE[0:P, 512:512 + E])
                    v.tensor_copy(SLT[:, c * NT + E: c * NT + 2 * E],
                                  PRE[0:P, 512 + E:512 + 2 * E])
                def prep_chunk_xgb(e, c):
                    sgx = sem["s_gx0"] if c % 2 == 0 else sem["s_gx1"]
                    v.wait_ge(sgx, 32 * (e * (TC // 2) + c // 2 + 1))
                    sxt = sem["s_xt0"] if c % 2 == 0 else sem["s_xt1"]
                    if e * (TC // 2) + c // 2 >= 1:
                        v.wait_ge(sxt, 128 * (e * (TC // 2) + c // 2))
                    v.tensor_scalar(XGB[c % 2][:, :], XROW[c % 2][:, :],
                                    RSG[c % 2][:, 0:1], RSG[c % 2][:, 1:2],
                                    ALU.mult, ALU.add).then_inc(sem["s_xgb"], 1)

                def extract(k):
                    v.wait_ge(sem["s_ptt"], k + 1)
                    off = (k % 2) * 512
                    v.tensor_copy(TKF[:, 0:2], PRE[0:P, off:off + 2])
                    v.drain()
                    v.tensor_copy(TKIA[:, k:k + 1], TKF[:, 0:1])
                    v.tensor_copy(TKWA[:, k:k + 1], TKF[:, 1:2]).then_inc(sem["s_tki"], 1)

                for k in range(E * TC):
                    e3, c3 = divmod(k, TC)
                    if k >= 2:
                        v.wait_ge(sem["s_ptt"], k - 1)
                    for tau in range(2):
                        v.tensor_scalar(PTTB[k % 2][:, tau * P:(tau + 1) * P], IOTR[:, :],
                                        SLT[:, c3 * NT + tau * E + e3: c3 * NT + tau * E + e3 + 1],
                                        None, ALU.is_equal)
                    v.drain()
                    v.nop()
                    v.sem_inc(sem["s_ptv"], 1)
                    if k >= 1:
                        extract(k - 1)
                    if 9 <= k <= 16:
                        prep_chunk_xgb(0, k - 9)
                extract(E * TC - 1)

                prep_chunk = prep_chunk_xgb
                for e in range(E):
                    for q in range(NQ):
                        idx = e * NQ + q
                        for c in range(TC):
                            yc = idx * TC + c
                            if e < E - 1 and q == 3:
                                prep_chunk(e + 1, c)
                            v.wait_ge(sem["s_m2"], yc + 1)
                            ssc = sem["s_sc0"] if yc % 2 == 0 else sem["s_sc1"]
                            if yc // 2 >= 1:
                                v.wait_ge(ssc, 16 * (yc // 2))
                            v.tensor_scalar(YSC[yc % 2][:, :], YP[yc % 2][:, 0:D],
                                            TKWA[:, e * TC + c:e * TC + c + 1], None,
                                            ALU.mult).then_inc(sem["s_ysc"], 1)
                for tt in range(NT):
                    v.wait_ge(sem["s_yl"], 16 * (tt + 1))
                    v.tensor_tensor(X[:, tt * D:(tt + 1) * D], X[:, tt * D:(tt + 1) * D],
                                    YDS[tt % 2][:, :], ALU.add).then_inc(sem["s_yev"], 1)

            @blk.sync
            def _(s):
                for e in range(E):
                    for c in range(TC):
                        pc = prepP(e, c)
                        sxt = sem["s_xt0"] if c % 2 == 0 else sem["s_xt1"]
                        npar = e * (TC // 2) + c // 2
                        if npar >= 1:
                            s.wait_ge(sxt, 128 * npar)
                        if e >= 1:
                            s.wait_ge(sem["s_m1"], 8 * ((4 * e - 1) * (TC // 2) + c // 2 + 1))
                        s.wait_ge(sem["s_xgb"], pc)
                        for dt in range(ND):
                            s.dma_start(XGT[:, dt * (TC * CAP) + c * CAP: dt * (TC * CAP) + (c + 1) * CAP],
                                        XGB[c % 2][:, dt * P:(dt + 1) * P],
                                        transpose=True).then_inc(sxt, 16)
                s.wait_ge(sem["s_sc0"], 16 * 128)
                s.wait_ge(sem["s_sc1"], 16 * 128)
                for tt in range(NT):
                    if tt >= 2:
                        s.wait_ge(sem["s_yev"], tt - 1)
                    s.dma_start(YDS[tt % 2][:, :],
                                yd_d[tt * P:(tt + 1) * P, :]).then_inc(sem["s_yl"], 16)
                for tt in range(NT):
                    s.wait_ge(sem["s_yev"], tt + 1)
                    s.dma_start(out_d[tt * P:(tt + 1) * P, :],
                                X[:, tt * D:(tt + 1) * D]).then_inc(sem["s_out"], 16)
                s.wait_ge(sem["s_out"], 16 * NT)

    return nc


_NC_CACHE = {}


def kernel(L0_attn_x, L0_attn_Mq, L0_attn_Mk, L0_attn_Mv,
           L0_ffn_router_w, L0_ffn_w1, L0_ffn_w2, L0_ffn_expert_scale):
    x = np.ascontiguousarray(np.asarray(L0_attn_x, dtype=np.float32))
    mq = np.ascontiguousarray(np.asarray(L0_attn_Mq, dtype=np.float32))
    mk = np.ascontiguousarray(np.asarray(L0_attn_Mk, dtype=np.float32))
    mv = np.ascontiguousarray(np.asarray(L0_attn_Mv, dtype=np.float32))
    rwt = np.ascontiguousarray(np.asarray(L0_ffn_router_w, dtype=np.float32).T)
    w1t = np.ascontiguousarray(np.asarray(L0_ffn_w1, dtype=np.float32).transpose(0, 2, 1))
    w2t = np.ascontiguousarray(np.asarray(L0_ffn_w2, dtype=np.float32).transpose(0, 2, 1))
    esc = np.ascontiguousarray(
        np.broadcast_to(np.asarray(L0_ffn_expert_scale, dtype=np.float32)[None, :], (P, E)))

    if "nc" not in _NC_CACHE:
        _NC_CACHE["nc"] = build_nc()
    nc = _NC_CACHE["nc"]

    in_maps = [{"x": np.ascontiguousarray(x[b]), "mq": mq, "mk": mk, "mv": mv,
                "rwt": rwt, "esc": esc, "w1t": w1t, "w2t": w2t} for b in range(8)]
    res = run_bass_kernel_spmd(nc, in_maps, core_ids=list(range(8)))
    return np.stack([np.asarray(res.results[b]["out"], dtype=np.float32) for b in range(8)], axis=0)


# version 18
# speedup vs baseline: 1.3401x; 1.3401x over previous
import contextlib
import numpy as np
import concourse.bass as bass
import concourse.mybir as mybir
from concourse.bass_utils import run_bass_kernel_spmd

F32 = mybir.dt.float32
BF16 = mybir.dt.bfloat16
AF = mybir.ActivationFunctionType
I32 = mybir.dt.int32
ALU = mybir.AluOpType
AX = mybir.AxisListType

P = 128
T = 2048
D = 1024
NT = 16
ND = 8
H = 8
E = 8
NQ = 4
FT_Q = 8
TC = 8
NIDX = E * NQ

WG_MQ, WG_MK, WG_MV, WG_RWT = 16, 32, 48, 64
def wg_q(idx):
    return 64 + 32 * idx + 32
N_PJEV = 64
DEBUG_STAGE = None


def build_nc():
    nc = bass.Bass("TRN2", target_bir_lowering=False, detect_race_conditions=False)

    x_d = nc.dram_tensor("x", [T, D], F32, kind="ExternalInput").ap()
    mq_d = nc.dram_tensor("mq", [D, 512], F32, kind="ExternalInput").ap()
    mk_d = nc.dram_tensor("mk", [D, 512], F32, kind="ExternalInput").ap()
    mv_d = nc.dram_tensor("mv", [D, D], F32, kind="ExternalInput").ap()
    rwt_d = nc.dram_tensor("rwt", [D, E], F32, kind="ExternalInput").ap()
    esc_d = nc.dram_tensor("esc", [P, E], F32, kind="ExternalInput").ap()
    w1t_d = nc.dram_tensor("w1t", [E, D, 4096], F32, kind="ExternalInput").ap()
    w2t_d = nc.dram_tensor("w2t", [E, 4096, D], F32, kind="ExternalInput").ap()
    out_d = nc.dram_tensor("out", [T, D], F32, kind="ExternalOutput").ap()
    yd_d = nc.dram_tensor("yd", [T, D], BF16, kind="Internal").ap()
    rsnm_d = nc.dram_tensor("rsnm", [T, 2], F32, kind="Internal").ap()
    if DEBUG_STAGE == "attn":
        dbg = {nm: nc.dram_tensor(nm, shp, dt, kind="ExternalOutput").ap()
               for nm, shp, dt in [("d_xnt", [P, ND * T], BF16), ("d_vp", [P, NT * D], BF16),
                                   ("d_qt", [P, 4 * T], BF16), ("d_kt", [P, 4 * T], BF16),
                                   ("d_comb", [P, NT * E], F32),
                                   ("d_s1", [P, NT], F32), ("d_nm", [P, NT], F32),
                                   ("d_ss", [P, NT], F32), ("d_veps", [P, NT], F32),
                                   ("d_sd", [P, NT], F32), ("d_rstd", [P, NT], F32),
                                   ("d_nmrs", [P, NT], F32)]}

    ctx = contextlib.ExitStack()
    with ctx:
        sb = lambda name, shape, dt: ctx.enter_context(nc.sbuf_tensor(name, shape, dt))
        X = sb("X", [P, NT * D], F32)
        XNT = sb("XNT", [P, ND * T], BF16)
        COMB = sb("COMB", [P, NT * E], F32)
        ESC = sb("ESC", [P, E], F32)
        RWTS = sb("RWTS", [P, ND * E], BF16)
        S1 = sb("S1", [P, NT], F32)
        NM = sb("NM", [P, NT], F32)
        SS = sb("SS", [P, NT], F32)
        VEPS = sb("VEPS", [P, NT], F32)
        SD = sb("SD", [P, NT], F32)
        RSTD = sb("RSTD", [P, NT], F32)
        NMRS = sb("NMRS", [P, NT], F32)
        MSQ = sb("MSQ", [P, NT], F32)
        V2 = sb("V2", [P, NT], F32)
        NM2 = sb("NM2", [P, 2], F32)
        NMmin = sb("NMmin", [P, 1], F32)
        NMmin8 = sb("NMmin8", [P, 1], F32)
        Zs = sb("Zs", [P, 1], F32)
        Rz = sb("Rz", [P, 1], F32)
        NMX = [sb(f"NMX{i}", [P, 1], F32) for i in range(2)]
        PBUF = [sb(f"PBUF{i}", [P, E], F32) for i in range(2)]
        SZ = [sb(f"SZ{i}", [P, 1], F32) for i in range(2)]
        RZr = [sb(f"RZr{i}", [P, 1], F32) for i in range(2)]
        PRB = [sb(f"PRB{i}", [P, E], F32) for i in range(2)]
        M12 = [sb(f"M12{i}", [P, 2], F32) for i in range(2)]
        MSK = [sb(f"MSK{i}", [P, E], F32) for i in range(2)]
        MKD = [sb(f"MKD{i}", [P, E], F32) for i in range(2)]
        EM2 = [sb(f"EM2{i}", [P, 2], F32) for i in range(2)]
        EPB = [sb(f"EPB{i}", [P, E], F32) for i in range(2)]
        DD = [sb(f"DD{i}", [P, 1], F32) for i in range(2)]
        RD = [sb(f"RD{i}", [P, 1], F32) for i in range(2)]
        CWA = [sb(f"CWA{i}", [P, E], F32) for i in range(2)]
        Z2p = [sb(f"Z2p{i}", [P, 2], F32) for i in range(2)]
        ZZ = sb("ZZ", [P, 512], BF16)
        IOTC = sb("IOTC", [P, 1], F32)
        IOTR = sb("IOTR", [P, P], F32)
        GIDC = sb("GIDC", [P, NT], F32)
        IDM = sb("IDM", [P, P], F32)
        RSNMS = sb("RSNMS", [P, 2 * NT], F32)
        CWB = [sb(f"CWB{i}", [P, E], F32) for i in range(2)]

        sem = {}
        for name in ["s_x", "s_hw", "s_wg", "s_xns", "s_xnt", "s_ln1", "s_ln2",
                     "s_ln3", "s_ln4", "s_pj", "s_pjev", "s_rt", "s_rv", "s_rs",
                     "s_gather", "s_schead", "s_mm", "s_vmax", "s_exp", "s_scl",
                     "s_av", "s_avev", "s_qdone", "s_m1", "s_silu", "s_m2",
                     "s_yev", "s_out", "s_zz", "s_x0", "s_x1", "s_xnt0", "s_xnt1",
                     "s_wq0", "s_wq1", "s_zs", "s_io", "s_rnv", "s_rn", "s_yz",
                     "s_p3v", "s_p3t", "s_ptv", "s_ptt", "s_tki", "s_gx0", "s_gx1",
                     "s_xgb", "s_xt0", "s_xt1", "s_ysc", "s_sc0", "s_sc1", "s_yl"]:
            sem[name] = ctx.enter_context(nc.semaphore(name))

        blk = ctx.enter_context(nc.Block())

        with (nc.sbuf_tensor("QT", [P, 4 * T], BF16) as QT,
              nc.sbuf_tensor("KT", [P, 4 * T], BF16) as KT,
              nc.sbuf_tensor("VP", [P, NT * D], BF16) as VP):

            p1 = contextlib.ExitStack()
            MQS = p1.enter_context(nc.sbuf_tensor("MQS", [P, ND * 512], BF16))
            MKS = p1.enter_context(nc.sbuf_tensor("MKS", [P, ND * 512], BF16))
            MVS = p1.enter_context(nc.sbuf_tensor("MVS", [P, ND * D], BF16))
            DUMP = p1.enter_context(nc.sbuf_tensor("DUMP", [P, D], BF16))
            XNS0 = p1.enter_context(nc.sbuf_tensor("XNS0", [P, D], BF16))
            XNS1 = p1.enter_context(nc.sbuf_tensor("XNS1", [P, D], BF16))
            LOG = p1.enter_context(nc.sbuf_tensor("LOG", [P, NT * E], F32))
            NMXb = p1.enter_context(nc.sbuf_tensor("NMXb", [P, NT], F32))
            PB = p1.enter_context(nc.sbuf_tensor("PB", [P, NT * E], F32))
            SZb = p1.enter_context(nc.sbuf_tensor("SZb", [P, NT], F32))
            RZb = p1.enter_context(nc.sbuf_tensor("RZb", [P, NT], F32))
            PRb = p1.enter_context(nc.sbuf_tensor("PRb", [P, NT * E], F32))
            M1b = p1.enter_context(nc.sbuf_tensor("M1b", [P, NT], F32))
            M2b = p1.enter_context(nc.sbuf_tensor("M2b", [P, NT], F32))
            MKb = p1.enter_context(nc.sbuf_tensor("MKb", [P, NT * E], F32))
            MDb = p1.enter_context(nc.sbuf_tensor("MDb", [P, NT * E], F32))
            EM1b = p1.enter_context(nc.sbuf_tensor("EM1b", [P, NT], F32))
            EM2b = p1.enter_context(nc.sbuf_tensor("EM2b", [P, NT], F32))
            EPb = p1.enter_context(nc.sbuf_tensor("EPb", [P, NT * E], F32))
            DDb = p1.enter_context(nc.sbuf_tensor("DDb", [P, NT], F32))
            RDb = p1.enter_context(nc.sbuf_tensor("RDb", [P, NT], F32))
            CAb = p1.enter_context(nc.sbuf_tensor("CAb", [P, NT * E], F32))
            CBb = p1.enter_context(nc.sbuf_tensor("CBb", [P, NT * E], F32))
            PJ0 = p1.enter_context(nc.psum_tensor("PJ0", [P, 512], F32))
            PJ1 = p1.enter_context(nc.psum_tensor("PJ1", [P, 512], F32))
            with p1:
                XNS = [XNS0, XNS1]
                PJ = [PJ0, PJ1]

                @blk.gpsimd
                def _(g):
                    g.iota(IOTC[:, 0:1], [[0, 1]], base=0, channel_multiplier=1,
                           allow_small_or_imprecise_dtypes=True)
                    g.iota(IOTR[:, :], [[1, P]], base=0, channel_multiplier=0,
                           allow_small_or_imprecise_dtypes=True)
                    g.iota(GIDC[:, :], [[P, NT]], base=0, channel_multiplier=1,
                           allow_small_or_imprecise_dtypes=True).then_inc(sem["s_io"], 1)
                    def xload(tt):
                        sx = sem["s_x0"] if tt % 2 == 0 else sem["s_x1"]
                        if tt >= 2:
                            g.wait_ge(sx, 16 * (tt // 2))
                        g.dma_start(X[:, tt * D:(tt + 1) * D],
                                    x_d[tt * P:(tt + 1) * P, :]).then_inc(sx, 16)
                    for tt in range(4):
                        xload(tt)
                    g.dma_start(MVS[:, :].rearrange("p (dt j) -> p dt j", dt=ND), mv_d.rearrange("(dt p) j -> p dt j", p=P)).then_inc(sem["s_wg"], 16)
                    for tt in range(4, 8):
                        xload(tt)
                    g.wait_ge(sem["s_wg"], 16)
                    g.dma_start(MQS[:, :].rearrange("p (dt j) -> p dt j", dt=ND), mq_d.rearrange("(dt p) j -> p dt j", p=P)).then_inc(sem["s_wg"], 16)
                    g.wait_ge(sem["s_wg"], 32)
                    g.dma_start(MKS[:, :].rearrange("p (dt j) -> p dt j", dt=ND), mk_d.rearrange("(dt p) j -> p dt j", p=P)).then_inc(sem["s_wg"], 16)
                    for tt in range(8, NT):
                        xload(tt)
                    g.wait_ge(sem["s_wg"], 48)
                    g.dma_start(RWTS[:, :].rearrange("p (dt j) -> p dt j", dt=ND), rwt_d.rearrange("(dt p) j -> p dt j", p=P)).then_inc(sem["s_wg"], 16)
                    g.wait_ge(sem["s_zz"], 1)
                    for tt in range(NT):
                        for hh in range(2):
                            g.dma_start(yd_d[tt * P:(tt + 1) * P, hh * 512:(hh + 1) * 512],
                                        ZZ[:, :]).then_inc(sem["s_yz"], 16)

                @blk.sync
                def _(s):
                    s.dma_start(ESC[:, :], esc_d).then_inc(sem["s_hw"], 16)
                    for tt in range(NT):
                        s.wait_ge(sem["s_xns"], tt + 1)
                        sxn = sem["s_xnt0"] if tt % 2 == 0 else sem["s_xnt1"]
                        if tt >= 2:
                            s.wait_ge(sxn, 128 * (tt // 2))
                        for dt in range(ND):
                            s.dma_start(XNT[:, dt * T + tt * P: dt * T + (tt + 1) * P],
                                        XNS[tt % 2][:, dt * P:(dt + 1) * P],
                                        transpose=True).then_inc(sxn, 16)
                    s.wait_ge(sem["s_rnv"], 1)
                    s.dma_start(rsnm_d.rearrange("(tt p) two -> p tt two", p=P),
                                RSNMS[:, :].rearrange("p (tt two) -> p tt two", two=2)
                                ).then_inc(sem["s_rn"], 16)

                @blk.tensor
                def _(t):
                    t.wait_ge(sem["s_wg"], 16)
                    gidx = 0
                    for tt in range(NT):
                        t.wait_ge(sem["s_xnt0"], 128 * (tt // 2 + 1))
                        if tt >= 1:
                            t.wait_ge(sem["s_xnt1"], 128 * ((tt + 1) // 2))
                        for jc in range(2):
                            ps = PJ[gidx % 2]
                            if gidx >= 2:
                                t.wait_ge(sem["s_pjev"], gidx - 1)
                            for dt in range(ND):
                                ins = t.matmul(ps[:, 0:512],
                                               XNT[:, dt * T + tt * P: dt * T + (tt + 1) * P],
                                               MVS[:, dt * D + jc * 512: dt * D + (jc + 1) * 512],
                                               start=(dt == 0), stop=(dt == ND - 1))
                            ins.then_inc(sem["s_pj"], 1)
                            gidx += 1
                    t.wait_ge(sem["s_xnt0"], 128 * (NT // 2))
                    t.wait_ge(sem["s_xnt1"], 128 * (NT // 2))
                    t.wait_ge(sem["s_wg"], 48)
                    for wsb in (MQS, MKS):
                        for rp in range(4):
                            for nb in range(4):
                                ps = PJ[gidx % 2]
                                t.wait_ge(sem["s_pjev"], gidx - 1)
                                for dt in range(ND):
                                    ins = t.matmul(ps[:, 0:512],
                                                   wsb[:, dt * 512 + rp * P: dt * 512 + (rp + 1) * P],
                                                   XNT[:, dt * T + nb * 512: dt * T + (nb + 1) * 512],
                                                   start=(dt == 0), stop=(dt == ND - 1))
                                ins.then_inc(sem["s_pj"], 1)
                                gidx += 1
                    t.wait_ge(sem["s_wg"], WG_RWT)
                    t.wait_ge(sem["s_pjev"], N_PJEV)
                    for tt in range(NT):
                        ps = PJ[tt % 2]
                        if tt >= 2:
                            t.wait_ge(sem["s_rt"], tt - 1)
                        for dt in range(ND):
                            ins = t.matmul(ps[:, 0:E],
                                           XNT[:, dt * T + tt * P: dt * T + (tt + 1) * P],
                                           RWTS[:, dt * E:(dt + 1) * E],
                                           start=(dt == 0), stop=(dt == ND - 1))
                        ins.then_inc(sem["s_pj"], 1)

                @blk.scalar
                def _(sc):
                    for tt in range(NT):
                        sc.wait_ge(sem["s_x0"] if tt % 2 == 0 else sem["s_x1"], 16 * (tt // 2 + 1))
                        sc.activation(DUMP[:, :], X[:, tt * D:(tt + 1) * D], AF.Square,
                                      accum_out=SS[:, tt:tt + 1]).then_inc(sem["s_ln2"], 1)
                    sc.wait_ge(sem["s_ln3"], 1)
                    sc.activation(SD[:, 0:NT], VEPS[:, 0:NT], AF.Sqrt).then_inc(sem["s_ln4"], 1)
                    gidx = 0
                    for tt in range(NT):
                        for jc in range(2):
                            sc.wait_ge(sem["s_pj"], gidx + 1)
                            sc.copy(VP[:, tt * D + jc * 512: tt * D + (jc + 1) * 512],
                                    PJ[gidx % 2][:, 0:512]).then_inc(sem["s_pjev"], 1)
                            gidx += 1
                    for dst in (QT, KT):
                        for rp in range(4):
                            for nb in range(4):
                                sc.wait_ge(sem["s_pj"], gidx + 1)
                                sc.copy(dst[:, rp * T + nb * 512: rp * T + (nb + 1) * 512],
                                        PJ[gidx % 2][:, 0:512]).then_inc(sem["s_pjev"], 1)
                                gidx += 1
                    sc.wait_ge(sem["s_rv"], 1)
                    for tt in range(NT):
                        ins = sc.activation(PB[:, tt * E:(tt + 1) * E], LOG[:, tt * E:(tt + 1) * E],
                                            AF.Exp, bias=NMXb[:, tt:tt + 1],
                                            accum_out=SZb[:, tt:tt + 1])
                    ins.then_inc(sem["s_rs"], 1)
                    sc.wait_ge(sem["s_rv"], 2)
                    sc.activation(EM1b[:, 0:NT], M1b[:, 0:NT], AF.Exp)
                    sc.activation(EM2b[:, 0:NT], M2b[:, 0:NT], AF.Exp)
                    sc.activation(EPb[:, :], PRb[:, :], AF.Exp).then_inc(sem["s_rs"], 1)

                @blk.vector
                def _(v):
                    v.memset(ZZ[:, :], 0.0).then_inc(sem["s_zz"], 1)
                    for tt in range(NT):
                        v.wait_ge(sem["s_x0"] if tt % 2 == 0 else sem["s_x1"], 16 * (tt // 2 + 1))
                        v.reduce_sum(S1[:, tt:tt + 1], X[:, tt * D:(tt + 1) * D], AX.X)
                    for tt in range(NT):
                        v.tensor_scalar(NM[:, tt:tt + 1], S1[:, tt:tt + 1], -1.0 / D, None,
                                        ALU.mult)
                    v.wait_ge(sem["s_ln2"], NT)
                    v.tensor_scalar(V2[:, 0:NT], SS[:, 0:NT], 1.0 / D, 1e-5,
                                    ALU.mult, ALU.add)
                    v.tensor_tensor(MSQ[:, 0:NT], NM[:, 0:NT], NM[:, 0:NT], ALU.mult)
                    v.drain()
                    v.tensor_tensor(VEPS[:, 0:NT], V2[:, 0:NT], MSQ[:, 0:NT],
                                    ALU.subtract).then_inc(sem["s_ln3"], 1)
                    v.wait_ge(sem["s_ln4"], 1)
                    v.reciprocal(RSTD[:, 0:NT], SD[:, 0:NT])
                    v.wait_ge(sem["s_io"], 1)
                    v.tensor_scalar(IDM[:, :], IOTR[:, :], IOTC[:, 0:1], None, ALU.is_equal)
                    v.tensor_tensor(NMRS[:, 0:NT], NM[:, 0:NT], RSTD[:, 0:NT], ALU.mult)
                    v.drain()
                    rsv = RSNMS[:, :].rearrange("p (tt two) -> p tt two", two=2)
                    v.tensor_copy(rsv[:, :, 0:1], RSTD[:, 0:NT].rearrange("p (tt one) -> p tt one", one=1))
                    v.tensor_copy(rsv[:, :, 1:2], NMRS[:, 0:NT].rearrange("p (tt one) -> p tt one", one=1)).then_inc(sem["s_rnv"], 1)
                    for tt in range(NT):
                        if tt >= 2:
                            v.wait_ge(sem["s_xnt0"] if tt % 2 == 0 else sem["s_xnt1"],
                                      128 * (tt // 2))
                        v.tensor_scalar(XNS[tt % 2][:, :], X[:, tt * D:(tt + 1) * D],
                                        RSTD[:, tt:tt + 1], NMRS[:, tt:tt + 1],
                                        ALU.mult, ALU.add).then_inc(sem["s_xns"], 1)
                    v.wait_ge(sem["s_hw"], 16)
                    for tt in range(NT):
                        v.wait_ge(sem["s_pj"], N_PJEV + tt + 1)
                        v.tensor_copy(LOG[:, tt * E:(tt + 1) * E],
                                      PJ[tt % 2][:, 0:E]).then_inc(sem["s_rt"], 1)
                    for tt in range(NT):
                        ins = v.tensor_reduce(NMXb[:, tt:tt + 1], LOG[:, tt * E:(tt + 1) * E],
                                              AX.X, ALU.max, negate=True)
                    ins.then_inc(sem["s_rv"], 1)
                    v.wait_ge(sem["s_rs"], 1)
                    for tt in range(NT):
                        v.reciprocal(RZb[:, tt:tt + 1], SZb[:, tt:tt + 1])
                    for tt in range(NT):
                        v.tensor_scalar(PRb[:, tt * E:(tt + 1) * E], PB[:, tt * E:(tt + 1) * E],
                                        RZb[:, tt:tt + 1], None, ALU.mult)
                    for tt in range(NT):
                        v.tensor_reduce(M1b[:, tt:tt + 1], PRb[:, tt * E:(tt + 1) * E], AX.X, ALU.max)
                    for tt in range(NT):
                        v.tensor_scalar(MKb[:, tt * E:(tt + 1) * E], PRb[:, tt * E:(tt + 1) * E],
                                        M1b[:, tt:tt + 1], None, ALU.is_ge)
                    for tt in range(NT):
                        v.scalar_tensor_tensor(MDb[:, tt * E:(tt + 1) * E], MKb[:, tt * E:(tt + 1) * E],
                                               -10.0, PRb[:, tt * E:(tt + 1) * E], ALU.mult, ALU.add)
                    for tt in range(NT):
                        ins = v.tensor_reduce(M2b[:, tt:tt + 1], MDb[:, tt * E:(tt + 1) * E],
                                              AX.X, ALU.max)
                    ins.then_inc(sem["s_rv"], 1)
                    v.wait_ge(sem["s_rs"], 2)
                    v.tensor_tensor(DDb[:, 0:NT], EM1b[:, 0:NT], EM2b[:, 0:NT], ALU.add)
                    for tt in range(NT):
                        v.tensor_scalar(MKb[:, tt * E:(tt + 1) * E], PRb[:, tt * E:(tt + 1) * E],
                                        M2b[:, tt:tt + 1], None, ALU.is_ge)
                    v.reciprocal(RDb[:, 0:NT], DDb[:, 0:NT])
                    v.tensor_tensor(CAb[:, :], EPb[:, :], MKb[:, :], ALU.mult)
                    v.drain()
                    for tt in range(NT):
                        v.tensor_scalar(CBb[:, tt * E:(tt + 1) * E], CAb[:, tt * E:(tt + 1) * E],
                                        RDb[:, tt:tt + 1], None, ALU.mult)
                    for tt in range(NT):
                        ins = v.tensor_tensor(COMB[:, tt * E:(tt + 1) * E], CBb[:, tt * E:(tt + 1) * E],
                                              ESC[:, :], ALU.mult)
                    ins.then_inc(sem["s_rv"], 1)

            PSA = ctx.enter_context(nc.psum_tensor("PSA", [P, 1024], F32))
            PSB = ctx.enter_context(nc.psum_tensor("PSB", [P, 1024], F32))
            AVP0 = ctx.enter_context(nc.psum_tensor("AVP0", [P, 1024], F32))
            AVP1 = ctx.enter_context(nc.psum_tensor("AVP1", [P, 1024], F32))
            with (nc.sbuf_tensor("QHp", [P, T], BF16) as QHp,
                  nc.sbuf_tensor("KHp", [P, T], BF16) as KHp,
                  nc.sbuf_tensor("EB0", [P, T], BF16) as EB0,
                  nc.sbuf_tensor("EB1", [P, T], BF16) as EB1,
                  nc.sbuf_tensor("E20", [P, T], BF16) as E20,
                  nc.sbuf_tensor("E21", [P, T], BF16) as E21):
                EB = [EB0, EB1]
                E2 = [E20, E21]
                AVP = [AVP0, AVP1]

                @blk.sync
                def _(s):
                    s.wait_ge(sem["s_pjev"], N_PJEV)
                    for h in range(H):
                        hb = (h % 2) * 64
                        if h >= 1:
                            s.wait_ge(sem["s_gather"], 256 * h)
                        if h >= 2:
                            s.wait_ge(sem["s_exp"], 32 * (h - 1))
                        for (buf, src) in ((QHp, QT), (KHp, KT)):
                            for pi in range(2):
                                for rp in range(4):
                                    s.dma_start(
                                        buf[hb:hb + 64, (2 * rp + pi) * 256:(2 * rp + pi + 1) * 256],
                                        src[pi * 64: pi * 64 + 64,
                                            rp * T + h * 256: rp * T + (h + 1) * 256],
                                    ).then_inc(sem["s_gather"], 16)

                @blk.tensor
                def _(t):
                    t.wait_ge(sem["s_rt"], NT)
                    t.wait_ge(sem["s_zz"], 1)
                    for h in range(H):
                        hb = (h % 2) * 64
                        t.wait_ge(sem["s_gather"], 256 * (h + 1))
                        if h >= 1:
                            t.wait_ge(sem["s_avev"], 2 * h)
                        for ha in range(2):
                            for bk in range(2):
                                t.matmul(AVP[ha][:, bk * 512:(bk + 1) * 512],
                                         ZZ[:, 0:128], ZZ[:, 0:512],
                                         start=True, stop=False)
                        def emit_av(j):
                            t.wait_ge(sem["s_scl"], 16 * h + j + 1)
                            rc, halfc = j // 2, j % 2
                            nmm = 0
                            for ha in range(2):
                                for r in range(8):
                                    nmm += 1
                                    ins = t.matmul(
                                        AVP[ha][:, r * P:(r + 1) * P],
                                        E2[j % 2][:, r * 256 + ha * P: r * 256 + ha * P + P],
                                        VP[:, (2 * r + halfc) * D + rc * P: (2 * r + halfc) * D + (rc + 1) * P],
                                        start=False, stop=(j == 15) and (r % 4 == 3))
                                    if nmm == 16:
                                        ins.then_inc(sem["s_av"], 1)
                        for i in range(16):
                            for half, ps in ((0, PSA), (1, PSB)):
                                w = 32 * h + 2 * (i - 1) + half + 1
                                if w > 0:
                                    t.wait_ge(sem["s_exp"], w)
                                for u in range(2):
                                    ins = t.matmul(ps[:, u * 512:(u + 1) * 512],
                                                   KHp[hb:hb + 64, i * P:(i + 1) * P],
                                                   QHp[hb:hb + 64,
                                                       half * 1024 + u * 512: half * 1024 + (u + 1) * 512],
                                                   start=True, stop=True)
                                ins.then_inc(sem["s_mm"], 1)
                            if i >= 1:
                                emit_av(i - 1)
                        emit_av(15)

                @blk.scalar
                def _(sc):
                    for h in range(H):
                        for i in range(16):
                            sc.wait_ge(sem["s_vmax"], 16 * h + i + 1)
                            w = 16 * h + i - 1
                            if w > 0:
                                sc.wait_ge(sem["s_zs"], w)
                            w2 = 16 * h + i - 1
                            if w2 > 0:
                                sc.wait_ge(sem["s_av"], w2)
                            sc.activation(EB[i % 2][:, 0:1024], PSA[:, 0:1024], AF.Exp,
                                          bias=NMmin8[:, 0:1], scale=0.125,
                                          accum_out=Z2p[i % 2][:, 0:1]).then_inc(sem["s_exp"], 1)
                            sc.activation(EB[i % 2][:, 1024:2048], PSB[:, 0:1024], AF.Exp,
                                          bias=NMmin8[:, 0:1], scale=0.125,
                                          accum_out=Z2p[i % 2][:, 1:2]).then_inc(sem["s_exp"], 1)

                @blk.vector
                def _(v):
                    for h in range(H):
                        for i in range(17):
                            j = i - 1
                            if j >= 0:
                                v.wait_ge(sem["s_exp"], 32 * h + 2 * j + 2)
                                v.reduce_sum(Zs[:, 0:1], Z2p[j % 2][:, 0:2],
                                             AX.X).then_inc(sem["s_zs"], 1)
                            if i < 16:
                                base = 32 * h + 2 * i
                                if base > 0:
                                    v.wait_ge(sem["s_exp"], base)
                                v.wait_ge(sem["s_mm"], base + 1)
                                v.tensor_reduce(NMmin[:, 0:1], PSA[:, 0:512], AX.X,
                                                ALU.max, negate=True)
                            if j >= 0:
                                if i >= 16:
                                    v.drain()
                                v.reciprocal(Rz[:, 0:1], Zs[:, 0:1])
                            if i < 16:
                                if j < 0:
                                    v.drain()
                                v.tensor_scalar(NMmin8[:, 0:1], NMmin[:, 0:1], 0.125, None,
                                                ALU.mult).then_inc(sem["s_vmax"], 1)
                            if j >= 0:
                                w = 16 * h + j - 1
                                if w > 0:
                                    v.wait_ge(sem["s_av"], w)
                                if i >= 16:
                                    v.drain()
                                v.tensor_scalar(E2[j % 2][:, 0:1024], EB[j % 2][:, 0:1024],
                                                Rz[:, 0:1], None, ALU.mult)
                                v.tensor_scalar(E2[j % 2][:, 1024:2048], EB[j % 2][:, 1024:2048],
                                                Rz[:, 0:1], None, ALU.mult).then_inc(sem["s_scl"], 1)
                        v.wait_ge(sem["s_av"], 16 * (h + 1))
                        for ha in range(2):
                            tt = 2 * h + ha
                            v.tensor_tensor(X[:, tt * D:(tt + 1) * D], X[:, tt * D:(tt + 1) * D],
                                            AVP[ha][:, 0:1024], ALU.add).then_inc(sem["s_avev"], 1)

                if DEBUG_STAGE == "attn":
                    @blk.sync
                    def _(s):
                        s.wait_ge(sem["s_avev"], 2 * H)
                        s.dma_start(out_d.rearrange("(tt p) d -> p tt d", p=P),
                                    X[:, :].rearrange("p (tt d) -> p tt d", d=D)).then_inc(sem["s_out"], 16)
                        s.dma_start(dbg["d_xnt"], XNT[:, :]).then_inc(sem["s_out"], 16)
                        s.dma_start(dbg["d_vp"], VP[:, :]).then_inc(sem["s_out"], 16)
                        s.dma_start(dbg["d_qt"], QT[:, :]).then_inc(sem["s_out"], 16)
                        s.dma_start(dbg["d_kt"], KT[:, :]).then_inc(sem["s_out"], 16)
                        s.dma_start(dbg["d_comb"], COMB[:, :]).then_inc(sem["s_out"], 16)
                        for nm_, tt_ in [("d_s1", S1), ("d_nm", NM), ("d_ss", SS), ("d_veps", VEPS),
                                         ("d_sd", SD), ("d_rstd", RSTD), ("d_nmrs", NMRS)]:
                            s.dma_start(dbg[nm_], tt_[:, :]).then_inc(sem["s_out"], 16)
                        s.wait_ge(sem["s_out"], 208)



        if DEBUG_STAGE == "attn":
            return nc
        CAP = 128
        p3s = contextlib.ExitStack()
        W1Q0 = p3s.enter_context(nc.sbuf_tensor("W1Q0", [P, ND * 1024], BF16))
        W2Q0 = p3s.enter_context(nc.sbuf_tensor("W2Q0", [P, FT_Q * 1024], BF16))
        W1Q1 = p3s.enter_context(nc.sbuf_tensor("W1Q1", [P, ND * 1024], BF16))
        W2Q1 = p3s.enter_context(nc.sbuf_tensor("W2Q1", [P, FT_Q * 1024], BF16))
        SHS0 = p3s.enter_context(nc.sbuf_tensor("SHS0", [P, FT_Q * 2 * CAP], BF16))
        SHS1 = p3s.enter_context(nc.sbuf_tensor("SHS1", [P, FT_Q * 2 * CAP], BF16))
        XGT = p3s.enter_context(nc.sbuf_tensor("XGT", [P, TC * ND * CAP], BF16))
        XROW0 = p3s.enter_context(nc.sbuf_tensor("XROW0", [P, D], BF16))
        XROW1 = p3s.enter_context(nc.sbuf_tensor("XROW1", [P, D], BF16))
        XGB0 = p3s.enter_context(nc.sbuf_tensor("XGB0", [P, D], BF16))
        XGB1 = p3s.enter_context(nc.sbuf_tensor("XGB1", [P, D], BF16))
        RSG0 = p3s.enter_context(nc.sbuf_tensor("RSG0", [P, 2], F32))
        RSG1 = p3s.enter_context(nc.sbuf_tensor("RSG1", [P, 2], F32))
        YSC0 = p3s.enter_context(nc.sbuf_tensor("YSC0", [P, D], BF16))
        YSC1 = p3s.enter_context(nc.sbuf_tensor("YSC1", [P, D], BF16))
        TKIA = p3s.enter_context(nc.sbuf_tensor("TKIA", [P, E * TC], I32))
        TKWA = p3s.enter_context(nc.sbuf_tensor("TKWA", [P, E * TC], F32))
        TKF = p3s.enter_context(nc.sbuf_tensor("TKF", [P, 2], F32))
        SELB = p3s.enter_context(nc.sbuf_tensor("SELB", [P, E], F32))
        SELT = p3s.enter_context(nc.sbuf_tensor("SELT", [P, 256], F32))
        SCN = p3s.enter_context(nc.sbuf_tensor("SCN", [P, 256], F32))
        SLX = p3s.enter_context(nc.sbuf_tensor("SLX", [P, 256], F32))
        SLT = p3s.enter_context(nc.sbuf_tensor("SLT", [P, TC * NT], F32))
        PTT0 = p3s.enter_context(nc.sbuf_tensor("PTT0", [P, 256], F32))
        PTT1 = p3s.enter_context(nc.sbuf_tensor("PTT1", [P, 256], F32))
        with p3s:
            W1Q = [W1Q0, W1Q1]
            W2Q = [W2Q0, W2Q1]
            SHS = [SHS0, SHS1]
            XROW = [XROW0, XROW1]
            XGB = [XGB0, XGB1]
            RSG = [RSG0, RSG1]
            YSC = [YSC0, YSC1]
            YDS = YSC
            PTTB = [PTT0, PTT1]
            YP = [PSA, PSB]
            HPq = AVP0
            PRE = AVP1

            def prepP(e, c):
                return e * TC + c + 1

            from concourse.bass import IndirectOffsetOnAxis

            @blk.gpsimd
            def _(g):
                g.wait_ge(sem["s_yz"], 16 * 2 * NT)

                def gathers(e, c):
                    pc = prepP(e, c)
                    sgx = sem["s_gx0"] if c % 2 == 0 else sem["s_gx1"]
                    npar = e * (TC // 2) + c // 2
                    if npar >= 1:
                        g.wait_ge(sgx, 32 * npar)
                    if pc >= 3:
                        g.wait_ge(sem["s_xgb"], pc - 2)
                    g.wait_ge(sem["s_tki"], pc)
                    g.indirect_dma_start(
                        XROW[c % 2][:, :], None,
                        x_d[0:T, :], IndirectOffsetOnAxis(TKIA[:, e * TC + c:e * TC + c + 1], 0),
                    ).then_inc(sgx, 16)
                    g.indirect_dma_start(
                        RSG[c % 2][:, :], None,
                        rsnm_d[0:T, :], IndirectOffsetOnAxis(TKIA[:, e * TC + c:e * TC + c + 1], 0),
                    ).then_inc(sgx, 16)

                def qload_slice(idx, j):
                    e, q = divmod(idx, NQ)
                    par = idx % 2
                    swq = sem["s_wq0"] if par == 0 else sem["s_wq1"]
                    if j == 0:
                        if idx == 0:
                            g.wait_ge(sem["s_gather"], 256 * H)
                        elif idx == 1:
                            g.wait_ge(sem["s_av"], 16 * H)
                        else:
                            g.wait_ge(sem["s_m2"], max(0, (idx - 1) * TC))
                        if idx >= 2:
                            g.wait_ge(swq, 64 * (idx // 2))
                    if j < 2:
                        src1 = w1t_d[e].rearrange("(dt p) f -> p dt f", p=P)[:, j * 4:(j + 1) * 4,
                                                                            q * 1024:(q + 1) * 1024]
                        dst1 = W1Q[par][:, :].rearrange("p (dt f) -> p dt f", dt=ND)[:, j * 4:(j + 1) * 4, :]
                        g.dma_start(dst1, src1).then_inc(swq, 16)
                    else:
                        jj = j - 2
                        src2 = w2t_d[e, q * 1024:(q + 1) * 1024, :].rearrange(
                            "(ft p) d -> p ft d", p=P)[:, jj * 4:(jj + 1) * 4, :]
                        dst2 = W2Q[par][:, :].rearrange("p (ft d) -> p ft d", ft=FT_Q)[:, jj * 4:(jj + 1) * 4, :]
                        g.dma_start(dst2, src2).then_inc(swq, 16)

                def qload(idx):
                    for j in range(4):
                        qload_slice(idx, j)

                def scatter_one(yc):
                    e2 = yc // (NQ * TC)
                    c2 = yc % TC
                    ssc = sem["s_sc0"] if yc % 2 == 0 else sem["s_sc1"]
                    if yc // 2 >= 1:
                        g.wait_ge(ssc, 16 * (yc // 2))
                    g.wait_ge(sem["s_ysc"], yc + 1)
                    g.indirect_dma_start(
                        yd_d[0:T, :], IndirectOffsetOnAxis(TKIA[:, e2 * TC + c2:e2 * TC + c2 + 1], 0),
                        YSC[yc % 2][:, :], None,
                        compute_op=ALU.add).then_inc(ssc, 16)

                for c in range(TC):
                    gathers(0, c)
                qload(0)
                qload(1)
                for idx in range(NIDX):
                    e, q = divmod(idx, NQ)
                    for c in range(TC):
                        yc = idx * TC + c
                        if yc >= 2:
                            scatter_one(yc - 2)
                        if idx + 1 < NIDX and 2 <= c <= 5:
                            qload_slice(idx + 1, c - 2)
                        if q == 3 and e < E - 1:
                            gathers(e + 1, c)
                scatter_one(NIDX * TC - 2)
                scatter_one(NIDX * TC - 1)

            @blk.tensor
            def _(t):
                t.wait_ge(sem["s_avev"], 2 * H)
                t.wait_ge(sem["s_rv"], 3)
                for c in range(TC):
                    t.wait_ge(sem["s_p3v"], 3 * c + 1)
                    t.transpose(PRE[0:E, 0:P], SELB[:, 0:E], IDM[:, :]).then_inc(sem["s_p3t"], 1)
                    t.wait_ge(sem["s_p3v"], 3 * c + 2)
                    t.transpose(PRE[0:E, 0:P], SELB[:, 0:E], IDM[:, :]).then_inc(sem["s_p3t"], 1)
                    t.wait_ge(sem["s_p3v"], 3 * c + 3)
                    for tau in range(2):
                        ins = t.transpose(PRE[0:P, 512 + tau * E: 512 + (tau + 1) * E],
                                          SLX[0:E, tau * P:(tau + 1) * P], IDM[0:E, 0:E])
                    ins.then_inc(sem["s_p3t"], 1)
                for k in range(E * TC):
                    e3, c3 = divmod(k, TC)
                    t.wait_ge(sem["s_ptv"], k + 1)
                    if k >= 2:
                        t.wait_ge(sem["s_tki"], k - 1)
                    off = (k % 2) * 512
                    for tau in range(2):
                        t.matmul(PRE[0:P, off:off + 1], PTTB[k % 2][:, tau * P:(tau + 1) * P],
                                 GIDC[:, 2 * c3 + tau: 2 * c3 + tau + 1],
                                 start=(tau == 0), stop=(tau == 1))
                    for tau in range(2):
                        ins = t.matmul(PRE[0:P, off + 1:off + 2], PTTB[k % 2][:, tau * P:(tau + 1) * P],
                                       COMB[:, (2 * c3 + tau) * E + e3: (2 * c3 + tau) * E + e3 + 1],
                                       start=(tau == 0), stop=(tau == 1))
                    ins.then_inc(sem["s_ptt"], 1)
                for e in range(E):
                    for q in range(NQ):
                        idx = e * NQ + q
                        par = idx % 2
                        t.wait_ge(sem["s_wq0"] if par == 0 else sem["s_wq1"],
                                  64 * (idx // 2 + 1))
                        for cp in range(TC // 2):
                            gp = (idx * (TC // 2) + cp)
                            t.wait_ge(sem["s_xt0"], 128 * (e * (TC // 2) + cp + 1))
                            t.wait_ge(sem["s_xt1"], 128 * (e * (TC // 2) + cp + 1))
                            for ft in range(FT_Q):
                                gg = gp * FT_Q + ft
                                if gg >= 2:
                                    t.wait_ge(sem["s_silu"], gg - 1)
                                hp = HPq[:, (gg % 2) * 512: (gg % 2) * 512 + 2 * CAP]
                                for dt in range(ND):
                                    ins = t.matmul(hp,
                                                   W1Q[par][:, dt * 1024 + ft * P: dt * 1024 + (ft + 1) * P],
                                                   XGT[:, dt * (TC * CAP) + 2 * cp * CAP: dt * (TC * CAP) + (2 * cp + 2) * CAP],
                                                   start=(dt == 0), stop=(dt == ND - 1))
                                ins.then_inc(sem["s_m1"], 1)
                            for sub in range(2):
                                c = 2 * cp + sub
                                yc = idx * TC + c
                                if yc >= 2:
                                    t.wait_ge(sem["s_ysc"], yc - 1)
                                for ft in range(FT_Q):
                                    gg = gp * FT_Q + ft
                                    t.wait_ge(sem["s_silu"], gg + 1)
                                    for nh in range(2):
                                        ins = t.matmul(YP[yc % 2][:, nh * 512:(nh + 1) * 512],
                                                       SHS[cp % 2][:, ft * 2 * CAP + sub * CAP: ft * 2 * CAP + (sub + 1) * CAP],
                                                       W2Q[par][:, ft * 1024 + nh * 512: ft * 1024 + (nh + 1) * 512],
                                                       start=(ft == 0), stop=(ft == FT_Q - 1))
                                ins.then_inc(sem["s_m2"], 1)

            @blk.scalar
            def _(sc):
                for idx in range(NIDX):
                    for cp in range(TC // 2):
                        gp = idx * (TC // 2) + cp
                        for ft in range(FT_Q):
                            gg = gp * FT_Q + ft
                            sc.wait_ge(sem["s_m1"], gg + 1)
                            if gp >= 2 and ft == 0:
                                sc.wait_ge(sem["s_m2"], 2 * gp - 2)
                            sc.activation(SHS[cp % 2][:, ft * 2 * CAP:(ft + 1) * 2 * CAP],
                                          HPq[:, (gg % 2) * 512: (gg % 2) * 512 + 2 * CAP],
                                          AF.Silu).then_inc(sem["s_silu"], 1)

            @blk.vector
            def _(v):
                for c in range(TC):
                    v.tensor_scalar(SELB[:, :], COMB[:, 2 * c * E:(2 * c + 1) * E],
                                    0.0, None, ALU.not_equal).then_inc(sem["s_p3v"], 1)
                    v.wait_ge(sem["s_p3t"], 3 * c + 1)
                    v.tensor_copy(SELT[0:E, 0:P], PRE[0:E, 0:P])
                    v.tensor_scalar(SELB[:, :], COMB[:, (2 * c + 1) * E:(2 * c + 2) * E],
                                    0.0, None, ALU.not_equal).then_inc(sem["s_p3v"], 1)
                    v.wait_ge(sem["s_p3t"], 3 * c + 2)
                    v.tensor_copy(SELT[0:E, P:256], PRE[0:E, 0:P])
                    v.drain()
                    v.tensor_tensor_scan(SCN[0:E, 0:256], SELT[0:E, 0:256], SELT[0:E, 0:256],
                                         0.0, ALU.add, ALU.bypass)
                    v.drain()
                    v.tensor_tensor(SCN[0:E, 0:256], SCN[0:E, 0:256], SELT[0:E, 0:256],
                                    ALU.subtract)
                    v.scalar_tensor_tensor(SLX[0:E, 0:256], SELT[0:E, 0:256], -999.0,
                                           SCN[0:E, 0:256], ALU.mult, ALU.add)
                    v.drain()
                    v.tensor_scalar(SLX[0:E, 0:256], SLX[0:E, 0:256], 999.0, None,
                                    ALU.add).then_inc(sem["s_p3v"], 1)
                    v.wait_ge(sem["s_p3t"], 3 * c + 3)
                    v.tensor_copy(SLT[:, c * NT: c * NT + E], PRE[0:P, 512:512 + E])
                    v.tensor_copy(SLT[:, c * NT + E: c * NT + 2 * E],
                                  PRE[0:P, 512 + E:512 + 2 * E])
                def prep_chunk_xgb(e, c):
                    sgx = sem["s_gx0"] if c % 2 == 0 else sem["s_gx1"]
                    v.wait_ge(sgx, 32 * (e * (TC // 2) + c // 2 + 1))
                    sxt = sem["s_xt0"] if c % 2 == 0 else sem["s_xt1"]
                    if e * (TC // 2) + c // 2 >= 1:
                        v.wait_ge(sxt, 128 * (e * (TC // 2) + c // 2))
                    v.tensor_scalar(XGB[c % 2][:, :], XROW[c % 2][:, :],
                                    RSG[c % 2][:, 0:1], RSG[c % 2][:, 1:2],
                                    ALU.mult, ALU.add).then_inc(sem["s_xgb"], 1)

                def extract(k):
                    v.wait_ge(sem["s_ptt"], k + 1)
                    off = (k % 2) * 512
                    v.tensor_copy(TKF[:, 0:2], PRE[0:P, off:off + 2])
                    v.drain()
                    v.tensor_copy(TKIA[:, k:k + 1], TKF[:, 0:1])
                    v.tensor_copy(TKWA[:, k:k + 1], TKF[:, 1:2]).then_inc(sem["s_tki"], 1)

                for k in range(E * TC):
                    e3, c3 = divmod(k, TC)
                    if k >= 2:
                        v.wait_ge(sem["s_ptt"], k - 1)
                    for tau in range(2):
                        v.tensor_scalar(PTTB[k % 2][:, tau * P:(tau + 1) * P], IOTR[:, :],
                                        SLT[:, c3 * NT + tau * E + e3: c3 * NT + tau * E + e3 + 1],
                                        None, ALU.is_equal)
                    v.drain()
                    v.nop()
                    v.sem_inc(sem["s_ptv"], 1)
                    if k >= 1:
                        extract(k - 1)
                    if 9 <= k <= 16:
                        prep_chunk_xgb(0, k - 9)
                extract(E * TC - 1)

                prep_chunk = prep_chunk_xgb
                for e in range(E):
                    for q in range(NQ):
                        idx = e * NQ + q
                        for c in range(TC):
                            yc = idx * TC + c
                            if e < E - 1 and q == 3:
                                prep_chunk(e + 1, c)
                            v.wait_ge(sem["s_m2"], yc + 1)
                            ssc = sem["s_sc0"] if yc % 2 == 0 else sem["s_sc1"]
                            if yc // 2 >= 1:
                                v.wait_ge(ssc, 16 * (yc // 2))
                            v.tensor_scalar(YSC[yc % 2][:, :], YP[yc % 2][:, 0:D],
                                            TKWA[:, e * TC + c:e * TC + c + 1], None,
                                            ALU.mult).then_inc(sem["s_ysc"], 1)
                for tt in range(NT):
                    v.wait_ge(sem["s_yl"], 16 * (tt + 1))
                    v.tensor_tensor(X[:, tt * D:(tt + 1) * D], X[:, tt * D:(tt + 1) * D],
                                    YDS[tt % 2][:, :], ALU.add).then_inc(sem["s_yev"], 1)

            @blk.sync
            def _(s):
                for e in range(E):
                    for c in range(TC):
                        pc = prepP(e, c)
                        sxt = sem["s_xt0"] if c % 2 == 0 else sem["s_xt1"]
                        npar = e * (TC // 2) + c // 2
                        if npar >= 1:
                            s.wait_ge(sxt, 128 * npar)
                        if e >= 1:
                            s.wait_ge(sem["s_m1"], 8 * ((4 * e - 1) * (TC // 2) + c // 2 + 1))
                        s.wait_ge(sem["s_xgb"], pc)
                        for dt in range(ND):
                            s.dma_start(XGT[:, dt * (TC * CAP) + c * CAP: dt * (TC * CAP) + (c + 1) * CAP],
                                        XGB[c % 2][:, dt * P:(dt + 1) * P],
                                        transpose=True).then_inc(sxt, 16)
                s.wait_ge(sem["s_sc0"], 16 * 128)
                s.wait_ge(sem["s_sc1"], 16 * 128)
                for tt in range(NT):
                    if tt >= 2:
                        s.wait_ge(sem["s_yev"], tt - 1)
                    s.dma_start(YDS[tt % 2][:, :],
                                yd_d[tt * P:(tt + 1) * P, :]).then_inc(sem["s_yl"], 16)
                for tt in range(NT):
                    s.wait_ge(sem["s_yev"], tt + 1)
                    s.dma_start(out_d[tt * P:(tt + 1) * P, :],
                                X[:, tt * D:(tt + 1) * D]).then_inc(sem["s_out"], 16)
                s.wait_ge(sem["s_out"], 16 * NT)

    return nc


_NC_CACHE = {}


def kernel(L0_attn_x, L0_attn_Mq, L0_attn_Mk, L0_attn_Mv,
           L0_ffn_router_w, L0_ffn_w1, L0_ffn_w2, L0_ffn_expert_scale):
    x = np.ascontiguousarray(np.asarray(L0_attn_x, dtype=np.float32))
    mq = np.ascontiguousarray(np.asarray(L0_attn_Mq, dtype=np.float32))
    mk = np.ascontiguousarray(np.asarray(L0_attn_Mk, dtype=np.float32))
    mv = np.ascontiguousarray(np.asarray(L0_attn_Mv, dtype=np.float32))
    rwt = np.ascontiguousarray(np.asarray(L0_ffn_router_w, dtype=np.float32).T)
    w1t = np.ascontiguousarray(np.asarray(L0_ffn_w1, dtype=np.float32).transpose(0, 2, 1))
    w2t = np.ascontiguousarray(np.asarray(L0_ffn_w2, dtype=np.float32).transpose(0, 2, 1))
    esc = np.ascontiguousarray(
        np.broadcast_to(np.asarray(L0_ffn_expert_scale, dtype=np.float32)[None, :], (P, E)))

    if "nc" not in _NC_CACHE:
        _NC_CACHE["nc"] = build_nc()
    nc = _NC_CACHE["nc"]

    in_maps = [{"x": np.ascontiguousarray(x[b]), "mq": mq, "mk": mk, "mv": mv,
                "rwt": rwt, "esc": esc, "w1t": w1t, "w2t": w2t} for b in range(8)]
    res = run_bass_kernel_spmd(nc, in_maps, core_ids=list(range(8)))
    return np.stack([np.asarray(res.results[b]["out"], dtype=np.float32) for b in range(8)], axis=0)


# version 19 (final)
# speedup vs baseline: 1.3477x; 1.0057x over previous
import contextlib
import numpy as np
import concourse.bass as bass
import concourse.mybir as mybir
from concourse.bass_utils import run_bass_kernel_spmd

F32 = mybir.dt.float32
BF16 = mybir.dt.bfloat16
AF = mybir.ActivationFunctionType
I32 = mybir.dt.int32
ALU = mybir.AluOpType
AX = mybir.AxisListType

P = 128
T = 2048
D = 1024
NT = 16
ND = 8
H = 8
E = 8
NQ = 4
FT_Q = 8
TC = 8
NIDX = E * NQ

WG_MQ, WG_MK, WG_MV, WG_RWT = 16, 32, 48, 64
def wg_q(idx):
    return 64 + 32 * idx + 32
N_PJEV = 64
DEBUG_STAGE = None


def build_nc():
    nc = bass.Bass("TRN2", target_bir_lowering=False, detect_race_conditions=False)

    x_d = nc.dram_tensor("x", [T, D], F32, kind="ExternalInput").ap()
    mq_d = nc.dram_tensor("mq", [D, 512], F32, kind="ExternalInput").ap()
    mk_d = nc.dram_tensor("mk", [D, 512], F32, kind="ExternalInput").ap()
    mv_d = nc.dram_tensor("mv", [D, D], F32, kind="ExternalInput").ap()
    rwt_d = nc.dram_tensor("rwt", [D, E], F32, kind="ExternalInput").ap()
    esc_d = nc.dram_tensor("esc", [P, E], F32, kind="ExternalInput").ap()
    w1t_d = nc.dram_tensor("w1t", [E, D, 4096], F32, kind="ExternalInput").ap()
    w2t_d = nc.dram_tensor("w2t", [E, 4096, D], F32, kind="ExternalInput").ap()
    out_d = nc.dram_tensor("out", [T, D], F32, kind="ExternalOutput").ap()
    yd_d = nc.dram_tensor("yd", [T + P, D], BF16, kind="Internal").ap()
    rsnm_d = nc.dram_tensor("rsnm", [T, 2], F32, kind="Internal").ap()
    if DEBUG_STAGE == "attn":
        dbg = {nm: nc.dram_tensor(nm, shp, dt, kind="ExternalOutput").ap()
               for nm, shp, dt in [("d_xnt", [P, ND * T], BF16), ("d_vp", [P, NT * D], BF16),
                                   ("d_qt", [P, 4 * T], BF16), ("d_kt", [P, 4 * T], BF16),
                                   ("d_comb", [P, NT * E], F32),
                                   ("d_s1", [P, NT], F32), ("d_nm", [P, NT], F32),
                                   ("d_ss", [P, NT], F32), ("d_veps", [P, NT], F32),
                                   ("d_sd", [P, NT], F32), ("d_rstd", [P, NT], F32),
                                   ("d_nmrs", [P, NT], F32)]}

    ctx = contextlib.ExitStack()
    with ctx:
        sb = lambda name, shape, dt: ctx.enter_context(nc.sbuf_tensor(name, shape, dt))
        X = sb("X", [P, NT * D], F32)
        XNT = sb("XNT", [P, ND * T], BF16)
        COMB = sb("COMB", [P, NT * E], F32)
        ESC = sb("ESC", [P, E], F32)
        RWTS = sb("RWTS", [P, ND * E], BF16)
        S1 = sb("S1", [P, NT], F32)
        NM = sb("NM", [P, NT], F32)
        SS = sb("SS", [P, NT], F32)
        VEPS = sb("VEPS", [P, NT], F32)
        SD = sb("SD", [P, NT], F32)
        RSTD = sb("RSTD", [P, NT], F32)
        NMRS = sb("NMRS", [P, NT], F32)
        MSQ = sb("MSQ", [P, NT], F32)
        V2 = sb("V2", [P, NT], F32)
        NM2 = sb("NM2", [P, 2], F32)
        NMmin = sb("NMmin", [P, 1], F32)
        NMmin8 = sb("NMmin8", [P, 1], F32)
        Zs = sb("Zs", [P, 1], F32)
        Rz = sb("Rz", [P, 1], F32)
        NMX = [sb(f"NMX{i}", [P, 1], F32) for i in range(2)]
        PBUF = [sb(f"PBUF{i}", [P, E], F32) for i in range(2)]
        SZ = [sb(f"SZ{i}", [P, 1], F32) for i in range(2)]
        RZr = [sb(f"RZr{i}", [P, 1], F32) for i in range(2)]
        PRB = [sb(f"PRB{i}", [P, E], F32) for i in range(2)]
        M12 = [sb(f"M12{i}", [P, 2], F32) for i in range(2)]
        MSK = [sb(f"MSK{i}", [P, E], F32) for i in range(2)]
        MKD = [sb(f"MKD{i}", [P, E], F32) for i in range(2)]
        EM2 = [sb(f"EM2{i}", [P, 2], F32) for i in range(2)]
        EPB = [sb(f"EPB{i}", [P, E], F32) for i in range(2)]
        DD = [sb(f"DD{i}", [P, 1], F32) for i in range(2)]
        RD = [sb(f"RD{i}", [P, 1], F32) for i in range(2)]
        CWA = [sb(f"CWA{i}", [P, E], F32) for i in range(2)]
        Z2p = [sb(f"Z2p{i}", [P, 2], F32) for i in range(2)]
        ZZ = sb("ZZ", [P, 512], BF16)
        IOTC = sb("IOTC", [P, 1], F32)
        IOTR = sb("IOTR", [P, P], F32)
        GIDC = sb("GIDC", [P, NT], F32)
        IDM = sb("IDM", [P, P], F32)
        RSNMS = sb("RSNMS", [P, 2 * NT], F32)
        CWB = [sb(f"CWB{i}", [P, E], F32) for i in range(2)]

        sem = {}
        for name in ["s_x", "s_hw", "s_wg", "s_xns", "s_xnt", "s_ln1", "s_ln2",
                     "s_ln3", "s_ln4", "s_pj", "s_pjev", "s_rt", "s_rv", "s_rs",
                     "s_gather", "s_schead", "s_mm", "s_vmax", "s_exp", "s_scl",
                     "s_av", "s_avev", "s_qdone", "s_m1", "s_silu", "s_m2",
                     "s_yev", "s_out", "s_zz", "s_x0", "s_x1", "s_xnt0", "s_xnt1",
                     "s_wq0", "s_wq1", "s_zs", "s_io", "s_rnv", "s_rn", "s_yz",
                     "s_p3v", "s_p3t", "s_ptv", "s_ptt", "s_tki", "s_gx0", "s_gx1",
                     "s_xgb", "s_xt0", "s_xt1", "s_ysc", "s_sc0", "s_sc1", "s_yl"]:
            sem[name] = ctx.enter_context(nc.semaphore(name))

        blk = ctx.enter_context(nc.Block())

        with (nc.sbuf_tensor("QT", [P, 4 * T], BF16) as QT,
              nc.sbuf_tensor("KT", [P, 4 * T], BF16) as KT,
              nc.sbuf_tensor("VP", [P, NT * D], BF16) as VP):

            p1 = contextlib.ExitStack()
            MQS = p1.enter_context(nc.sbuf_tensor("MQS", [P, ND * 512], BF16))
            MKS = p1.enter_context(nc.sbuf_tensor("MKS", [P, ND * 512], BF16))
            MVS = p1.enter_context(nc.sbuf_tensor("MVS", [P, ND * D], BF16))
            DUMP = p1.enter_context(nc.sbuf_tensor("DUMP", [P, D], BF16))
            XNS0 = p1.enter_context(nc.sbuf_tensor("XNS0", [P, D], BF16))
            XNS1 = p1.enter_context(nc.sbuf_tensor("XNS1", [P, D], BF16))
            LOG = p1.enter_context(nc.sbuf_tensor("LOG", [P, NT * E], F32))
            NMXb = p1.enter_context(nc.sbuf_tensor("NMXb", [P, NT], F32))
            PB = p1.enter_context(nc.sbuf_tensor("PB", [P, NT * E], F32))
            SZb = p1.enter_context(nc.sbuf_tensor("SZb", [P, NT], F32))
            RZb = p1.enter_context(nc.sbuf_tensor("RZb", [P, NT], F32))
            PRb = p1.enter_context(nc.sbuf_tensor("PRb", [P, NT * E], F32))
            M1b = p1.enter_context(nc.sbuf_tensor("M1b", [P, NT], F32))
            M2b = p1.enter_context(nc.sbuf_tensor("M2b", [P, NT], F32))
            MKb = p1.enter_context(nc.sbuf_tensor("MKb", [P, NT * E], F32))
            MDb = p1.enter_context(nc.sbuf_tensor("MDb", [P, NT * E], F32))
            EM1b = p1.enter_context(nc.sbuf_tensor("EM1b", [P, NT], F32))
            EM2b = p1.enter_context(nc.sbuf_tensor("EM2b", [P, NT], F32))
            EPb = p1.enter_context(nc.sbuf_tensor("EPb", [P, NT * E], F32))
            DDb = p1.enter_context(nc.sbuf_tensor("DDb", [P, NT], F32))
            RDb = p1.enter_context(nc.sbuf_tensor("RDb", [P, NT], F32))
            CAb = p1.enter_context(nc.sbuf_tensor("CAb", [P, NT * E], F32))
            CBb = p1.enter_context(nc.sbuf_tensor("CBb", [P, NT * E], F32))
            PJ0 = p1.enter_context(nc.psum_tensor("PJ0", [P, 512], F32))
            PJ1 = p1.enter_context(nc.psum_tensor("PJ1", [P, 512], F32))
            with p1:
                XNS = [XNS0, XNS1]
                PJ = [PJ0, PJ1]

                @blk.gpsimd
                def _(g):
                    g.iota(IOTC[:, 0:1], [[0, 1]], base=0, channel_multiplier=1,
                           allow_small_or_imprecise_dtypes=True)
                    g.iota(IOTR[:, :], [[1, P]], base=0, channel_multiplier=0,
                           allow_small_or_imprecise_dtypes=True)
                    g.iota(GIDC[:, :], [[P, NT]], base=0, channel_multiplier=1,
                           allow_small_or_imprecise_dtypes=True).then_inc(sem["s_io"], 1)
                    def xload(tt):
                        sx = sem["s_x0"] if tt % 2 == 0 else sem["s_x1"]
                        if tt >= 2:
                            g.wait_ge(sx, 16 * (tt // 2))
                        g.dma_start(X[:, tt * D:(tt + 1) * D],
                                    x_d[tt * P:(tt + 1) * P, :]).then_inc(sx, 16)
                    for tt in range(4):
                        xload(tt)
                    g.dma_start(MVS[:, :].rearrange("p (dt j) -> p dt j", dt=ND), mv_d.rearrange("(dt p) j -> p dt j", p=P)).then_inc(sem["s_wg"], 16)
                    for tt in range(4, 8):
                        xload(tt)
                    g.wait_ge(sem["s_wg"], 16)
                    g.dma_start(MQS[:, :].rearrange("p (dt j) -> p dt j", dt=ND), mq_d.rearrange("(dt p) j -> p dt j", p=P)).then_inc(sem["s_wg"], 16)
                    g.wait_ge(sem["s_wg"], 32)
                    g.dma_start(MKS[:, :].rearrange("p (dt j) -> p dt j", dt=ND), mk_d.rearrange("(dt p) j -> p dt j", p=P)).then_inc(sem["s_wg"], 16)
                    for tt in range(8, NT):
                        xload(tt)
                    g.wait_ge(sem["s_wg"], 48)
                    g.dma_start(RWTS[:, :].rearrange("p (dt j) -> p dt j", dt=ND), rwt_d.rearrange("(dt p) j -> p dt j", p=P)).then_inc(sem["s_wg"], 16)
                    g.wait_ge(sem["s_zz"], 1)
                    for tt in range(NT):
                        for hh in range(2):
                            g.dma_start(yd_d[tt * P:(tt + 1) * P, hh * 512:(hh + 1) * 512],
                                        ZZ[:, :]).then_inc(sem["s_yz"], 16)

                @blk.sync
                def _(s):
                    s.dma_start(ESC[:, :], esc_d).then_inc(sem["s_hw"], 16)
                    for tt in range(NT):
                        s.wait_ge(sem["s_xns"], tt + 1)
                        sxn = sem["s_xnt0"] if tt % 2 == 0 else sem["s_xnt1"]
                        if tt >= 2:
                            s.wait_ge(sxn, 128 * (tt // 2))
                        for dt in range(ND):
                            s.dma_start(XNT[:, dt * T + tt * P: dt * T + (tt + 1) * P],
                                        XNS[tt % 2][:, dt * P:(dt + 1) * P],
                                        transpose=True).then_inc(sxn, 16)
                    s.wait_ge(sem["s_rnv"], 1)
                    s.dma_start(rsnm_d.rearrange("(tt p) two -> p tt two", p=P),
                                RSNMS[:, :].rearrange("p (tt two) -> p tt two", two=2)
                                ).then_inc(sem["s_rn"], 16)

                @blk.tensor
                def _(t):
                    t.wait_ge(sem["s_wg"], 16)
                    gidx = 0
                    for tt in range(NT):
                        t.wait_ge(sem["s_xnt0"], 128 * (tt // 2 + 1))
                        if tt >= 1:
                            t.wait_ge(sem["s_xnt1"], 128 * ((tt + 1) // 2))
                        for jc in range(2):
                            ps = PJ[gidx % 2]
                            if gidx >= 2:
                                t.wait_ge(sem["s_pjev"], gidx - 1)
                            for dt in range(ND):
                                ins = t.matmul(ps[:, 0:512],
                                               XNT[:, dt * T + tt * P: dt * T + (tt + 1) * P],
                                               MVS[:, dt * D + jc * 512: dt * D + (jc + 1) * 512],
                                               start=(dt == 0), stop=(dt == ND - 1))
                            ins.then_inc(sem["s_pj"], 1)
                            gidx += 1
                    t.wait_ge(sem["s_xnt0"], 128 * (NT // 2))
                    t.wait_ge(sem["s_xnt1"], 128 * (NT // 2))
                    t.wait_ge(sem["s_wg"], 48)
                    for wsb in (MQS, MKS):
                        for rp in range(4):
                            for nb in range(4):
                                ps = PJ[gidx % 2]
                                t.wait_ge(sem["s_pjev"], gidx - 1)
                                for dt in range(ND):
                                    ins = t.matmul(ps[:, 0:512],
                                                   wsb[:, dt * 512 + rp * P: dt * 512 + (rp + 1) * P],
                                                   XNT[:, dt * T + nb * 512: dt * T + (nb + 1) * 512],
                                                   start=(dt == 0), stop=(dt == ND - 1))
                                ins.then_inc(sem["s_pj"], 1)
                                gidx += 1
                    t.wait_ge(sem["s_wg"], WG_RWT)
                    t.wait_ge(sem["s_pjev"], N_PJEV)
                    for tt in range(NT):
                        ps = PJ[tt % 2]
                        if tt >= 2:
                            t.wait_ge(sem["s_rt"], tt - 1)
                        for dt in range(ND):
                            ins = t.matmul(ps[:, 0:E],
                                           XNT[:, dt * T + tt * P: dt * T + (tt + 1) * P],
                                           RWTS[:, dt * E:(dt + 1) * E],
                                           start=(dt == 0), stop=(dt == ND - 1))
                        ins.then_inc(sem["s_pj"], 1)

                @blk.scalar
                def _(sc):
                    for tt in range(NT):
                        sc.wait_ge(sem["s_x0"] if tt % 2 == 0 else sem["s_x1"], 16 * (tt // 2 + 1))
                        sc.activation(DUMP[:, :], X[:, tt * D:(tt + 1) * D], AF.Square,
                                      accum_out=SS[:, tt:tt + 1]).then_inc(sem["s_ln2"], 1)
                    sc.wait_ge(sem["s_ln3"], 1)
                    sc.activation(SD[:, 0:NT], VEPS[:, 0:NT], AF.Sqrt).then_inc(sem["s_ln4"], 1)
                    gidx = 0
                    for tt in range(NT):
                        for jc in range(2):
                            sc.wait_ge(sem["s_pj"], gidx + 1)
                            sc.copy(VP[:, tt * D + jc * 512: tt * D + (jc + 1) * 512],
                                    PJ[gidx % 2][:, 0:512]).then_inc(sem["s_pjev"], 1)
                            gidx += 1
                    for dst in (QT, KT):
                        for rp in range(4):
                            for nb in range(4):
                                sc.wait_ge(sem["s_pj"], gidx + 1)
                                sc.copy(dst[:, rp * T + nb * 512: rp * T + (nb + 1) * 512],
                                        PJ[gidx % 2][:, 0:512]).then_inc(sem["s_pjev"], 1)
                                gidx += 1
                    sc.wait_ge(sem["s_rv"], 1)
                    for tt in range(NT):
                        ins = sc.activation(PB[:, tt * E:(tt + 1) * E], LOG[:, tt * E:(tt + 1) * E],
                                            AF.Exp, bias=NMXb[:, tt:tt + 1],
                                            accum_out=SZb[:, tt:tt + 1])
                    ins.then_inc(sem["s_rs"], 1)
                    sc.wait_ge(sem["s_rv"], 2)
                    sc.activation(EM1b[:, 0:NT], M1b[:, 0:NT], AF.Exp)
                    sc.activation(EM2b[:, 0:NT], M2b[:, 0:NT], AF.Exp)
                    sc.activation(EPb[:, :], PRb[:, :], AF.Exp).then_inc(sem["s_rs"], 1)

                @blk.vector
                def _(v):
                    v.memset(ZZ[:, :], 0.0).then_inc(sem["s_zz"], 1)
                    for tt in range(NT):
                        v.wait_ge(sem["s_x0"] if tt % 2 == 0 else sem["s_x1"], 16 * (tt // 2 + 1))
                        v.reduce_sum(S1[:, tt:tt + 1], X[:, tt * D:(tt + 1) * D], AX.X)
                    for tt in range(NT):
                        v.tensor_scalar(NM[:, tt:tt + 1], S1[:, tt:tt + 1], -1.0 / D, None,
                                        ALU.mult)
                    v.wait_ge(sem["s_ln2"], NT)
                    v.tensor_scalar(V2[:, 0:NT], SS[:, 0:NT], 1.0 / D, 1e-5,
                                    ALU.mult, ALU.add)
                    v.tensor_tensor(MSQ[:, 0:NT], NM[:, 0:NT], NM[:, 0:NT], ALU.mult)
                    v.drain()
                    v.tensor_tensor(VEPS[:, 0:NT], V2[:, 0:NT], MSQ[:, 0:NT],
                                    ALU.subtract).then_inc(sem["s_ln3"], 1)
                    v.wait_ge(sem["s_ln4"], 1)
                    v.reciprocal(RSTD[:, 0:NT], SD[:, 0:NT])
                    v.wait_ge(sem["s_io"], 1)
                    v.tensor_scalar(IDM[:, :], IOTR[:, :], IOTC[:, 0:1], None, ALU.is_equal)
                    v.tensor_tensor(NMRS[:, 0:NT], NM[:, 0:NT], RSTD[:, 0:NT], ALU.mult)
                    v.drain()
                    rsv = RSNMS[:, :].rearrange("p (tt two) -> p tt two", two=2)
                    v.tensor_copy(rsv[:, :, 0:1], RSTD[:, 0:NT].rearrange("p (tt one) -> p tt one", one=1))
                    v.tensor_copy(rsv[:, :, 1:2], NMRS[:, 0:NT].rearrange("p (tt one) -> p tt one", one=1)).then_inc(sem["s_rnv"], 1)
                    for tt in range(NT):
                        if tt >= 2:
                            v.wait_ge(sem["s_xnt0"] if tt % 2 == 0 else sem["s_xnt1"],
                                      128 * (tt // 2))
                        v.tensor_scalar(XNS[tt % 2][:, :], X[:, tt * D:(tt + 1) * D],
                                        RSTD[:, tt:tt + 1], NMRS[:, tt:tt + 1],
                                        ALU.mult, ALU.add).then_inc(sem["s_xns"], 1)
                    v.wait_ge(sem["s_hw"], 16)
                    for tt in range(NT):
                        v.wait_ge(sem["s_pj"], N_PJEV + tt + 1)
                        v.tensor_copy(LOG[:, tt * E:(tt + 1) * E],
                                      PJ[tt % 2][:, 0:E]).then_inc(sem["s_rt"], 1)
                    for tt in range(NT):
                        ins = v.tensor_reduce(NMXb[:, tt:tt + 1], LOG[:, tt * E:(tt + 1) * E],
                                              AX.X, ALU.max, negate=True)
                    ins.then_inc(sem["s_rv"], 1)
                    v.wait_ge(sem["s_rs"], 1)
                    for tt in range(NT):
                        v.reciprocal(RZb[:, tt:tt + 1], SZb[:, tt:tt + 1])
                    for tt in range(NT):
                        v.tensor_scalar(PRb[:, tt * E:(tt + 1) * E], PB[:, tt * E:(tt + 1) * E],
                                        RZb[:, tt:tt + 1], None, ALU.mult)
                    for tt in range(NT):
                        v.tensor_reduce(M1b[:, tt:tt + 1], PRb[:, tt * E:(tt + 1) * E], AX.X, ALU.max)
                    for tt in range(NT):
                        v.tensor_scalar(MKb[:, tt * E:(tt + 1) * E], PRb[:, tt * E:(tt + 1) * E],
                                        M1b[:, tt:tt + 1], None, ALU.is_ge)
                    for tt in range(NT):
                        v.scalar_tensor_tensor(MDb[:, tt * E:(tt + 1) * E], MKb[:, tt * E:(tt + 1) * E],
                                               -10.0, PRb[:, tt * E:(tt + 1) * E], ALU.mult, ALU.add)
                    for tt in range(NT):
                        ins = v.tensor_reduce(M2b[:, tt:tt + 1], MDb[:, tt * E:(tt + 1) * E],
                                              AX.X, ALU.max)
                    ins.then_inc(sem["s_rv"], 1)
                    v.wait_ge(sem["s_rs"], 2)
                    v.tensor_tensor(DDb[:, 0:NT], EM1b[:, 0:NT], EM2b[:, 0:NT], ALU.add)
                    for tt in range(NT):
                        v.tensor_scalar(MKb[:, tt * E:(tt + 1) * E], PRb[:, tt * E:(tt + 1) * E],
                                        M2b[:, tt:tt + 1], None, ALU.is_ge)
                    v.reciprocal(RDb[:, 0:NT], DDb[:, 0:NT])
                    v.tensor_tensor(CAb[:, :], EPb[:, :], MKb[:, :], ALU.mult)
                    v.drain()
                    for tt in range(NT):
                        v.tensor_scalar(CBb[:, tt * E:(tt + 1) * E], CAb[:, tt * E:(tt + 1) * E],
                                        RDb[:, tt:tt + 1], None, ALU.mult)
                    for tt in range(NT):
                        ins = v.tensor_tensor(COMB[:, tt * E:(tt + 1) * E], CBb[:, tt * E:(tt + 1) * E],
                                              ESC[:, :], ALU.mult)
                    ins.then_inc(sem["s_rv"], 1)

            PSA = ctx.enter_context(nc.psum_tensor("PSA", [P, 1024], F32))
            PSB = ctx.enter_context(nc.psum_tensor("PSB", [P, 1024], F32))
            AVP0 = ctx.enter_context(nc.psum_tensor("AVP0", [P, 1024], F32))
            AVP1 = ctx.enter_context(nc.psum_tensor("AVP1", [P, 1024], F32))
            with (nc.sbuf_tensor("QHp", [P, T], BF16) as QHp,
                  nc.sbuf_tensor("KHp", [P, T], BF16) as KHp,
                  nc.sbuf_tensor("EB0", [P, T], BF16) as EB0,
                  nc.sbuf_tensor("EB1", [P, T], BF16) as EB1,
                  nc.sbuf_tensor("E20", [P, T], BF16) as E20,
                  nc.sbuf_tensor("E21", [P, T], BF16) as E21):
                EB = [EB0, EB1]
                E2 = [E20, E21]
                AVP = [AVP0, AVP1]

                @blk.sync
                def _(s):
                    s.wait_ge(sem["s_pjev"], N_PJEV)
                    for h in range(H):
                        hb = (h % 2) * 64
                        if h >= 1:
                            s.wait_ge(sem["s_gather"], 256 * h)
                        if h >= 2:
                            s.wait_ge(sem["s_exp"], 32 * (h - 1))
                        for (buf, src) in ((QHp, QT), (KHp, KT)):
                            for pi in range(2):
                                for rp in range(4):
                                    s.dma_start(
                                        buf[hb:hb + 64, (2 * rp + pi) * 256:(2 * rp + pi + 1) * 256],
                                        src[pi * 64: pi * 64 + 64,
                                            rp * T + h * 256: rp * T + (h + 1) * 256],
                                    ).then_inc(sem["s_gather"], 16)

                @blk.tensor
                def _(t):
                    t.wait_ge(sem["s_rt"], NT)
                    t.wait_ge(sem["s_zz"], 1)
                    for h in range(H):
                        hb = (h % 2) * 64
                        t.wait_ge(sem["s_gather"], 256 * (h + 1))
                        if h >= 1:
                            t.wait_ge(sem["s_avev"], 2 * h)
                        for ha in range(2):
                            for bk in range(2):
                                t.matmul(AVP[ha][:, bk * 512:(bk + 1) * 512],
                                         ZZ[:, 0:128], ZZ[:, 0:512],
                                         start=True, stop=False)
                        def emit_av(j):
                            t.wait_ge(sem["s_scl"], 16 * h + j + 1)
                            rc, halfc = j // 2, j % 2
                            nmm = 0
                            for ha in range(2):
                                for r in range(8):
                                    nmm += 1
                                    ins = t.matmul(
                                        AVP[ha][:, r * P:(r + 1) * P],
                                        E2[j % 2][:, r * 256 + ha * P: r * 256 + ha * P + P],
                                        VP[:, (2 * r + halfc) * D + rc * P: (2 * r + halfc) * D + (rc + 1) * P],
                                        start=False, stop=(j == 15) and (r % 4 == 3))
                                    if nmm == 16:
                                        ins.then_inc(sem["s_av"], 1)
                        for i in range(16):
                            for half, ps in ((0, PSA), (1, PSB)):
                                w = 32 * h + 2 * (i - 1) + half + 1
                                if w > 0:
                                    t.wait_ge(sem["s_exp"], w)
                                for u in range(2):
                                    ins = t.matmul(ps[:, u * 512:(u + 1) * 512],
                                                   KHp[hb:hb + 64, i * P:(i + 1) * P],
                                                   QHp[hb:hb + 64,
                                                       half * 1024 + u * 512: half * 1024 + (u + 1) * 512],
                                                   start=True, stop=True)
                                ins.then_inc(sem["s_mm"], 1)
                            if i >= 1:
                                emit_av(i - 1)
                        emit_av(15)

                @blk.scalar
                def _(sc):
                    for h in range(H):
                        for i in range(16):
                            sc.wait_ge(sem["s_vmax"], 16 * h + i + 1)
                            w = 16 * h + i - 1
                            if w > 0:
                                sc.wait_ge(sem["s_zs"], w)
                            w2 = 16 * h + i - 1
                            if w2 > 0:
                                sc.wait_ge(sem["s_av"], w2)
                            sc.activation(EB[i % 2][:, 0:1024], PSA[:, 0:1024], AF.Exp,
                                          bias=NMmin8[:, 0:1], scale=0.125,
                                          accum_out=Z2p[i % 2][:, 0:1]).then_inc(sem["s_exp"], 1)
                            sc.activation(EB[i % 2][:, 1024:2048], PSB[:, 0:1024], AF.Exp,
                                          bias=NMmin8[:, 0:1], scale=0.125,
                                          accum_out=Z2p[i % 2][:, 1:2]).then_inc(sem["s_exp"], 1)

                @blk.vector
                def _(v):
                    for h in range(H):
                        for i in range(17):
                            j = i - 1
                            if j >= 0:
                                v.wait_ge(sem["s_exp"], 32 * h + 2 * j + 2)
                                v.reduce_sum(Zs[:, 0:1], Z2p[j % 2][:, 0:2],
                                             AX.X).then_inc(sem["s_zs"], 1)
                            if i < 16:
                                base = 32 * h + 2 * i
                                if base > 0:
                                    v.wait_ge(sem["s_exp"], base)
                                v.wait_ge(sem["s_mm"], base + 1)
                                v.tensor_reduce(NMmin[:, 0:1], PSA[:, 0:512], AX.X,
                                                ALU.max, negate=True)
                            if j >= 0:
                                if i >= 16:
                                    v.drain()
                                v.reciprocal(Rz[:, 0:1], Zs[:, 0:1])
                            if i < 16:
                                if j < 0:
                                    v.drain()
                                v.tensor_scalar(NMmin8[:, 0:1], NMmin[:, 0:1], 0.125, None,
                                                ALU.mult).then_inc(sem["s_vmax"], 1)
                            if j >= 0:
                                w = 16 * h + j - 1
                                if w > 0:
                                    v.wait_ge(sem["s_av"], w)
                                if i >= 16:
                                    v.drain()
                                v.tensor_scalar(E2[j % 2][:, 0:1024], EB[j % 2][:, 0:1024],
                                                Rz[:, 0:1], None, ALU.mult)
                                v.tensor_scalar(E2[j % 2][:, 1024:2048], EB[j % 2][:, 1024:2048],
                                                Rz[:, 0:1], None, ALU.mult).then_inc(sem["s_scl"], 1)
                        v.wait_ge(sem["s_av"], 16 * (h + 1))
                        for ha in range(2):
                            tt = 2 * h + ha
                            v.tensor_tensor(X[:, tt * D:(tt + 1) * D], X[:, tt * D:(tt + 1) * D],
                                            AVP[ha][:, 0:1024], ALU.add).then_inc(sem["s_avev"], 1)

                if DEBUG_STAGE == "attn":
                    @blk.sync
                    def _(s):
                        s.wait_ge(sem["s_avev"], 2 * H)
                        s.dma_start(out_d.rearrange("(tt p) d -> p tt d", p=P),
                                    X[:, :].rearrange("p (tt d) -> p tt d", d=D)).then_inc(sem["s_out"], 16)
                        s.dma_start(dbg["d_xnt"], XNT[:, :]).then_inc(sem["s_out"], 16)
                        s.dma_start(dbg["d_vp"], VP[:, :]).then_inc(sem["s_out"], 16)
                        s.dma_start(dbg["d_qt"], QT[:, :]).then_inc(sem["s_out"], 16)
                        s.dma_start(dbg["d_kt"], KT[:, :]).then_inc(sem["s_out"], 16)
                        s.dma_start(dbg["d_comb"], COMB[:, :]).then_inc(sem["s_out"], 16)
                        for nm_, tt_ in [("d_s1", S1), ("d_nm", NM), ("d_ss", SS), ("d_veps", VEPS),
                                         ("d_sd", SD), ("d_rstd", RSTD), ("d_nmrs", NMRS)]:
                            s.dma_start(dbg[nm_], tt_[:, :]).then_inc(sem["s_out"], 16)
                        s.wait_ge(sem["s_out"], 208)



        if DEBUG_STAGE == "attn":
            return nc
        CAP = 128
        p3s = contextlib.ExitStack()
        W1Q0 = p3s.enter_context(nc.sbuf_tensor("W1Q0", [P, ND * 1024], BF16))
        W2Q0 = p3s.enter_context(nc.sbuf_tensor("W2Q0", [P, FT_Q * 1024], BF16))
        W1Q1 = p3s.enter_context(nc.sbuf_tensor("W1Q1", [P, ND * 1024], BF16))
        W2Q1 = p3s.enter_context(nc.sbuf_tensor("W2Q1", [P, FT_Q * 1024], BF16))
        SHS0 = p3s.enter_context(nc.sbuf_tensor("SHS0", [P, FT_Q * 2 * CAP], BF16))
        SHS1 = p3s.enter_context(nc.sbuf_tensor("SHS1", [P, FT_Q * 2 * CAP], BF16))
        XGT = p3s.enter_context(nc.sbuf_tensor("XGT", [P, TC * ND * CAP], BF16))
        XROW0 = p3s.enter_context(nc.sbuf_tensor("XROW0", [P, D], BF16))
        XROW1 = p3s.enter_context(nc.sbuf_tensor("XROW1", [P, D], BF16))
        XGB0 = p3s.enter_context(nc.sbuf_tensor("XGB0", [P, D], BF16))
        XGB1 = p3s.enter_context(nc.sbuf_tensor("XGB1", [P, D], BF16))
        RSG0 = p3s.enter_context(nc.sbuf_tensor("RSG0", [P, 2], F32))
        RSG1 = p3s.enter_context(nc.sbuf_tensor("RSG1", [P, 2], F32))
        YSC0 = p3s.enter_context(nc.sbuf_tensor("YSC0", [P, D], BF16))
        YSC1 = p3s.enter_context(nc.sbuf_tensor("YSC1", [P, D], BF16))
        TKIA = p3s.enter_context(nc.sbuf_tensor("TKIA", [P, E * TC], I32))
        TKWA = p3s.enter_context(nc.sbuf_tensor("TKWA", [P, E * TC], F32))
        TKSA = p3s.enter_context(nc.sbuf_tensor("TKSA", [P, E * TC], I32))
        TKF = p3s.enter_context(nc.sbuf_tensor("TKF", [P, 3], F32))
        TKU = p3s.enter_context(nc.sbuf_tensor("TKU", [P, 3], F32))
        ONESC = p3s.enter_context(nc.sbuf_tensor("ONESC", [P, 1], F32))
        IOC2K = p3s.enter_context(nc.sbuf_tensor("IOC2K", [P, 1], F32))
        SELB = p3s.enter_context(nc.sbuf_tensor("SELB", [P, E], F32))
        SELT = p3s.enter_context(nc.sbuf_tensor("SELT", [P, 256], F32))
        SCN = p3s.enter_context(nc.sbuf_tensor("SCN", [P, 256], F32))
        SLX = p3s.enter_context(nc.sbuf_tensor("SLX", [P, 256], F32))
        SLT = p3s.enter_context(nc.sbuf_tensor("SLT", [P, TC * NT], F32))
        PTT0 = p3s.enter_context(nc.sbuf_tensor("PTT0", [P, 256], F32))
        PTT1 = p3s.enter_context(nc.sbuf_tensor("PTT1", [P, 256], F32))
        with p3s:
            W1Q = [W1Q0, W1Q1]
            W2Q = [W2Q0, W2Q1]
            SHS = [SHS0, SHS1]
            XROW = [XROW0, XROW1]
            XGB = [XGB0, XGB1]
            RSG = [RSG0, RSG1]
            YSC = [YSC0, YSC1]
            YDS = YSC
            PTTB = [PTT0, PTT1]
            YP = [PSA, PSB]
            HPq = AVP0
            PRE = AVP1

            def prepP(e, c):
                return e * TC + c + 1

            from concourse.bass import IndirectOffsetOnAxis

            @blk.gpsimd
            def _(g):
                g.wait_ge(sem["s_yz"], 16 * 2 * NT)

                def gathers(e, c):
                    pc = prepP(e, c)
                    sgx = sem["s_gx0"] if c % 2 == 0 else sem["s_gx1"]
                    npar = e * (TC // 2) + c // 2
                    if npar >= 1:
                        g.wait_ge(sgx, 32 * npar)
                    if pc >= 3:
                        g.wait_ge(sem["s_xgb"], pc - 2)
                    g.wait_ge(sem["s_tki"], pc)
                    g.indirect_dma_start(
                        XROW[c % 2][:, :], None,
                        x_d[0:T, :], IndirectOffsetOnAxis(TKIA[:, e * TC + c:e * TC + c + 1], 0),
                    ).then_inc(sgx, 16)
                    g.indirect_dma_start(
                        RSG[c % 2][:, :], None,
                        rsnm_d[0:T, :], IndirectOffsetOnAxis(TKIA[:, e * TC + c:e * TC + c + 1], 0),
                    ).then_inc(sgx, 16)

                def qload_slice(idx, j):
                    e, q = divmod(idx, NQ)
                    par = idx % 2
                    swq = sem["s_wq0"] if par == 0 else sem["s_wq1"]
                    if j == 0:
                        if idx == 0:
                            g.wait_ge(sem["s_gather"], 256 * H)
                        elif idx == 1:
                            g.wait_ge(sem["s_av"], 16 * H)
                        else:
                            g.wait_ge(sem["s_m2"], max(0, (idx - 1) * TC))
                        if idx >= 2:
                            g.wait_ge(swq, 64 * (idx // 2))
                    if j < 2:
                        src1 = w1t_d[e].rearrange("(dt p) f -> p dt f", p=P)[:, j * 4:(j + 1) * 4,
                                                                            q * 1024:(q + 1) * 1024]
                        dst1 = W1Q[par][:, :].rearrange("p (dt f) -> p dt f", dt=ND)[:, j * 4:(j + 1) * 4, :]
                        g.dma_start(dst1, src1).then_inc(swq, 16)
                    else:
                        jj = j - 2
                        src2 = w2t_d[e, q * 1024:(q + 1) * 1024, :].rearrange(
                            "(ft p) d -> p ft d", p=P)[:, jj * 4:(jj + 1) * 4, :]
                        dst2 = W2Q[par][:, :].rearrange("p (ft d) -> p ft d", ft=FT_Q)[:, jj * 4:(jj + 1) * 4, :]
                        g.dma_start(dst2, src2).then_inc(swq, 16)

                def qload(idx):
                    for j in range(4):
                        qload_slice(idx, j)

                def scatter_one(yc):
                    e2 = yc // (NQ * TC)
                    c2 = yc % TC
                    ssc = sem["s_sc0"] if yc % 2 == 0 else sem["s_sc1"]
                    if yc // 2 >= 1:
                        g.wait_ge(ssc, 16 * (yc // 2))
                    g.wait_ge(sem["s_ysc"], yc + 1)
                    g.indirect_dma_start(
                        yd_d[0:T + P, :], IndirectOffsetOnAxis(TKSA[:, e2 * TC + c2:e2 * TC + c2 + 1], 0),
                        YSC[yc % 2][:, :], None,
                        compute_op=ALU.add).then_inc(ssc, 16)

                for c in range(TC):
                    gathers(0, c)
                qload(0)
                qload(1)
                for idx in range(NIDX):
                    e, q = divmod(idx, NQ)
                    for c in range(TC):
                        yc = idx * TC + c
                        if yc >= 2:
                            scatter_one(yc - 2)
                        if idx + 1 < NIDX and 2 <= c <= 5:
                            qload_slice(idx + 1, c - 2)
                        if q == 3 and e < E - 1:
                            gathers(e + 1, c)
                scatter_one(NIDX * TC - 2)
                scatter_one(NIDX * TC - 1)

            @blk.tensor
            def _(t):
                t.wait_ge(sem["s_avev"], 2 * H)
                t.wait_ge(sem["s_rv"], 3)
                for c in range(TC):
                    t.wait_ge(sem["s_p3v"], 3 * c + 1)
                    t.transpose(PRE[0:E, 0:P], SELB[:, 0:E], IDM[:, :]).then_inc(sem["s_p3t"], 1)
                    t.wait_ge(sem["s_p3v"], 3 * c + 2)
                    t.transpose(PRE[0:E, 0:P], SELB[:, 0:E], IDM[:, :]).then_inc(sem["s_p3t"], 1)
                    t.wait_ge(sem["s_p3v"], 3 * c + 3)
                    for tau in range(2):
                        ins = t.transpose(PRE[0:P, 512 + tau * E: 512 + (tau + 1) * E],
                                          SLX[0:E, tau * P:(tau + 1) * P], IDM[0:E, 0:E])
                    ins.then_inc(sem["s_p3t"], 1)
                for k in range(E * TC):
                    e3, c3 = divmod(k, TC)
                    t.wait_ge(sem["s_ptv"], k + 1)
                    if k >= 2:
                        t.wait_ge(sem["s_tki"], k - 1)
                    off = (k % 2) * 512
                    for tau in range(2):
                        t.matmul(PRE[0:P, off:off + 1], PTTB[k % 2][:, tau * P:(tau + 1) * P],
                                 GIDC[:, 2 * c3 + tau: 2 * c3 + tau + 1],
                                 start=(tau == 0), stop=(tau == 1))
                    for tau in range(2):
                        t.matmul(PRE[0:P, off + 1:off + 2], PTTB[k % 2][:, tau * P:(tau + 1) * P],
                                 COMB[:, (2 * c3 + tau) * E + e3: (2 * c3 + tau) * E + e3 + 1],
                                 start=(tau == 0), stop=(tau == 1))
                    for tau in range(2):
                        ins = t.matmul(PRE[0:P, off + 2:off + 3], PTTB[k % 2][:, tau * P:(tau + 1) * P],
                                       ONESC[:, 0:1], start=(tau == 0), stop=(tau == 1))
                    ins.then_inc(sem["s_ptt"], 1)
                for e in range(E):
                    for q in range(NQ):
                        idx = e * NQ + q
                        par = idx % 2
                        t.wait_ge(sem["s_wq0"] if par == 0 else sem["s_wq1"],
                                  64 * (idx // 2 + 1))
                        for cp in range(TC // 2):
                            gp = (idx * (TC // 2) + cp)
                            t.wait_ge(sem["s_xt0"], 128 * (e * (TC // 2) + cp + 1))
                            t.wait_ge(sem["s_xt1"], 128 * (e * (TC // 2) + cp + 1))
                            for ft in range(FT_Q):
                                gg = gp * FT_Q + ft
                                if gg >= 2:
                                    t.wait_ge(sem["s_silu"], gg - 1)
                                hp = HPq[:, (gg % 2) * 512: (gg % 2) * 512 + 2 * CAP]
                                for dt in range(ND):
                                    ins = t.matmul(hp,
                                                   W1Q[par][:, dt * 1024 + ft * P: dt * 1024 + (ft + 1) * P],
                                                   XGT[:, dt * (TC * CAP) + 2 * cp * CAP: dt * (TC * CAP) + (2 * cp + 2) * CAP],
                                                   start=(dt == 0), stop=(dt == ND - 1))
                                ins.then_inc(sem["s_m1"], 1)
                            for sub in range(2):
                                c = 2 * cp + sub
                                yc = idx * TC + c
                                if yc >= 2:
                                    t.wait_ge(sem["s_ysc"], yc - 1)
                                for ft in range(FT_Q):
                                    gg = gp * FT_Q + ft
                                    t.wait_ge(sem["s_silu"], gg + 1)
                                    for nh in range(2):
                                        ins = t.matmul(YP[yc % 2][:, nh * 512:(nh + 1) * 512],
                                                       SHS[cp % 2][:, ft * 2 * CAP + sub * CAP: ft * 2 * CAP + (sub + 1) * CAP],
                                                       W2Q[par][:, ft * 1024 + nh * 512: ft * 1024 + (nh + 1) * 512],
                                                       start=(ft == 0), stop=(ft == FT_Q - 1))
                                ins.then_inc(sem["s_m2"], 1)

            @blk.scalar
            def _(sc):
                for idx in range(NIDX):
                    for cp in range(TC // 2):
                        gp = idx * (TC // 2) + cp
                        for ft in range(FT_Q):
                            gg = gp * FT_Q + ft
                            sc.wait_ge(sem["s_m1"], gg + 1)
                            if gp >= 2 and ft == 0:
                                sc.wait_ge(sem["s_m2"], 2 * gp - 2)
                            sc.activation(SHS[cp % 2][:, ft * 2 * CAP:(ft + 1) * 2 * CAP],
                                          HPq[:, (gg % 2) * 512: (gg % 2) * 512 + 2 * CAP],
                                          AF.Silu).then_inc(sem["s_silu"], 1)

            @blk.vector
            def _(v):
                for c in range(TC):
                    v.tensor_scalar(SELB[:, :], COMB[:, 2 * c * E:(2 * c + 1) * E],
                                    0.0, None, ALU.not_equal).then_inc(sem["s_p3v"], 1)
                    v.wait_ge(sem["s_p3t"], 3 * c + 1)
                    v.tensor_copy(SELT[0:E, 0:P], PRE[0:E, 0:P])
                    v.tensor_scalar(SELB[:, :], COMB[:, (2 * c + 1) * E:(2 * c + 2) * E],
                                    0.0, None, ALU.not_equal).then_inc(sem["s_p3v"], 1)
                    v.wait_ge(sem["s_p3t"], 3 * c + 2)
                    v.tensor_copy(SELT[0:E, P:256], PRE[0:E, 0:P])
                    v.drain()
                    v.tensor_tensor_scan(SCN[0:E, 0:256], SELT[0:E, 0:256], SELT[0:E, 0:256],
                                         0.0, ALU.add, ALU.bypass)
                    v.drain()
                    v.tensor_tensor(SCN[0:E, 0:256], SCN[0:E, 0:256], SELT[0:E, 0:256],
                                    ALU.subtract)
                    v.scalar_tensor_tensor(SLX[0:E, 0:256], SELT[0:E, 0:256], -999.0,
                                           SCN[0:E, 0:256], ALU.mult, ALU.add)
                    v.drain()
                    v.tensor_scalar(SLX[0:E, 0:256], SLX[0:E, 0:256], 999.0, None,
                                    ALU.add).then_inc(sem["s_p3v"], 1)
                    v.wait_ge(sem["s_p3t"], 3 * c + 3)
                    v.tensor_copy(SLT[:, c * NT: c * NT + E], PRE[0:P, 512:512 + E])
                    v.tensor_copy(SLT[:, c * NT + E: c * NT + 2 * E],
                                  PRE[0:P, 512 + E:512 + 2 * E])
                def prep_chunk_xgb(e, c):
                    sgx = sem["s_gx0"] if c % 2 == 0 else sem["s_gx1"]
                    v.wait_ge(sgx, 32 * (e * (TC // 2) + c // 2 + 1))
                    sxt = sem["s_xt0"] if c % 2 == 0 else sem["s_xt1"]
                    if e * (TC // 2) + c // 2 >= 1:
                        v.wait_ge(sxt, 128 * (e * (TC // 2) + c // 2))
                    v.tensor_scalar(XGB[c % 2][:, :], XROW[c % 2][:, :],
                                    RSG[c % 2][:, 0:1], RSG[c % 2][:, 1:2],
                                    ALU.mult, ALU.add).then_inc(sem["s_xgb"], 1)

                def extract(k):
                    v.wait_ge(sem["s_ptt"], k + 1)
                    off = (k % 2) * 512
                    v.tensor_copy(TKF[:, 0:3], PRE[0:P, off:off + 3])
                    v.drain()
                    v.tensor_scalar(TKU[:, 0:1], TKF[:, 2:3], -1.0, 1.0, ALU.mult, ALU.add)
                    v.tensor_copy(TKIA[:, k:k + 1], TKF[:, 0:1])
                    v.scalar_tensor_tensor(TKU[:, 1:2], TKU[:, 0:1], 1.0, IOC2K[:, 0:1],
                                           ALU.mult, ALU.mult)
                    v.tensor_copy(TKWA[:, k:k + 1], TKF[:, 1:2])
                    v.tensor_tensor(TKU[:, 2:3], TKU[:, 1:2], TKF[:, 0:1], ALU.add)
                    v.tensor_scalar(TKU[:, 0:1], TKF[:, 2:3], -1.0, 1.0, ALU.mult, ALU.add)
                    v.tensor_copy(TKSA[:, k:k + 1], TKU[:, 2:3]).then_inc(sem["s_tki"], 1)

                v.memset(ONESC[:, 0:1], 1.0)
                v.tensor_scalar(IOC2K[:, 0:1], IOTC[:, 0:1], 1.0, float(T), ALU.mult, ALU.add)
                for k in range(E * TC):
                    e3, c3 = divmod(k, TC)
                    if k >= 2:
                        v.wait_ge(sem["s_ptt"], k - 1)
                    for tau in range(2):
                        v.tensor_scalar(PTTB[k % 2][:, tau * P:(tau + 1) * P], IOTR[:, :],
                                        SLT[:, c3 * NT + tau * E + e3: c3 * NT + tau * E + e3 + 1],
                                        None, ALU.is_equal)
                    v.drain()
                    v.nop()
                    v.sem_inc(sem["s_ptv"], 1)
                    if k >= 1:
                        extract(k - 1)
                    if 9 <= k <= 16:
                        prep_chunk_xgb(0, k - 9)
                extract(E * TC - 1)

                prep_chunk = prep_chunk_xgb
                for e in range(E):
                    for q in range(NQ):
                        idx = e * NQ + q
                        for c in range(TC):
                            yc = idx * TC + c
                            if e < E - 1 and q == 3:
                                prep_chunk(e + 1, c)
                            v.wait_ge(sem["s_m2"], yc + 1)
                            ssc = sem["s_sc0"] if yc % 2 == 0 else sem["s_sc1"]
                            if yc // 2 >= 1:
                                v.wait_ge(ssc, 16 * (yc // 2))
                            v.tensor_scalar(YSC[yc % 2][:, :], YP[yc % 2][:, 0:D],
                                            TKWA[:, e * TC + c:e * TC + c + 1], None,
                                            ALU.mult).then_inc(sem["s_ysc"], 1)
                for tt in range(NT):
                    v.wait_ge(sem["s_yl"], 16 * (tt + 1))
                    v.tensor_tensor(X[:, tt * D:(tt + 1) * D], X[:, tt * D:(tt + 1) * D],
                                    YDS[tt % 2][:, :], ALU.add).then_inc(sem["s_yev"], 1)

            @blk.sync
            def _(s):
                for e in range(E):
                    for c in range(TC):
                        pc = prepP(e, c)
                        sxt = sem["s_xt0"] if c % 2 == 0 else sem["s_xt1"]
                        npar = e * (TC // 2) + c // 2
                        if npar >= 1:
                            s.wait_ge(sxt, 128 * npar)
                        if e >= 1:
                            s.wait_ge(sem["s_m1"], 8 * ((4 * e - 1) * (TC // 2) + c // 2 + 1))
                        s.wait_ge(sem["s_xgb"], pc)
                        for dt in range(ND):
                            s.dma_start(XGT[:, dt * (TC * CAP) + c * CAP: dt * (TC * CAP) + (c + 1) * CAP],
                                        XGB[c % 2][:, dt * P:(dt + 1) * P],
                                        transpose=True).then_inc(sxt, 16)
                s.wait_ge(sem["s_sc0"], 16 * 128)
                s.wait_ge(sem["s_sc1"], 16 * 128)
                for tt in range(NT):
                    if tt >= 2:
                        s.wait_ge(sem["s_yev"], tt - 1)
                    s.dma_start(YDS[tt % 2][:, :],
                                yd_d[tt * P:(tt + 1) * P, :]).then_inc(sem["s_yl"], 16)
                for tt in range(NT):
                    s.wait_ge(sem["s_yev"], tt + 1)
                    s.dma_start(out_d[tt * P:(tt + 1) * P, :],
                                X[:, tt * D:(tt + 1) * D]).then_inc(sem["s_out"], 16)
                s.wait_ge(sem["s_out"], 16 * NT)

    return nc


_NC_CACHE = {}


def kernel(L0_attn_x, L0_attn_Mq, L0_attn_Mk, L0_attn_Mv,
           L0_ffn_router_w, L0_ffn_w1, L0_ffn_w2, L0_ffn_expert_scale):
    x = np.ascontiguousarray(np.asarray(L0_attn_x, dtype=np.float32))
    mq = np.ascontiguousarray(np.asarray(L0_attn_Mq, dtype=np.float32))
    mk = np.ascontiguousarray(np.asarray(L0_attn_Mk, dtype=np.float32))
    mv = np.ascontiguousarray(np.asarray(L0_attn_Mv, dtype=np.float32))
    rwt = np.ascontiguousarray(np.asarray(L0_ffn_router_w, dtype=np.float32).T)
    w1t = np.ascontiguousarray(np.asarray(L0_ffn_w1, dtype=np.float32).transpose(0, 2, 1))
    w2t = np.ascontiguousarray(np.asarray(L0_ffn_w2, dtype=np.float32).transpose(0, 2, 1))
    esc = np.ascontiguousarray(
        np.broadcast_to(np.asarray(L0_ffn_expert_scale, dtype=np.float32)[None, :], (P, E)))

    if "nc" not in _NC_CACHE:
        _NC_CACHE["nc"] = build_nc()
    nc = _NC_CACHE["nc"]

    in_maps = [{"x": np.ascontiguousarray(x[b]), "mq": mq, "mk": mk, "mv": mv,
                "rwt": rwt, "esc": esc, "w1t": w1t, "w2t": w2t} for b in range(8)]
    res = run_bass_kernel_spmd(nc, in_maps, core_ids=list(range(8)))
    return np.stack([np.asarray(res.results[b]["out"], dtype=np.float32) for b in range(8)], axis=0)


# version 20
# speedup vs baseline: 1.3498x; 1.0016x over previous
import contextlib
import numpy as np
import concourse.bass as bass
import concourse.mybir as mybir
from concourse.bass_utils import run_bass_kernel_spmd

F32 = mybir.dt.float32
BF16 = mybir.dt.bfloat16
AF = mybir.ActivationFunctionType
I32 = mybir.dt.int32
ALU = mybir.AluOpType
AX = mybir.AxisListType

P = 128
T = 2048
D = 1024
NT = 16
ND = 8
H = 8
E = 8
NQ = 4
FT_Q = 8
TC = 8
NIDX = E * NQ

WG_MQ, WG_MK, WG_MV, WG_RWT = 16, 32, 48, 64
def wg_q(idx):
    return 64 + 32 * idx + 32
N_PJEV = 64
DEBUG_STAGE = None


def build_nc():
    nc = bass.Bass("TRN2", target_bir_lowering=False, detect_race_conditions=False)

    x_d = nc.dram_tensor("x", [T, D], F32, kind="ExternalInput").ap()
    mq_d = nc.dram_tensor("mq", [D, 512], F32, kind="ExternalInput").ap()
    mk_d = nc.dram_tensor("mk", [D, 512], F32, kind="ExternalInput").ap()
    mv_d = nc.dram_tensor("mv", [D, D], F32, kind="ExternalInput").ap()
    rwt_d = nc.dram_tensor("rwt", [D, E], F32, kind="ExternalInput").ap()
    esc_d = nc.dram_tensor("esc", [P, E], F32, kind="ExternalInput").ap()
    w1t_d = nc.dram_tensor("w1t", [E, D, 4096], F32, kind="ExternalInput").ap()
    w2t_d = nc.dram_tensor("w2t", [E, 4096, D], F32, kind="ExternalInput").ap()
    out_d = nc.dram_tensor("out", [T, D], F32, kind="ExternalOutput").ap()
    yd_d = nc.dram_tensor("yd", [T + P, D], BF16, kind="Internal").ap()
    rsnm_d = nc.dram_tensor("rsnm", [T, 2], F32, kind="Internal").ap()
    if DEBUG_STAGE == "attn":
        dbg = {nm: nc.dram_tensor(nm, shp, dt, kind="ExternalOutput").ap()
               for nm, shp, dt in [("d_xnt", [P, ND * T], BF16), ("d_vp", [P, NT * D], BF16),
                                   ("d_qt", [P, 4 * T], BF16), ("d_kt", [P, 4 * T], BF16),
                                   ("d_comb", [P, NT * E], F32),
                                   ("d_s1", [P, NT], F32), ("d_nm", [P, NT], F32),
                                   ("d_ss", [P, NT], F32), ("d_veps", [P, NT], F32),
                                   ("d_sd", [P, NT], F32), ("d_rstd", [P, NT], F32),
                                   ("d_nmrs", [P, NT], F32)]}

    ctx = contextlib.ExitStack()
    with ctx:
        sb = lambda name, shape, dt: ctx.enter_context(nc.sbuf_tensor(name, shape, dt))
        X = sb("X", [P, NT * D], F32)
        XNT = sb("XNT", [P, ND * T], BF16)
        COMB = sb("COMB", [P, NT * E], F32)
        ESC = sb("ESC", [P, E], F32)
        RWTS = sb("RWTS", [P, ND * E], BF16)
        S1 = sb("S1", [P, NT], F32)
        NM = sb("NM", [P, NT], F32)
        SS = sb("SS", [P, NT], F32)
        VEPS = sb("VEPS", [P, NT], F32)
        SD = sb("SD", [P, NT], F32)
        RSTD = sb("RSTD", [P, NT], F32)
        NMRS = sb("NMRS", [P, NT], F32)
        MSQ = sb("MSQ", [P, NT], F32)
        V2 = sb("V2", [P, NT], F32)
        NM2 = sb("NM2", [P, 2], F32)
        NMmin = sb("NMmin", [P, 1], F32)
        NMmin8 = sb("NMmin8", [P, 1], F32)
        Zs = sb("Zs", [P, 1], F32)
        Rz = sb("Rz", [P, 1], F32)
        NMX = [sb(f"NMX{i}", [P, 1], F32) for i in range(2)]
        PBUF = [sb(f"PBUF{i}", [P, E], F32) for i in range(2)]
        SZ = [sb(f"SZ{i}", [P, 1], F32) for i in range(2)]
        RZr = [sb(f"RZr{i}", [P, 1], F32) for i in range(2)]
        PRB = [sb(f"PRB{i}", [P, E], F32) for i in range(2)]
        M12 = [sb(f"M12{i}", [P, 2], F32) for i in range(2)]
        MSK = [sb(f"MSK{i}", [P, E], F32) for i in range(2)]
        MKD = [sb(f"MKD{i}", [P, E], F32) for i in range(2)]
        EM2 = [sb(f"EM2{i}", [P, 2], F32) for i in range(2)]
        EPB = [sb(f"EPB{i}", [P, E], F32) for i in range(2)]
        DD = [sb(f"DD{i}", [P, 1], F32) for i in range(2)]
        RD = [sb(f"RD{i}", [P, 1], F32) for i in range(2)]
        CWA = [sb(f"CWA{i}", [P, E], F32) for i in range(2)]
        Z2p = [sb(f"Z2p{i}", [P, 2], F32) for i in range(2)]
        ZZ = sb("ZZ", [P, 512], BF16)
        IOTC = sb("IOTC", [P, 1], F32)
        IOTR = sb("IOTR", [P, P], F32)
        GIDC = sb("GIDC", [P, NT], F32)
        IDM = sb("IDM", [P, P], F32)
        RSNMS = sb("RSNMS", [P, 2 * NT], F32)
        CWB = [sb(f"CWB{i}", [P, E], F32) for i in range(2)]

        sem = {}
        for name in ["s_x", "s_hw", "s_wg", "s_xns", "s_xnt", "s_ln1", "s_ln2",
                     "s_ln3", "s_ln4", "s_pj", "s_pjev", "s_rt", "s_rv", "s_rs",
                     "s_gather", "s_schead", "s_mm", "s_vmax", "s_exp", "s_scl",
                     "s_av", "s_avev", "s_qdone", "s_m1", "s_silu", "s_m2",
                     "s_yev", "s_out", "s_zz", "s_x0", "s_x1", "s_xnt0", "s_xnt1",
                     "s_wq0", "s_wq1", "s_zs", "s_io", "s_rnv", "s_rn", "s_yz",
                     "s_p3v", "s_p3t", "s_ptv", "s_ptt", "s_tki", "s_gx0", "s_gx1",
                     "s_xgb", "s_xt0", "s_xt1", "s_ysc", "s_sc0", "s_sc1", "s_yl"]:
            sem[name] = ctx.enter_context(nc.semaphore(name))

        blk = ctx.enter_context(nc.Block())

        with (nc.sbuf_tensor("QT", [P, 4 * T], BF16) as QT,
              nc.sbuf_tensor("KT", [P, 4 * T], BF16) as KT,
              nc.sbuf_tensor("VP", [P, NT * D], BF16) as VP):

            p1 = contextlib.ExitStack()
            MQS = p1.enter_context(nc.sbuf_tensor("MQS", [P, ND * 512], BF16))
            MKS = p1.enter_context(nc.sbuf_tensor("MKS", [P, ND * 512], BF16))
            MVS = p1.enter_context(nc.sbuf_tensor("MVS", [P, ND * D], BF16))
            DUMP = p1.enter_context(nc.sbuf_tensor("DUMP", [P, D], BF16))
            XNS0 = p1.enter_context(nc.sbuf_tensor("XNS0", [P, D], BF16))
            XNS1 = p1.enter_context(nc.sbuf_tensor("XNS1", [P, D], BF16))
            LOG = p1.enter_context(nc.sbuf_tensor("LOG", [P, NT * E], F32))
            NMXb = p1.enter_context(nc.sbuf_tensor("NMXb", [P, NT], F32))
            PB = p1.enter_context(nc.sbuf_tensor("PB", [P, NT * E], F32))
            SZb = p1.enter_context(nc.sbuf_tensor("SZb", [P, NT], F32))
            RZb = p1.enter_context(nc.sbuf_tensor("RZb", [P, NT], F32))
            PRb = p1.enter_context(nc.sbuf_tensor("PRb", [P, NT * E], F32))
            M1b = p1.enter_context(nc.sbuf_tensor("M1b", [P, NT], F32))
            M2b = p1.enter_context(nc.sbuf_tensor("M2b", [P, NT], F32))
            MKb = p1.enter_context(nc.sbuf_tensor("MKb", [P, NT * E], F32))
            MDb = p1.enter_context(nc.sbuf_tensor("MDb", [P, NT * E], F32))
            EM1b = p1.enter_context(nc.sbuf_tensor("EM1b", [P, NT], F32))
            EM2b = p1.enter_context(nc.sbuf_tensor("EM2b", [P, NT], F32))
            EPb = p1.enter_context(nc.sbuf_tensor("EPb", [P, NT * E], F32))
            DDb = p1.enter_context(nc.sbuf_tensor("DDb", [P, NT], F32))
            RDb = p1.enter_context(nc.sbuf_tensor("RDb", [P, NT], F32))
            CAb = p1.enter_context(nc.sbuf_tensor("CAb", [P, NT * E], F32))
            CBb = p1.enter_context(nc.sbuf_tensor("CBb", [P, NT * E], F32))
            PJ0 = p1.enter_context(nc.psum_tensor("PJ0", [P, 512], F32))
            PJ1 = p1.enter_context(nc.psum_tensor("PJ1", [P, 512], F32))
            with p1:
                XNS = [XNS0, XNS1]
                PJ = [PJ0, PJ1]

                @blk.gpsimd
                def _(g):
                    g.iota(IOTC[:, 0:1], [[0, 1]], base=0, channel_multiplier=1,
                           allow_small_or_imprecise_dtypes=True)
                    g.iota(IOTR[:, :], [[1, P]], base=0, channel_multiplier=0,
                           allow_small_or_imprecise_dtypes=True)
                    g.iota(GIDC[:, :], [[P, NT]], base=0, channel_multiplier=1,
                           allow_small_or_imprecise_dtypes=True).then_inc(sem["s_io"], 1)
                    def xload(tt):
                        sx = sem["s_x0"] if tt % 2 == 0 else sem["s_x1"]
                        if tt >= 2:
                            g.wait_ge(sx, 16 * (tt // 2))
                        g.dma_start(X[:, tt * D:(tt + 1) * D],
                                    x_d[tt * P:(tt + 1) * P, :]).then_inc(sx, 16)
                    for tt in range(4):
                        xload(tt)
                    g.dma_start(MVS[:, :].rearrange("p (dt j) -> p dt j", dt=ND), mv_d.rearrange("(dt p) j -> p dt j", p=P)).then_inc(sem["s_wg"], 16)
                    for tt in range(4, 8):
                        xload(tt)
                    g.wait_ge(sem["s_wg"], 16)
                    g.dma_start(MQS[:, :].rearrange("p (dt j) -> p dt j", dt=ND), mq_d.rearrange("(dt p) j -> p dt j", p=P)).then_inc(sem["s_wg"], 16)
                    g.wait_ge(sem["s_wg"], 32)
                    g.dma_start(MKS[:, :].rearrange("p (dt j) -> p dt j", dt=ND), mk_d.rearrange("(dt p) j -> p dt j", p=P)).then_inc(sem["s_wg"], 16)
                    for tt in range(8, NT):
                        xload(tt)
                    g.wait_ge(sem["s_wg"], 48)
                    g.dma_start(RWTS[:, :].rearrange("p (dt j) -> p dt j", dt=ND), rwt_d.rearrange("(dt p) j -> p dt j", p=P)).then_inc(sem["s_wg"], 16)
                    g.wait_ge(sem["s_zz"], 1)
                    for tt in range(NT):
                        for hh in range(2):
                            g.dma_start(yd_d[tt * P:(tt + 1) * P, hh * 512:(hh + 1) * 512],
                                        ZZ[:, :]).then_inc(sem["s_yz"], 16)

                @blk.sync
                def _(s):
                    s.dma_start(ESC[:, :], esc_d).then_inc(sem["s_hw"], 16)
                    for tt in range(NT):
                        s.wait_ge(sem["s_xns"], tt + 1)
                        sxn = sem["s_xnt0"] if tt % 2 == 0 else sem["s_xnt1"]
                        if tt >= 2:
                            s.wait_ge(sxn, 128 * (tt // 2))
                        for dt in range(ND):
                            s.dma_start(XNT[:, dt * T + tt * P: dt * T + (tt + 1) * P],
                                        XNS[tt % 2][:, dt * P:(dt + 1) * P],
                                        transpose=True).then_inc(sxn, 16)
                    s.wait_ge(sem["s_rnv"], 1)
                    s.dma_start(rsnm_d.rearrange("(tt p) two -> p tt two", p=P),
                                RSNMS[:, :].rearrange("p (tt two) -> p tt two", two=2)
                                ).then_inc(sem["s_rn"], 16)

                @blk.tensor
                def _(t):
                    t.wait_ge(sem["s_wg"], 16)
                    gidx = 0
                    for tt in range(NT):
                        t.wait_ge(sem["s_xnt0"], 128 * (tt // 2 + 1))
                        if tt >= 1:
                            t.wait_ge(sem["s_xnt1"], 128 * ((tt + 1) // 2))
                        for jc in range(2):
                            ps = PJ[gidx % 2]
                            if gidx >= 2:
                                t.wait_ge(sem["s_pjev"], gidx - 1)
                            for dt in range(ND):
                                ins = t.matmul(ps[:, 0:512],
                                               XNT[:, dt * T + tt * P: dt * T + (tt + 1) * P],
                                               MVS[:, dt * D + jc * 512: dt * D + (jc + 1) * 512],
                                               start=(dt == 0), stop=(dt == ND - 1))
                            ins.then_inc(sem["s_pj"], 1)
                            gidx += 1
                    t.wait_ge(sem["s_xnt0"], 128 * (NT // 2))
                    t.wait_ge(sem["s_xnt1"], 128 * (NT // 2))
                    t.wait_ge(sem["s_wg"], 48)
                    for wsb in (MQS, MKS):
                        for rp in range(4):
                            for nb in range(4):
                                ps = PJ[gidx % 2]
                                t.wait_ge(sem["s_pjev"], gidx - 1)
                                for dt in range(ND):
                                    ins = t.matmul(ps[:, 0:512],
                                                   wsb[:, dt * 512 + rp * P: dt * 512 + (rp + 1) * P],
                                                   XNT[:, dt * T + nb * 512: dt * T + (nb + 1) * 512],
                                                   start=(dt == 0), stop=(dt == ND - 1))
                                ins.then_inc(sem["s_pj"], 1)
                                gidx += 1
                    t.wait_ge(sem["s_wg"], WG_RWT)
                    t.wait_ge(sem["s_pjev"], N_PJEV)
                    for tt in range(NT):
                        ps = PJ[tt % 2]
                        if tt >= 2:
                            t.wait_ge(sem["s_rt"], tt - 1)
                        for dt in range(ND):
                            ins = t.matmul(ps[:, 0:E],
                                           XNT[:, dt * T + tt * P: dt * T + (tt + 1) * P],
                                           RWTS[:, dt * E:(dt + 1) * E],
                                           start=(dt == 0), stop=(dt == ND - 1))
                        ins.then_inc(sem["s_pj"], 1)

                @blk.scalar
                def _(sc):
                    for tt in range(NT):
                        sc.wait_ge(sem["s_x0"] if tt % 2 == 0 else sem["s_x1"], 16 * (tt // 2 + 1))
                        sc.activation(DUMP[:, :], X[:, tt * D:(tt + 1) * D], AF.Square,
                                      accum_out=SS[:, tt:tt + 1]).then_inc(sem["s_ln2"], 1)
                    sc.wait_ge(sem["s_ln3"], 1)
                    sc.activation(SD[:, 0:NT], VEPS[:, 0:NT], AF.Sqrt).then_inc(sem["s_ln4"], 1)
                    gidx = 0
                    for tt in range(NT):
                        for jc in range(2):
                            sc.wait_ge(sem["s_pj"], gidx + 1)
                            sc.copy(VP[:, tt * D + jc * 512: tt * D + (jc + 1) * 512],
                                    PJ[gidx % 2][:, 0:512]).then_inc(sem["s_pjev"], 1)
                            gidx += 1
                    for dst in (QT, KT):
                        for rp in range(4):
                            for nb in range(4):
                                sc.wait_ge(sem["s_pj"], gidx + 1)
                                sc.copy(dst[:, rp * T + nb * 512: rp * T + (nb + 1) * 512],
                                        PJ[gidx % 2][:, 0:512]).then_inc(sem["s_pjev"], 1)
                                gidx += 1
                    sc.wait_ge(sem["s_rv"], 1)
                    for tt in range(NT):
                        ins = sc.activation(PB[:, tt * E:(tt + 1) * E], LOG[:, tt * E:(tt + 1) * E],
                                            AF.Exp, bias=NMXb[:, tt:tt + 1],
                                            accum_out=SZb[:, tt:tt + 1])
                    ins.then_inc(sem["s_rs"], 1)
                    sc.wait_ge(sem["s_rv"], 2)
                    sc.activation(EM1b[:, 0:NT], M1b[:, 0:NT], AF.Exp)
                    sc.activation(EM2b[:, 0:NT], M2b[:, 0:NT], AF.Exp)
                    sc.activation(EPb[:, :], PRb[:, :], AF.Exp).then_inc(sem["s_rs"], 1)

                @blk.vector
                def _(v):
                    v.memset(ZZ[:, :], 0.0).then_inc(sem["s_zz"], 1)
                    for tt in range(NT):
                        v.wait_ge(sem["s_x0"] if tt % 2 == 0 else sem["s_x1"], 16 * (tt // 2 + 1))
                        v.reduce_sum(S1[:, tt:tt + 1], X[:, tt * D:(tt + 1) * D], AX.X)
                    for tt in range(NT):
                        v.tensor_scalar(NM[:, tt:tt + 1], S1[:, tt:tt + 1], -1.0 / D, None,
                                        ALU.mult)
                    v.wait_ge(sem["s_ln2"], NT)
                    v.tensor_scalar(V2[:, 0:NT], SS[:, 0:NT], 1.0 / D, 1e-5,
                                    ALU.mult, ALU.add)
                    v.tensor_tensor(MSQ[:, 0:NT], NM[:, 0:NT], NM[:, 0:NT], ALU.mult)
                    v.drain()
                    v.tensor_tensor(VEPS[:, 0:NT], V2[:, 0:NT], MSQ[:, 0:NT],
                                    ALU.subtract).then_inc(sem["s_ln3"], 1)
                    v.wait_ge(sem["s_ln4"], 1)
                    v.reciprocal(RSTD[:, 0:NT], SD[:, 0:NT])
                    v.wait_ge(sem["s_io"], 1)
                    v.tensor_scalar(IDM[:, :], IOTR[:, :], IOTC[:, 0:1], None, ALU.is_equal)
                    v.tensor_tensor(NMRS[:, 0:NT], NM[:, 0:NT], RSTD[:, 0:NT], ALU.mult)
                    v.drain()
                    rsv = RSNMS[:, :].rearrange("p (tt two) -> p tt two", two=2)
                    v.tensor_copy(rsv[:, :, 0:1], RSTD[:, 0:NT].rearrange("p (tt one) -> p tt one", one=1))
                    v.tensor_copy(rsv[:, :, 1:2], NMRS[:, 0:NT].rearrange("p (tt one) -> p tt one", one=1)).then_inc(sem["s_rnv"], 1)
                    for tt in range(NT):
                        if tt >= 2:
                            v.wait_ge(sem["s_xnt0"] if tt % 2 == 0 else sem["s_xnt1"],
                                      128 * (tt // 2))
                        v.tensor_scalar(XNS[tt % 2][:, :], X[:, tt * D:(tt + 1) * D],
                                        RSTD[:, tt:tt + 1], NMRS[:, tt:tt + 1],
                                        ALU.mult, ALU.add).then_inc(sem["s_xns"], 1)
                    v.wait_ge(sem["s_hw"], 16)
                    for tt in range(NT):
                        v.wait_ge(sem["s_pj"], N_PJEV + tt + 1)
                        v.tensor_copy(LOG[:, tt * E:(tt + 1) * E],
                                      PJ[tt % 2][:, 0:E]).then_inc(sem["s_rt"], 1)
                    for tt in range(NT):
                        ins = v.tensor_reduce(NMXb[:, tt:tt + 1], LOG[:, tt * E:(tt + 1) * E],
                                              AX.X, ALU.max, negate=True)
                    ins.then_inc(sem["s_rv"], 1)
                    v.wait_ge(sem["s_rs"], 1)
                    for tt in range(NT):
                        v.reciprocal(RZb[:, tt:tt + 1], SZb[:, tt:tt + 1])
                    for tt in range(NT):
                        v.tensor_scalar(PRb[:, tt * E:(tt + 1) * E], PB[:, tt * E:(tt + 1) * E],
                                        RZb[:, tt:tt + 1], None, ALU.mult)
                    for tt in range(NT):
                        v.tensor_reduce(M1b[:, tt:tt + 1], PRb[:, tt * E:(tt + 1) * E], AX.X, ALU.max)
                    for tt in range(NT):
                        v.tensor_scalar(MKb[:, tt * E:(tt + 1) * E], PRb[:, tt * E:(tt + 1) * E],
                                        M1b[:, tt:tt + 1], None, ALU.is_ge)
                    for tt in range(NT):
                        v.scalar_tensor_tensor(MDb[:, tt * E:(tt + 1) * E], MKb[:, tt * E:(tt + 1) * E],
                                               -10.0, PRb[:, tt * E:(tt + 1) * E], ALU.mult, ALU.add)
                    for tt in range(NT):
                        ins = v.tensor_reduce(M2b[:, tt:tt + 1], MDb[:, tt * E:(tt + 1) * E],
                                              AX.X, ALU.max)
                    ins.then_inc(sem["s_rv"], 1)
                    v.wait_ge(sem["s_rs"], 2)
                    v.tensor_tensor(DDb[:, 0:NT], EM1b[:, 0:NT], EM2b[:, 0:NT], ALU.add)
                    for tt in range(NT):
                        v.tensor_scalar(MKb[:, tt * E:(tt + 1) * E], PRb[:, tt * E:(tt + 1) * E],
                                        M2b[:, tt:tt + 1], None, ALU.is_ge)
                    v.reciprocal(RDb[:, 0:NT], DDb[:, 0:NT])
                    v.tensor_tensor(CAb[:, :], EPb[:, :], MKb[:, :], ALU.mult)
                    v.drain()
                    for tt in range(NT):
                        v.tensor_scalar(CBb[:, tt * E:(tt + 1) * E], CAb[:, tt * E:(tt + 1) * E],
                                        RDb[:, tt:tt + 1], None, ALU.mult)
                    for tt in range(NT):
                        ins = v.tensor_tensor(COMB[:, tt * E:(tt + 1) * E], CBb[:, tt * E:(tt + 1) * E],
                                              ESC[:, :], ALU.mult)
                    ins.then_inc(sem["s_rv"], 1)

            PSA = ctx.enter_context(nc.psum_tensor("PSA", [P, 1024], F32))
            PSB = ctx.enter_context(nc.psum_tensor("PSB", [P, 1024], F32))
            AVP0 = ctx.enter_context(nc.psum_tensor("AVP0", [P, 1024], F32))
            AVP1 = ctx.enter_context(nc.psum_tensor("AVP1", [P, 1024], F32))
            with (nc.sbuf_tensor("QHp", [P, T], BF16) as QHp,
                  nc.sbuf_tensor("KHp", [P, T], BF16) as KHp,
                  nc.sbuf_tensor("EB0", [P, T], BF16) as EB0,
                  nc.sbuf_tensor("EB1", [P, T], BF16) as EB1,
                  nc.sbuf_tensor("E20", [P, T], BF16) as E20,
                  nc.sbuf_tensor("E21", [P, T], BF16) as E21):
                EB = [EB0, EB1]
                E2 = [E20, E21]
                AVP = [AVP0, AVP1]

                @blk.sync
                def _(s):
                    s.wait_ge(sem["s_pjev"], N_PJEV)
                    for h in range(H):
                        hb = (h % 2) * 64
                        if h >= 1:
                            s.wait_ge(sem["s_gather"], 256 * h)
                        if h >= 2:
                            s.wait_ge(sem["s_exp"], 32 * (h - 1))
                        for (buf, src) in ((QHp, QT), (KHp, KT)):
                            for pi in range(2):
                                for rp in range(4):
                                    s.dma_start(
                                        buf[hb:hb + 64, (2 * rp + pi) * 256:(2 * rp + pi + 1) * 256],
                                        src[pi * 64: pi * 64 + 64,
                                            rp * T + h * 256: rp * T + (h + 1) * 256],
                                    ).then_inc(sem["s_gather"], 16)

                @blk.tensor
                def _(t):
                    t.wait_ge(sem["s_rt"], NT)
                    t.wait_ge(sem["s_zz"], 1)
                    for h in range(H):
                        hb = (h % 2) * 64
                        t.wait_ge(sem["s_gather"], 256 * (h + 1))
                        if h >= 1:
                            t.wait_ge(sem["s_avev"], 2 * h)
                        for ha in range(2):
                            for bk in range(2):
                                t.matmul(AVP[ha][:, bk * 512:(bk + 1) * 512],
                                         ZZ[:, 0:128], ZZ[:, 0:512],
                                         start=True, stop=False)
                        def emit_av(j):
                            t.wait_ge(sem["s_scl"], 16 * h + j + 1)
                            rc, halfc = j // 2, j % 2
                            nmm = 0
                            for ha in range(2):
                                for r in range(8):
                                    nmm += 1
                                    ins = t.matmul(
                                        AVP[ha][:, r * P:(r + 1) * P],
                                        E2[j % 2][:, r * 256 + ha * P: r * 256 + ha * P + P],
                                        VP[:, (2 * r + halfc) * D + rc * P: (2 * r + halfc) * D + (rc + 1) * P],
                                        start=False, stop=(j == 15) and (r % 4 == 3))
                                    if nmm == 16:
                                        ins.then_inc(sem["s_av"], 1)
                        for i in range(16):
                            for half, ps in ((0, PSA), (1, PSB)):
                                w = 32 * h + 2 * (i - 1) + half + 1
                                if w > 0:
                                    t.wait_ge(sem["s_exp"], w)
                                for u in range(2):
                                    ins = t.matmul(ps[:, u * 512:(u + 1) * 512],
                                                   KHp[hb:hb + 64, i * P:(i + 1) * P],
                                                   QHp[hb:hb + 64,
                                                       half * 1024 + u * 512: half * 1024 + (u + 1) * 512],
                                                   start=True, stop=True)
                                ins.then_inc(sem["s_mm"], 1)
                            if i >= 1:
                                emit_av(i - 1)
                        emit_av(15)

                @blk.scalar
                def _(sc):
                    for h in range(H):
                        for i in range(16):
                            sc.wait_ge(sem["s_vmax"], 16 * h + i + 1)
                            w = 16 * h + i - 1
                            if w > 0:
                                sc.wait_ge(sem["s_zs"], w)
                            w2 = 16 * h + i - 1
                            if w2 > 0:
                                sc.wait_ge(sem["s_av"], w2)
                            sc.activation(EB[i % 2][:, 0:1024], PSA[:, 0:1024], AF.Exp,
                                          bias=NMmin8[:, 0:1], scale=0.125,
                                          accum_out=Z2p[i % 2][:, 0:1]).then_inc(sem["s_exp"], 1)
                            sc.activation(EB[i % 2][:, 1024:2048], PSB[:, 0:1024], AF.Exp,
                                          bias=NMmin8[:, 0:1], scale=0.125,
                                          accum_out=Z2p[i % 2][:, 1:2]).then_inc(sem["s_exp"], 1)

                @blk.vector
                def _(v):
                    for h in range(H):
                        for i in range(17):
                            j = i - 1
                            if j >= 0:
                                v.wait_ge(sem["s_exp"], 32 * h + 2 * j + 2)
                                v.reduce_sum(Zs[:, 0:1], Z2p[j % 2][:, 0:2],
                                             AX.X).then_inc(sem["s_zs"], 1)
                            if i < 16:
                                base = 32 * h + 2 * i
                                if base > 0:
                                    v.wait_ge(sem["s_exp"], base)
                                v.wait_ge(sem["s_mm"], base + 1)
                                v.tensor_reduce(NMmin[:, 0:1], PSA[:, 0:512], AX.X,
                                                ALU.max, negate=True)
                            if j >= 0:
                                if i >= 16:
                                    v.drain()
                                v.reciprocal(Rz[:, 0:1], Zs[:, 0:1])
                            if i < 16:
                                if j < 0:
                                    v.drain()
                                v.tensor_scalar(NMmin8[:, 0:1], NMmin[:, 0:1], 0.125, None,
                                                ALU.mult).then_inc(sem["s_vmax"], 1)
                            if j >= 0:
                                w = 16 * h + j - 1
                                if w > 0:
                                    v.wait_ge(sem["s_av"], w)
                                if i >= 16:
                                    v.drain()
                                v.tensor_scalar(E2[j % 2][:, 0:1024], EB[j % 2][:, 0:1024],
                                                Rz[:, 0:1], None, ALU.mult)
                                v.tensor_scalar(E2[j % 2][:, 1024:2048], EB[j % 2][:, 1024:2048],
                                                Rz[:, 0:1], None, ALU.mult).then_inc(sem["s_scl"], 1)
                        v.wait_ge(sem["s_av"], 16 * (h + 1))
                        for ha in range(2):
                            tt = 2 * h + ha
                            v.tensor_tensor(X[:, tt * D:(tt + 1) * D], X[:, tt * D:(tt + 1) * D],
                                            AVP[ha][:, 0:1024], ALU.add).then_inc(sem["s_avev"], 1)

                if DEBUG_STAGE == "attn":
                    @blk.sync
                    def _(s):
                        s.wait_ge(sem["s_avev"], 2 * H)
                        s.dma_start(out_d.rearrange("(tt p) d -> p tt d", p=P),
                                    X[:, :].rearrange("p (tt d) -> p tt d", d=D)).then_inc(sem["s_out"], 16)
                        s.dma_start(dbg["d_xnt"], XNT[:, :]).then_inc(sem["s_out"], 16)
                        s.dma_start(dbg["d_vp"], VP[:, :]).then_inc(sem["s_out"], 16)
                        s.dma_start(dbg["d_qt"], QT[:, :]).then_inc(sem["s_out"], 16)
                        s.dma_start(dbg["d_kt"], KT[:, :]).then_inc(sem["s_out"], 16)
                        s.dma_start(dbg["d_comb"], COMB[:, :]).then_inc(sem["s_out"], 16)
                        for nm_, tt_ in [("d_s1", S1), ("d_nm", NM), ("d_ss", SS), ("d_veps", VEPS),
                                         ("d_sd", SD), ("d_rstd", RSTD), ("d_nmrs", NMRS)]:
                            s.dma_start(dbg[nm_], tt_[:, :]).then_inc(sem["s_out"], 16)
                        s.wait_ge(sem["s_out"], 208)



        if DEBUG_STAGE == "attn":
            return nc
        CAP = 128
        p3s = contextlib.ExitStack()
        W1Q0 = p3s.enter_context(nc.sbuf_tensor("W1Q0", [P, ND * 1024], BF16))
        W2Q0 = p3s.enter_context(nc.sbuf_tensor("W2Q0", [P, FT_Q * 1024], BF16))
        W1Q1 = p3s.enter_context(nc.sbuf_tensor("W1Q1", [P, ND * 1024], BF16))
        W2Q1 = p3s.enter_context(nc.sbuf_tensor("W2Q1", [P, FT_Q * 1024], BF16))
        SHS0 = p3s.enter_context(nc.sbuf_tensor("SHS0", [P, FT_Q * 2 * CAP], BF16))
        SHS1 = p3s.enter_context(nc.sbuf_tensor("SHS1", [P, FT_Q * 2 * CAP], BF16))
        XGT = p3s.enter_context(nc.sbuf_tensor("XGT", [P, TC * ND * CAP], BF16))
        XROW0 = p3s.enter_context(nc.sbuf_tensor("XROW0", [P, D], BF16))
        XROW1 = p3s.enter_context(nc.sbuf_tensor("XROW1", [P, D], BF16))
        XGB0 = p3s.enter_context(nc.sbuf_tensor("XGB0", [P, D], BF16))
        XGB1 = p3s.enter_context(nc.sbuf_tensor("XGB1", [P, D], BF16))
        RSG0 = p3s.enter_context(nc.sbuf_tensor("RSG0", [P, 2], F32))
        RSG1 = p3s.enter_context(nc.sbuf_tensor("RSG1", [P, 2], F32))
        YSC0 = p3s.enter_context(nc.sbuf_tensor("YSC0", [P, D], BF16))
        YSC1 = p3s.enter_context(nc.sbuf_tensor("YSC1", [P, D], BF16))
        TKIA = p3s.enter_context(nc.sbuf_tensor("TKIA", [P, E * TC], I32))
        TKWA = p3s.enter_context(nc.sbuf_tensor("TKWA", [P, E * TC], F32))
        TKSA = p3s.enter_context(nc.sbuf_tensor("TKSA", [P, E * TC], I32))
        TKF = p3s.enter_context(nc.sbuf_tensor("TKF", [P, 3], F32))
        TKU = p3s.enter_context(nc.sbuf_tensor("TKU", [P, 3], F32))
        ONESC = p3s.enter_context(nc.sbuf_tensor("ONESC", [P, 1], F32))
        IOC2K = p3s.enter_context(nc.sbuf_tensor("IOC2K", [P, 1], F32))
        SELB = p3s.enter_context(nc.sbuf_tensor("SELB", [P, E], F32))
        SELT = p3s.enter_context(nc.sbuf_tensor("SELT", [P, 256], F32))
        SCN = p3s.enter_context(nc.sbuf_tensor("SCN", [P, 256], F32))
        SLX = p3s.enter_context(nc.sbuf_tensor("SLX", [P, 256], F32))
        SLT = p3s.enter_context(nc.sbuf_tensor("SLT", [P, TC * NT], F32))
        PTT0 = p3s.enter_context(nc.sbuf_tensor("PTT0", [P, 256], F32))
        PTT1 = p3s.enter_context(nc.sbuf_tensor("PTT1", [P, 256], F32))
        with p3s:
            W1Q = [W1Q0, W1Q1]
            W2Q = [W2Q0, W2Q1]
            SHS = [SHS0, SHS1]
            XROW = [XROW0, XROW1]
            XGB = [XGB0, XGB1]
            RSG = [RSG0, RSG1]
            YSC = [YSC0, YSC1]
            YDS = YSC
            PTTB = [PTT0, PTT1]
            YP = [PSA, PSB]
            HPq = AVP0
            PRE = AVP1

            def prepP(e, c):
                return e * TC + c + 1

            from concourse.bass import IndirectOffsetOnAxis

            @blk.gpsimd
            def _(g):
                g.wait_ge(sem["s_yz"], 16 * 2 * NT)

                def gathers(e, c):
                    pc = prepP(e, c)
                    sgx = sem["s_gx0"] if c % 2 == 0 else sem["s_gx1"]
                    npar = e * (TC // 2) + c // 2
                    if npar >= 1:
                        g.wait_ge(sgx, 32 * npar)
                    if pc >= 3:
                        g.wait_ge(sem["s_xgb"], pc - 2)
                    g.wait_ge(sem["s_tki"], pc)
                    g.indirect_dma_start(
                        XROW[c % 2][:, :], None,
                        x_d[0:T, :], IndirectOffsetOnAxis(TKIA[:, e * TC + c:e * TC + c + 1], 0),
                    ).then_inc(sgx, 16)
                    g.indirect_dma_start(
                        RSG[c % 2][:, :], None,
                        rsnm_d[0:T, :], IndirectOffsetOnAxis(TKIA[:, e * TC + c:e * TC + c + 1], 0),
                    ).then_inc(sgx, 16)

                def qload_slice(idx, j):
                    e, q = divmod(idx, NQ)
                    par = idx % 2
                    swq = sem["s_wq0"] if par == 0 else sem["s_wq1"]
                    if j == 0:
                        if idx == 0:
                            g.wait_ge(sem["s_gather"], 256 * H)
                        elif idx == 1:
                            g.wait_ge(sem["s_av"], 16 * H)
                        else:
                            g.wait_ge(sem["s_m2"], max(0, (idx - 1) * TC))
                        if idx >= 2:
                            g.wait_ge(swq, 64 * (idx // 2))
                    if j < 2:
                        src1 = w1t_d[e].rearrange("(dt p) f -> p dt f", p=P)[:, j * 4:(j + 1) * 4,
                                                                            q * 1024:(q + 1) * 1024]
                        dst1 = W1Q[par][:, :].rearrange("p (dt f) -> p dt f", dt=ND)[:, j * 4:(j + 1) * 4, :]
                        g.dma_start(dst1, src1).then_inc(swq, 16)
                    else:
                        jj = j - 2
                        src2 = w2t_d[e, q * 1024:(q + 1) * 1024, :].rearrange(
                            "(ft p) d -> p ft d", p=P)[:, jj * 4:(jj + 1) * 4, :]
                        dst2 = W2Q[par][:, :].rearrange("p (ft d) -> p ft d", ft=FT_Q)[:, jj * 4:(jj + 1) * 4, :]
                        g.dma_start(dst2, src2).then_inc(swq, 16)

                def qload(idx):
                    for j in range(4):
                        qload_slice(idx, j)

                def scatter_one(yc):
                    e2 = yc // (NQ * TC)
                    c2 = yc % TC
                    ssc = sem["s_sc0"] if yc % 2 == 0 else sem["s_sc1"]
                    if yc // 2 >= 1:
                        g.wait_ge(ssc, 16 * (yc // 2))
                    g.wait_ge(sem["s_ysc"], yc + 1)
                    g.indirect_dma_start(
                        yd_d[0:T + P, :], IndirectOffsetOnAxis(TKSA[:, e2 * TC + c2:e2 * TC + c2 + 1], 0),
                        YSC[yc % 2][:, :], None,
                        compute_op=ALU.add).then_inc(ssc, 16)

                for c in range(TC):
                    gathers(0, c)
                qload(0)
                qload(1)
                for idx in range(NIDX):
                    e, q = divmod(idx, NQ)
                    for c in range(TC):
                        yc = idx * TC + c
                        if yc >= 2:
                            scatter_one(yc - 2)
                        if idx + 1 < NIDX and 2 <= c <= 5:
                            qload_slice(idx + 1, c - 2)
                        if q == 3 and e < E - 1:
                            gathers(e + 1, c)
                scatter_one(NIDX * TC - 2)
                scatter_one(NIDX * TC - 1)

            @blk.tensor
            def _(t):
                t.wait_ge(sem["s_avev"], 2 * H)
                t.wait_ge(sem["s_rv"], 3)
                for c in range(TC):
                    t.wait_ge(sem["s_p3v"], 3 * c + 1)
                    t.transpose(PRE[0:E, 0:P], SELB[:, 0:E], IDM[:, :]).then_inc(sem["s_p3t"], 1)
                    t.wait_ge(sem["s_p3v"], 3 * c + 2)
                    t.transpose(PRE[0:E, 0:P], SELB[:, 0:E], IDM[:, :]).then_inc(sem["s_p3t"], 1)
                    t.wait_ge(sem["s_p3v"], 3 * c + 3)
                    for tau in range(2):
                        ins = t.transpose(PRE[0:P, 512 + tau * E: 512 + (tau + 1) * E],
                                          SLX[0:E, tau * P:(tau + 1) * P], IDM[0:E, 0:E])
                    ins.then_inc(sem["s_p3t"], 1)
                for k in range(E * TC):
                    e3, c3 = divmod(k, TC)
                    t.wait_ge(sem["s_ptv"], k + 1)
                    if k >= 2:
                        t.wait_ge(sem["s_tki"], k - 1)
                    off = (k % 2) * 512
                    for tau in range(2):
                        t.matmul(PRE[0:P, off:off + 1], PTTB[k % 2][:, tau * P:(tau + 1) * P],
                                 GIDC[:, 2 * c3 + tau: 2 * c3 + tau + 1],
                                 start=(tau == 0), stop=(tau == 1))
                    for tau in range(2):
                        t.matmul(PRE[0:P, off + 1:off + 2], PTTB[k % 2][:, tau * P:(tau + 1) * P],
                                 COMB[:, (2 * c3 + tau) * E + e3: (2 * c3 + tau) * E + e3 + 1],
                                 start=(tau == 0), stop=(tau == 1))
                    for tau in range(2):
                        ins = t.matmul(PRE[0:P, off + 2:off + 3], PTTB[k % 2][:, tau * P:(tau + 1) * P],
                                       ONESC[:, 0:1], start=(tau == 0), stop=(tau == 1))
                    ins.then_inc(sem["s_ptt"], 1)
                for e in range(E):
                    for q in range(NQ):
                        idx = e * NQ + q
                        par = idx % 2
                        t.wait_ge(sem["s_wq0"] if par == 0 else sem["s_wq1"],
                                  64 * (idx // 2 + 1))
                        for cp in range(TC // 2):
                            gp = (idx * (TC // 2) + cp)
                            t.wait_ge(sem["s_xt0"], 128 * (e * (TC // 2) + cp + 1))
                            t.wait_ge(sem["s_xt1"], 128 * (e * (TC // 2) + cp + 1))
                            for ft in range(FT_Q):
                                gg = gp * FT_Q + ft
                                if gg >= 2:
                                    t.wait_ge(sem["s_silu"], gg - 1)
                                hp = HPq[:, (gg % 2) * 512: (gg % 2) * 512 + 2 * CAP]
                                for dt in range(ND):
                                    ins = t.matmul(hp,
                                                   W1Q[par][:, dt * 1024 + ft * P: dt * 1024 + (ft + 1) * P],
                                                   XGT[:, dt * (TC * CAP) + 2 * cp * CAP: dt * (TC * CAP) + (2 * cp + 2) * CAP],
                                                   start=(dt == 0), stop=(dt == ND - 1))
                                ins.then_inc(sem["s_m1"], 1)
                            for sub in range(2):
                                c = 2 * cp + sub
                                yc = idx * TC + c
                                if yc >= 2:
                                    t.wait_ge(sem["s_ysc"], yc - 1)
                                for ft in range(FT_Q):
                                    gg = gp * FT_Q + ft
                                    t.wait_ge(sem["s_silu"], gg + 1)
                                    for nh in range(2):
                                        ins = t.matmul(YP[yc % 2][:, nh * 512:(nh + 1) * 512],
                                                       SHS[cp % 2][:, ft * 2 * CAP + sub * CAP: ft * 2 * CAP + (sub + 1) * CAP],
                                                       W2Q[par][:, ft * 1024 + nh * 512: ft * 1024 + (nh + 1) * 512],
                                                       start=(ft == 0), stop=(ft == FT_Q - 1))
                                ins.then_inc(sem["s_m2"], 1)

            @blk.scalar
            def _(sc):
                for idx in range(NIDX):
                    for cp in range(TC // 2):
                        gp = idx * (TC // 2) + cp
                        for ft in range(FT_Q):
                            gg = gp * FT_Q + ft
                            sc.wait_ge(sem["s_m1"], gg + 1)
                            if gp >= 2 and ft == 0:
                                sc.wait_ge(sem["s_m2"], 2 * gp - 2)
                            sc.activation(SHS[cp % 2][:, ft * 2 * CAP:(ft + 1) * 2 * CAP],
                                          HPq[:, (gg % 2) * 512: (gg % 2) * 512 + 2 * CAP],
                                          AF.Silu).then_inc(sem["s_silu"], 1)

            @blk.vector
            def _(v):
                for c in range(TC):
                    v.tensor_scalar(SELB[:, :], COMB[:, 2 * c * E:(2 * c + 1) * E],
                                    0.0, None, ALU.not_equal).then_inc(sem["s_p3v"], 1)
                    v.wait_ge(sem["s_p3t"], 3 * c + 1)
                    v.tensor_copy(SELT[0:E, 0:P], PRE[0:E, 0:P])
                    v.tensor_scalar(SELB[:, :], COMB[:, (2 * c + 1) * E:(2 * c + 2) * E],
                                    0.0, None, ALU.not_equal).then_inc(sem["s_p3v"], 1)
                    v.wait_ge(sem["s_p3t"], 3 * c + 2)
                    v.tensor_copy(SELT[0:E, P:256], PRE[0:E, 0:P])
                    v.drain()
                    v.tensor_tensor_scan(SCN[0:E, 0:256], SELT[0:E, 0:256], SELT[0:E, 0:256],
                                         0.0, ALU.add, ALU.bypass)
                    v.drain()
                    v.tensor_tensor(SCN[0:E, 0:256], SCN[0:E, 0:256], SELT[0:E, 0:256],
                                    ALU.subtract)
                    v.scalar_tensor_tensor(SLX[0:E, 0:256], SELT[0:E, 0:256], -999.0,
                                           SCN[0:E, 0:256], ALU.mult, ALU.add)
                    v.drain()
                    v.tensor_scalar(SLX[0:E, 0:256], SLX[0:E, 0:256], 999.0, None,
                                    ALU.add).then_inc(sem["s_p3v"], 1)
                    v.wait_ge(sem["s_p3t"], 3 * c + 3)
                    v.tensor_copy(SLT[:, c * NT: c * NT + E], PRE[0:P, 512:512 + E])
                    v.tensor_copy(SLT[:, c * NT + E: c * NT + 2 * E],
                                  PRE[0:P, 512 + E:512 + 2 * E])
                def prep_chunk_xgb(e, c):
                    sgx = sem["s_gx0"] if c % 2 == 0 else sem["s_gx1"]
                    v.wait_ge(sgx, 32 * (e * (TC // 2) + c // 2 + 1))
                    sxt = sem["s_xt0"] if c % 2 == 0 else sem["s_xt1"]
                    if e * (TC // 2) + c // 2 >= 1:
                        v.wait_ge(sxt, 128 * (e * (TC // 2) + c // 2))
                    v.tensor_scalar(XGB[c % 2][:, :], XROW[c % 2][:, :],
                                    RSG[c % 2][:, 0:1], RSG[c % 2][:, 1:2],
                                    ALU.mult, ALU.add).then_inc(sem["s_xgb"], 1)

                def extract(k):
                    v.wait_ge(sem["s_ptt"], k + 1)
                    off = (k % 2) * 512
                    v.tensor_copy(TKF[:, 0:3], PRE[0:P, off:off + 3])
                    v.drain()
                    v.tensor_scalar(TKU[:, 0:1], TKF[:, 2:3], -1.0, 1.0, ALU.mult, ALU.add)
                    v.tensor_copy(TKIA[:, k:k + 1], TKF[:, 0:1])
                    v.scalar_tensor_tensor(TKU[:, 1:2], TKU[:, 0:1], 1.0, IOC2K[:, 0:1],
                                           ALU.mult, ALU.mult)
                    v.tensor_copy(TKWA[:, k:k + 1], TKF[:, 1:2])
                    v.tensor_tensor(TKU[:, 2:3], TKU[:, 1:2], TKF[:, 0:1], ALU.add)
                    v.tensor_scalar(TKU[:, 0:1], TKF[:, 2:3], -1.0, 1.0, ALU.mult, ALU.add)
                    v.tensor_copy(TKSA[:, k:k + 1], TKU[:, 2:3]).then_inc(sem["s_tki"], 1)

                v.memset(ONESC[:, 0:1], 1.0)
                v.tensor_scalar(IOC2K[:, 0:1], IOTC[:, 0:1], 1.0, float(T), ALU.mult, ALU.add)
                for k in range(E * TC):
                    e3, c3 = divmod(k, TC)
                    if k >= 2:
                        v.wait_ge(sem["s_ptt"], k - 1)
                    for tau in range(2):
                        ins = v.tensor_scalar(PTTB[k % 2][:, tau * P:(tau + 1) * P], IOTR[:, :],
                                              SLT[:, c3 * NT + tau * E + e3: c3 * NT + tau * E + e3 + 1],
                                              None, ALU.is_equal)
                    ins.then_inc(sem["s_ptv"], 1)
                    if k >= 1:
                        extract(k - 1)
                    if 9 <= k <= 16:
                        prep_chunk_xgb(0, k - 9)
                extract(E * TC - 1)

                prep_chunk = prep_chunk_xgb
                for e in range(E):
                    for q in range(NQ):
                        idx = e * NQ + q
                        for c in range(TC):
                            yc = idx * TC + c
                            if e < E - 1 and q == 3:
                                prep_chunk(e + 1, c)
                            v.wait_ge(sem["s_m2"], yc + 1)
                            ssc = sem["s_sc0"] if yc % 2 == 0 else sem["s_sc1"]
                            if yc // 2 >= 1:
                                v.wait_ge(ssc, 16 * (yc // 2))
                            v.tensor_scalar(YSC[yc % 2][:, :], YP[yc % 2][:, 0:D],
                                            TKWA[:, e * TC + c:e * TC + c + 1], None,
                                            ALU.mult).then_inc(sem["s_ysc"], 1)
                for tt in range(NT):
                    v.wait_ge(sem["s_yl"], 16 * (tt + 1))
                    v.tensor_tensor(X[:, tt * D:(tt + 1) * D], X[:, tt * D:(tt + 1) * D],
                                    YDS[tt % 2][:, :], ALU.add).then_inc(sem["s_yev"], 1)

            @blk.sync
            def _(s):
                for e in range(E):
                    for c in range(TC):
                        pc = prepP(e, c)
                        sxt = sem["s_xt0"] if c % 2 == 0 else sem["s_xt1"]
                        npar = e * (TC // 2) + c // 2
                        if npar >= 1:
                            s.wait_ge(sxt, 128 * npar)
                        if e >= 1:
                            s.wait_ge(sem["s_m1"], 8 * ((4 * e - 1) * (TC // 2) + c // 2 + 1))
                        s.wait_ge(sem["s_xgb"], pc)
                        for dt in range(ND):
                            s.dma_start(XGT[:, dt * (TC * CAP) + c * CAP: dt * (TC * CAP) + (c + 1) * CAP],
                                        XGB[c % 2][:, dt * P:(dt + 1) * P],
                                        transpose=True).then_inc(sxt, 16)
                s.wait_ge(sem["s_sc0"], 16 * 128)
                s.wait_ge(sem["s_sc1"], 16 * 128)
                for tt in range(NT):
                    if tt >= 2:
                        s.wait_ge(sem["s_yev"], tt - 1)
                    s.dma_start(YDS[tt % 2][:, :],
                                yd_d[tt * P:(tt + 1) * P, :]).then_inc(sem["s_yl"], 16)
                for tt in range(NT):
                    s.wait_ge(sem["s_yev"], tt + 1)
                    s.dma_start(out_d[tt * P:(tt + 1) * P, :],
                                X[:, tt * D:(tt + 1) * D]).then_inc(sem["s_out"], 16)
                s.wait_ge(sem["s_out"], 16 * NT)

    return nc


_NC_CACHE = {}


def kernel(L0_attn_x, L0_attn_Mq, L0_attn_Mk, L0_attn_Mv,
           L0_ffn_router_w, L0_ffn_w1, L0_ffn_w2, L0_ffn_expert_scale):
    x = np.ascontiguousarray(np.asarray(L0_attn_x, dtype=np.float32))
    mq = np.ascontiguousarray(np.asarray(L0_attn_Mq, dtype=np.float32))
    mk = np.ascontiguousarray(np.asarray(L0_attn_Mk, dtype=np.float32))
    mv = np.ascontiguousarray(np.asarray(L0_attn_Mv, dtype=np.float32))
    rwt = np.ascontiguousarray(np.asarray(L0_ffn_router_w, dtype=np.float32).T)
    w1t = np.ascontiguousarray(np.asarray(L0_ffn_w1, dtype=np.float32).transpose(0, 2, 1))
    w2t = np.ascontiguousarray(np.asarray(L0_ffn_w2, dtype=np.float32).transpose(0, 2, 1))
    esc = np.ascontiguousarray(
        np.broadcast_to(np.asarray(L0_ffn_expert_scale, dtype=np.float32)[None, :], (P, E)))

    if "nc" not in _NC_CACHE:
        _NC_CACHE["nc"] = build_nc()
    nc = _NC_CACHE["nc"]

    in_maps = [{"x": np.ascontiguousarray(x[b]), "mq": mq, "mk": mk, "mv": mv,
                "rwt": rwt, "esc": esc, "w1t": w1t, "w2t": w2t} for b in range(8)]
    res = run_bass_kernel_spmd(nc, in_maps, core_ids=list(range(8)))
    return np.stack([np.asarray(res.results[b]["out"], dtype=np.float32) for b in range(8)], axis=0)
